# version 19
# baseline (speedup 1.0000x reference)
import numpy as np
from contextlib import ExitStack

import concourse.bass as bass
import concourse.mybir as mybir
from concourse.bass_utils import run_bass_kernel_spmd

F32 = mybir.dt.float32
BF16 = mybir.dt.bfloat16
AF = mybir.ActivationFunctionType
ALU = mybir.AluOpType
AX = mybir.AxisListType

P = 128
C = 1024
NT = 4096
NTIL = NT // P
H = 16
Dh = 64
I = 2816
N3 = [(0, 1024), (1024, 2048), (2048, 2816)]
THR = [0.25, 0.75, 1.25, 1.75, 2.5, 3.5, 5.0]

ENGS = ("sync", "scalar", "vector", "tensor", "gpsimd")


class Sem:
    __slots__ = ("name", "n", "h")

    def __init__(self, name):
        self.name, self.n, self.h = name, 0, None


class Rec:

    def __init__(self):
        self.ops = {e: [] for e in ENGS}
        self.sems = []
        self.ck = {}
        self.bankprev = {}
        self.jnk = None
        self.jnkS = None

    def sem(self, name):
        s = Sem(name)
        self.sems.append(s)
        return s

    def _junk(self, eng):
        if eng == "scalar":
            j = self.jnkS
            return lambda e, o=j: e.memzero(o)
        j = self.jnk
        return lambda e, o=j: e.memset(o, 1.0)

    def op(self, eng, fn, waits=(), incs=(), pad=False, prepad=False, fence=False):
        w = tuple((s, v) for s, v in waits if v > 0)
        if prepad:
            self.ops[eng].append((w, self._junk(eng), ()))
        if fence:
            self.ops[eng].append((w, fn, ()))
            for s, k in incs:
                s.n += k
            self.ops[eng].append(((), self._junk(eng), tuple(incs)))
        else:
            for s, k in incs:
                s.n += k
            self.ops[eng].append((w, fn, tuple(incs)))
        if pad:
            self.ops[eng].append(((), self._junk(eng), ()))

    def mark(self, name, sem):
        self.ck[name] = (sem, sem.n)

    def at(self, name):
        return self.ck[name]

    def wtap(self, name):
        return [self.ck[name]] if name in self.ck else []

    def replay(self, e, eng):
        lastw = {}
        for waits, fn, incs in self.ops[eng]:
            for s, v in waits:
                if lastw.get(id(s), 0) >= v:
                    continue
                e.wait_ge(s.h, v)
                lastw[id(s)] = v
            ins = fn(e) if fn is not None else None
            for s, k in incs:
                ins.then_inc(s.h, k)


def build_program(taps=False):
    nc = bass.Bass("TRN2", target_bir_lowering=False)

    x_d = nc.dram_tensor("x", [NT, C], F32, kind="ExternalInput").ap()
    wqkv_d = nc.dram_tensor("wqkv", [C, 3 * C], F32, kind="ExternalInput").ap()
    wg_d = nc.dram_tensor("wg", [C, C], F32, kind="ExternalInput").ap()
    wp_d = nc.dram_tensor("wp", [C, C], F32, kind="ExternalInput").ap()
    wgate_d = nc.dram_tensor("wgate", [I, C], F32, kind="ExternalInput").ap()
    wup_d = nc.dram_tensor("wup", [I, C], F32, kind="ExternalInput").ap()
    wdown_d = nc.dram_tensor("wdown", [C, I], F32, kind="ExternalInput").ap()
    bgt_d = nc.dram_tensor("bgt", [P, C], F32, kind="ExternalInput").ap()
    bpt_d = nc.dram_tensor("bpt", [P, C], F32, kind="ExternalInput").ap()
    csts_d = nc.dram_tensor("csts_d", [P, 8], F32, kind="ExternalInput").ap()
    y_d = nc.dram_tensor("y", [NT, C], F32, kind="ExternalOutput").ap()

    gates_sp = nc.dram_tensor("gates_sp", [NT, C], F32, kind="Internal").ap()
    q1_sp = nc.dram_tensor("q1_sp", [NT, C], F32, kind="Internal").ap()
    part_sp = nc.dram_tensor("part_sp", [NT, C], F32, kind="Internal").ap()
    xnh_sp = nc.dram_tensor("xnh_sp", [NTIL, P, 8, P], BF16, kind="Internal").ap()
    xnl_sp = nc.dram_tensor("xnl_sp", [NTIL, P, 8, P], BF16, kind="Internal").ap()
    xqh_sp = nc.dram_tensor("xqh_sp", [NTIL, P, 8, P], BF16, kind="Internal").ap()
    xql_sp = nc.dram_tensor("xql_sp", [NTIL, P, 8, P], BF16, kind="Internal").ap()
    wgh_sp = nc.dram_tensor("wgh_sp", [P, 8, I], BF16, kind="Internal").ap()
    wgl_sp = nc.dram_tensor("wgl_sp", [P, 8, I], BF16, kind="Internal").ap()
    wuh_sp = nc.dram_tensor("wuh_sp", [P, 8, I], BF16, kind="Internal").ap()
    wul_sp = nc.dram_tensor("wul_sp", [P, 8, I], BF16, kind="Internal").ap()
    wdh_sp = nc.dram_tensor("wdh_sp", [P, 22, C], BF16, kind="Internal").ap()
    wdl_sp = nc.dram_tensor("wdl_sp", [P, 22, C], BF16, kind="Internal").ap()
    cc_in = nc.dram_tensor("cc_in", [65, C], F32, kind="Internal").ap()
    cc_out = nc.dram_tensor("cc_out", [65, C], F32, kind="Internal").ap()

    tap = {}
    if taps:
        for nm, shp, dt in [("t_xn", [640, C], F32), ("t_gates", [640, C], F32),
                            ("t_q1", [640, C], F32), ("t_k1", [640, C], F32),
                            ("t_kvpart", [65, C], F32), ("t_kvall", [65, C], F32),
                            ("t_part", [256, C], F32), ("t_h", [P, 1024], F32),
                            ("t_xqh", [640, C], BF16), ("t_xql", [640, C], BF16),
                            ("t_ksb", [P, C], F32), ("t_norm", [P, H], F32),
                            ("t_qn", [P, C], F32), ("t_w2h", [P, C], BF16),
                            ("t_w2l", [P, C], BF16), ("t_qnh", [P, C], BF16),
                            ("t_qnl", [P, C], BF16), ("t_bfh", [P, 8, P], BF16),
                            ("t_cgh", [P, 8, 1024], BF16), ("t_cgl", [P, 8, 1024], BF16),
                            ("t_cdh", [P, 8, C], BF16), ("t_cdl", [P, 8, C], BF16),
                            *[(f"t_dbg{k}", [P, 1], F32) for k in range(15)],
                            ("t_psph", [P, 1408], BF16), ("t_pspl", [P, 1408], BF16),
                            ("t_pyq", [P, 1408], F32), ("t_paa", [P, 1408], F32)]:
            tap[nm] = nc.dram_tensor(nm, shp, dt, kind="ExternalOutput").ap()

    R = Rec()
    sLd = R.sem("sLd")
    sTr = R.sem("sTr")
    sSt = R.sem("sSt")
    sS = R.sem("sS")
    sV = R.sem("sV")
    sT = R.sem("sT")
    sCC = R.sem("sCC")
    sP = R.sem("sP")

    def sb(name, shape, dt):
        g = nc.sbuf_tensor(name, shape, dt)
        t = g.__enter__()
        return t.ap(), g

    def scope_alloc(scope, name, shape, dt):
        ap, g = sb(name, shape, dt)
        scope.append(g)
        return ap

    def scope_exit(scope):
        for g in reversed(scope):
            g.__exit__(None, None, None)
        scope.clear()

    csts, _ = sb("csts", [P, 8], F32)
    f2 = [sb(f"f2_{j}", [P, C], F32)[0] for j in range(2)]
    g2 = [sb(f"g2_{j}", [P, C], F32)[0] for j in range(2)]
    acc1 = [sb(f"acc1_{j}", [P, 1], F32)[0] for j in range(2)]
    acc2 = [sb(f"acc2_{j}", [P, 1], F32)[0] for j in range(2)]
    tmean = [sb(f"tmean_{j}", [P, 1], F32)[0] for j in range(2)]
    tvar = [sb(f"tvar_{j}", [P, 1], F32)[0] for j in range(2)]
    trstd = [sb(f"trstd_{j}", [P, 1], F32)[0] for j in range(2)]
    tnmr = [sb(f"tnmr_{j}", [P, 1], F32)[0] for j in range(2)]
    ln_a = [sb(f"ln_a{j}", [P, 1], F32)[0] for j in range(2)]
    ln_b = [sb(f"ln_b{j}", [P, 1], F32)[0] for j in range(2)]
    red2 = [sb(f"red2_{j}", [P, 88], F32)[0] for j in range(2)]
    sq2 = [sb(f"sq2_{j}", [P, 88], F32)[0] for j in range(2)]
    rq2 = [sb(f"rq2_{j}", [P, 88], F32)[0] for j in range(2)]
    kvacc, _ = sb("kvacc", [65, C], F32)
    jnkt, _ = sb("jnkt", [P, 8], F32)
    jnkS, _ = sb("jnkS", [P, 8], F32)
    R.jnk = jnkt[:, :]
    R.jnkS = jnkS[:, :]

    eps_c = csts[:, 0:1]
    thr_c = [csts[:, 1 + j:2 + j] for j in range(7)]

    ps_kv = nc.psum_tensor("ps_kv", [P, C], F32).__enter__().ap()
    pp = [nc.psum_tensor(f"pp{j}", [P, 512], F32).__enter__().ap() for j in range(2)]
    psW = nc.psum_tensor("psW", [P, C], F32).__enter__().ap()
    psD = [nc.psum_tensor(f"psD{j}", [P, 512], F32).__enter__().ap() for j in range(2)]

    def dma(eng, out, in_, sem, waits=(), transpose=False):
        if transpose:
            R.op(eng, lambda e, o=out, i=in_: e.dma_start(out=o, in_=i, transpose=True),
                 waits=waits, incs=[(sem, 16)])
        else:
            R.op(eng, lambda e, o=out, i=in_: e.dma_start(out=o, in_=i),
                 waits=waits, incs=[(sem, 16)])

    def mm_group(bank, bankkey, triples, waits_first):
        w0 = list(waits_first)
        prev = R.bankprev.get(bankkey)
        if prev is not None:
            w0.append(prev)
        n = len(triples)
        for t, (o, l, r) in enumerate(triples):
            R.op("tensor",
                 lambda e, o=o, l=l, r=r, s=(t == 0), p=(t == n - 1):
                 e.matmul(o, l, r, start=s, stop=p),
                 waits=w0 if t == 0 else [],
                 incs=[(sT, 1)] if t == n - 1 else [])

    def bank_release(bankkey, sem, val):
        R.bankprev[bankkey] = (sem, val)

    def proj3(bank, bankkey, lh, ll, wh, wl, ncols, csl, waits_first):
        triples = []
        for cch in range(8):
            triples.append((bank[:, 0:ncols], lh[:, cch, :], wh[:, cch, csl]))
            triples.append((bank[:, 0:ncols], lh[:, cch, :], wl[:, cch, csl]))
            triples.append((bank[:, 0:ncols], ll[:, cch, :], wh[:, cch, csl]))
        mm_group(bank, bankkey, triples, waits_first)
        return sT.n

    qprev = {}

    def quant_stage1(pool, key, src, width, idx, waits_v):
        nb = width // 16
        red, sq, rq = red2[idx][:, 0:nb], sq2[idx][:, 0:nb], rq2[idx][:, 0:nb]
        yq = pool["yq"][idx][:, 0:width]
        prev = [qprev[key]] if key in qprev else []
        R.op("vector", lambda e, r=red, s=src: e.tensor_reduce(
            r, s.rearrange("p (b k) -> p b k", k=16), AX.X, ALU.max,
            apply_absolute_value=True), waits=list(waits_v) + prev,
            incs=[(sV, 1)])
        R.op("vector", lambda e, o=sq, i=red: e.tensor_scalar(
            o, i, 1.0 / 6.0, 1e-12, ALU.mult, ALU.max), waits=[(sV, sV.n)],
            incs=[(sV, 1)])
        v_sq = sV.n
        R.op("vector", lambda e, o=rq, i=sq: e.reciprocal(o, i),
             waits=[(sV, v_sq)], incs=[(sV, 1)])
        v_rq = sV.n
        R.op("vector", lambda e, o=yq, s=src, r=rq, k=nb: e.tensor_tensor(
            o.rearrange("p (b k) -> p b k", k=16),
            s.rearrange("p (b k) -> p b k", k=16),
            r.broadcast_to((P, k, 16)), ALU.mult), waits=[(sV, v_rq)],
            incs=[(sV, 1)])
        return {"v_yq": sV.n, "v_sq": v_sq, "sq": sq, "yq": yq,
                "idx": idx, "width": width, "nb": nb}

    def quant_stage2(pool, key, st, dst_h, dst_l, waits2=(), use_pool=True):
        width, nb, idx = st["width"], st["nb"], st["idx"]
        yq, sq = st["yq"], st["sq"]
        aa = pool["aa"][idx][:, 0:width]
        sb7 = [b[:, 0:width] for b in pool["sb"]]
        ub = [b[:, 0:width] for b in pool["ub"]]
        sgb = pool["sg"][idx][:, 0:width]
        prev = [qprev[key]] if key in qprev else []
        R.op("scalar", lambda e, o=sgb, i=yq: e.activation(o, i, AF.Sign),
             waits=[(sV, st["v_yq"])] + prev)
        R.op("scalar", lambda e, o=aa, i=yq: e.activation(o, i, AF.Abs),
             incs=[(sS, 1)])
        s_abs = sS.n
        for t in range(4):
            R.op("scalar", lambda e, o=sb7[t], i=aa, b=thr_c[t]: e.activation(
                o, i, AF.Sign, bias=b), waits=[(sS, s_abs)] if t == 0 else [],
                incs=[(sS, 1)] if t == 3 else [])
        s_ab4 = sS.n
        for t in range(4, 7):
            R.op("scalar", lambda e, o=sb7[t], i=aa, b=thr_c[t]: e.activation(
                o, i, AF.Sign, bias=b), incs=[(sS, 1)] if t == 6 else [])
        s_all = sS.n
        if use_pool:
            R.op("gpsimd", lambda e, o=ub[0], a=sb7[0], b=sb7[1]: e.tensor_tensor(
                o, a, b, ALU.add), waits=[(sS, s_ab4)] + list(waits2),
                incs=[(sP, 1)])
            R.op("gpsimd", lambda e, o=ub[1], a=sb7[2], b=sb7[3]: e.tensor_tensor(
                o, a, b, ALU.add), incs=[(sP, 1)])
            R.op("gpsimd", lambda e, o=ub[0], a=ub[0], b=ub[1]: e.tensor_tensor(
                o, a, b, ALU.add), waits=[(sP, sP.n)], incs=[(sP, 1)])
            R.op("gpsimd", lambda e, o=ub[1], a=sb7[4], b=sb7[5]: e.tensor_tensor(
                o, a, b, ALU.add), waits=[(sS, s_all)], incs=[(sP, 1)],
                fence=True)
            p_tree = sP.n
            if use_pool == "adds":
                R.op("vector", lambda e, o=ub[1], a=sb7[6], b=ub[1]:
                     e.scalar_tensor_tensor(o, a, 2.0, b, ALU.mult, ALU.add),
                     waits=[(sS, s_all), (sP, p_tree)], incs=[(sV, 1)])
                R.op("vector", lambda e, o=ub[0], a=ub[1], b=ub[0]:
                     e.scalar_tensor_tensor(o, a, 2.0, b, ALU.mult, ALU.add),
                     waits=[(sV, sV.n)], incs=[(sV, 1)])
                R.op("vector", lambda e, o=ub[1], i=ub[0]: e.tensor_scalar(
                    o, i, 0.25, 3.0, ALU.mult, ALU.add), waits=[(sV, sV.n)],
                    incs=[(sV, 1)])
                R.op("vector", lambda e, o=ub[0], a=ub[1], b=sgb: e.tensor_tensor(
                    o, a, b, ALU.mult), waits=[(sV, sV.n)], incs=[(sV, 1)])
                R.op("vector", lambda e, o=yq, a=ub[0], s=sq, k=nb: e.tensor_tensor(
                    o.rearrange("p (b k) -> p b k", k=16),
                    a.rearrange("p (b k) -> p b k", k=16),
                    s.broadcast_to((P, k, 16)), ALU.mult),
                    waits=[(sV, st["v_sq"]), (sV, sV.n)], incs=[(sV, 1)])
                R.op("vector", lambda e, o=dst_h, i=yq: e.tensor_copy(o, i),
                     waits=[(sV, sV.n)], incs=[(sV, 1)])
                R.op("vector", lambda e, o=dst_l, a=yq, b=dst_h: e.tensor_tensor(
                    o, a, b, ALU.subtract), waits=[(sV, sV.n)], incs=[(sV, 1)],
                    fence=True)
                qprev[key] = (sV, sV.n)
                return qprev[key]
            R.op("vector", lambda e, o=ub[1], a=sb7[6], b=ub[1]:
                 e.scalar_tensor_tensor(o, a, 2.0, b, ALU.mult, ALU.add),
                 waits=[(sS, s_all), (sP, p_tree)], incs=[(sV, 1)])
            R.op("vector", lambda e, o=ub[0], a=ub[1], b=ub[0]:
                 e.scalar_tensor_tensor(o, a, 2.0, b, ALU.mult, ALU.add),
                 waits=[(sV, sV.n)], incs=[(sV, 1)])
            R.op("vector", lambda e, o=ub[1], i=ub[0]: e.tensor_scalar(
                o, i, 0.25, 3.0, ALU.mult, ALU.add), waits=[(sV, sV.n)],
                incs=[(sV, 1)])
            R.op("vector", lambda e, o=ub[0], a=ub[1], b=sgb: e.tensor_tensor(
                o, a, b, ALU.mult), waits=[(sV, sV.n)], incs=[(sV, 1)],
                fence=True)
            v_g = sV.n
            R.op("gpsimd", lambda e, o=yq, a=ub[0], s=sq, k=nb: e.tensor_tensor(
                o.rearrange("p (b k) -> p b k", k=16),
                a.rearrange("p (b k) -> p b k", k=16),
                s.broadcast_to((P, k, 16)), ALU.mult),
                waits=[(sV, v_g), (sV, st["v_sq"])], incs=[(sP, 1)])
            R.op("gpsimd", lambda e, o=dst_h, i=yq: e.tensor_copy(o, i),
                 waits=[(sP, sP.n)], incs=[(sP, 1)])
            R.op("gpsimd", lambda e, o=dst_l, a=yq, b=dst_h: e.tensor_tensor(
                o, a, b, ALU.subtract), waits=[(sP, sP.n)], incs=[(sP, 1)],
                fence=True)
            qprev[key] = (sP, sP.n)
            return qprev[key]
        R.op("vector", lambda e, o=ub[0], a=sb7[0], b=sb7[1]: e.tensor_tensor(
            o, a, b, ALU.add), waits=[(sS, s_ab4)] + list(waits2),
            incs=[(sV, 1)])
        R.op("vector", lambda e, o=ub[1], a=sb7[2], b=sb7[3]: e.tensor_tensor(
            o, a, b, ALU.add), incs=[(sV, 1)])
        R.op("vector", lambda e, o=ub[0], a=ub[0], b=ub[1]: e.tensor_tensor(
            o, a, b, ALU.add), waits=[(sV, sV.n)], incs=[(sV, 1)])
        R.op("vector", lambda e, o=ub[1], a=sb7[4], b=sb7[5]: e.tensor_tensor(
            o, a, b, ALU.add), waits=[(sS, s_all)], incs=[(sV, 1)])
        R.op("vector", lambda e, o=ub[1], a=sb7[6], b=ub[1]:
             e.scalar_tensor_tensor(o, a, 2.0, b, ALU.mult, ALU.add),
             waits=[(sV, sV.n)], incs=[(sV, 1)])
        R.op("vector", lambda e, o=ub[0], a=ub[1], b=ub[0]:
             e.scalar_tensor_tensor(o, a, 2.0, b, ALU.mult, ALU.add),
             waits=[(sV, sV.n)], incs=[(sV, 1)])
        R.op("vector", lambda e, o=ub[1], i=ub[0]: e.tensor_scalar(
            o, i, 0.25, 3.0, ALU.mult, ALU.add), waits=[(sV, sV.n)],
            incs=[(sV, 1)])
        R.op("vector", lambda e, o=ub[0], a=ub[1], b=sgb: e.tensor_tensor(
            o, a, b, ALU.mult), waits=[(sV, sV.n)], incs=[(sV, 1)])
        R.op("vector", lambda e, o=yq, a=ub[0], s=sq, k=nb: e.tensor_tensor(
            o.rearrange("p (b k) -> p b k", k=16),
            a.rearrange("p (b k) -> p b k", k=16),
            s.broadcast_to((P, k, 16)), ALU.mult),
            waits=[(sV, st["v_sq"]), (sV, sV.n)], incs=[(sV, 1)])
        R.op("vector", lambda e, o=dst_h, i=yq: e.tensor_copy(o, i),
             waits=[(sV, sV.n)], incs=[(sV, 1)])
        R.op("vector", lambda e, o=dst_l, a=yq, b=dst_h: e.tensor_tensor(
            o, a, b, ALU.subtract), waits=[(sV, sV.n)], incs=[(sV, 1)],
            fence=True)
        qprev[key] = (sV, sV.n)
        return qprev[key]

    def quant_chain(pool, key, src, width, idx, dst_h, dst_l, waits_v,
                    mode=False):
        st = quant_stage1(pool, key, src, width, idx, waits_v)
        return quant_stage2(pool, key, st, dst_h, dst_l, use_pool=mode)

    scPREP = []
    PW = 1024
    pstg = [scope_alloc(scPREP, f"pstg{j}", [P, PW], F32) for j in range(3)]
    psp_h = scope_alloc(scPREP, "psp_h", [P, PW], BF16)
    psp_l = scope_alloc(scPREP, "psp_l", [P, PW], BF16)
    pwtr = [scope_alloc(scPREP, "pwtr0", [P, 8, P], BF16)] * 2
    qpP = {
        "yq": [scope_alloc(scPREP, f"yqP{j}", [P, PW], F32) for j in range(2)],
    }
    qpP["aa"] = qpP["yq"]
    qpP.update({
        "sb": [scope_alloc(scPREP, f"sbP{j}", [P, PW], BF16) for j in range(7)],
        "ub": [scope_alloc(scPREP, f"ubP{j}", [P, PW], BF16) for j in range(2)],
        "sg": [scope_alloc(scPREP, "sgP0", [P, PW], BF16)] * 2,
    })

    scA1 = []
    x2 = [scope_alloc(scA1, f"x2a{j}", [P, C], F32) for j in range(3)]
    xn3 = [scope_alloc(scA1, f"xn3_{j}", [P, C], F32) for j in range(3)]
    scrA1 = scope_alloc(scA1, "scrA1", [P, C], F32)
    wgh = scope_alloc(scA1, "wgh", [P, 8, C], BF16)
    wgl = scope_alloc(scA1, "wgl", [P, 8, C], BF16)
    BG = scope_alloc(scA1, "BG", [P, C], F32)
    qpA = {
        "yq": [b[:, 0:C] for b in qpP["yq"]],
        "aa": [b[:, 0:C] for b in qpP["aa"]],
        "sb": qpP["sb"],
        "ub": qpP["ub"],
        "sg": [scope_alloc(scA1, f"sgA{j}", [P, C], BF16) for j in range(2)],
    }
    xnh_b = [scope_alloc(scA1, f"xnhb{j}", [P, C], BF16) for j in range(2)]
    xnl_b = [scope_alloc(scA1, f"xnlb{j}", [P, C], BF16) for j in range(2)]
    xqh_b = [scope_alloc(scA1, f"xqhb{j}", [P, C], BF16) for j in range(2)]
    xql_b = [scope_alloc(scA1, f"xqlb{j}", [P, C], BF16) for j in range(2)]
    fmh = [scope_alloc(scA1, f"fmh{j}", [P, 8, P], BF16) for j in range(2)]
    fml = [scope_alloc(scA1, f"fml{j}", [P, 8, P], BF16) for j in range(2)]
    fqh = [scope_alloc(scA1, f"fqh{j}", [P, 8, P], BF16) for j in range(2)]
    fql = [scope_alloc(scA1, f"fql{j}", [P, 8, P], BF16) for j in range(2)]

    dma("sync", csts[:, :], csts_d[:, :], sLd)
    dma("sync", BG[:, :], bgt_d[:, :], sLd)
    for cch in range(8):
        w_stg = [R.at(f"wgsp_{cch - 1}")] if cch >= 1 else []
        dma("sync", pstg[cch % 2][:, 0:C], wg_d[cch * P:(cch + 1) * P, :], sLd,
            waits=w_stg)
        ldv = sLd.n
        R.op("vector", lambda e, o=wgh[:, cch, :], i=pstg[cch % 2][:, 0:C]:
             e.tensor_copy(o, i), waits=[(sLd, ldv)], incs=[(sV, 1)])
        R.op("vector", lambda e, o=wgl[:, cch, :], a=pstg[cch % 2][:, 0:C],
             b=wgh[:, cch, :]: e.tensor_tensor(o, a, b, ALU.subtract),
             waits=[(sV, sV.n)], incs=[(sV, 1)], fence=True)
        R.mark(f"wgsp_{cch}", sV)
    R.mark("wg_ready", sV)

    R.op("vector", lambda e, o=kvacc[:, :]: e.memset(o, 0.0), incs=[(sV, 1)])

    s_er_tab = {}
    a1banks = [pp[0], pp[1], psW[:, 0:512], psW[:, 512:1024]]

    ld_tab = {}

    def a1_xload(L):
        w_x = [R.at(f"a1_xfree_{L - 3}")] if L >= 3 else []
        dma("sync", x2[L % 3][:, :], x_d[L * P:(L + 1) * P, :], sLd, waits=w_x)
        ld_tab[L] = sLd.n

    def a1_tail(j, st):
        idx, xnt, s_xn, t_gs, gbt = st
        for nchv in range(2):
            bk = (2 * j + nchv) % 4
            w_gb = [(sT, t_gs[nchv])]
            if nchv == 0 and j >= 2:
                w_gb.append(R.at(f"a1_gatesfree_{j - 2}"))
            if nchv == 0:
                w_gb += R.wtap(f"tf_gates_{j - 2}")
            R.op("vector", lambda e, o=gbt[:, nchv * 512:(nchv + 1) * 512],
                 a=a1banks[bk][:, :], b=BG[:, nchv * 512:(nchv + 1) * 512]:
                 e.scalar_tensor_tensor(o, a, 1.0, b, ALU.mult, ALU.add),
                 waits=w_gb, incs=[(sV, 1)], prepad=True)
            bank_release(f"g{bk}", sV, sV.n)
        v_gb = sV.n
        R.op("scalar", lambda e, o=gbt, i=gbt: e.activation(o[:, :], i[:, :],
             AF.Sigmoid), waits=[(sV, v_gb)], incs=[(sS, 1)])
        s_gates = sS.n
        dma("gpsimd", gates_sp[j * P:(j + 1) * P, :], gbt[:, :], sSt,
            waits=[(sS, s_gates)])
        R.mark(f"a1_gatesfree_{j}", sSt)
        if taps and j < 5:
            dma("gpsimd", tap["t_xn"][j * P:(j + 1) * P, :], xnt[:, :], sSt,
                waits=[(sS, s_xn)])
            R.mark(f"tf_xn_{j}", sSt)
            dma("gpsimd", tap["t_gates"][j * P:(j + 1) * P, :], gbt[:, :], sSt,
                waits=[(sS, s_gates)])
            R.mark(f"tf_gates_{j}", sSt)

        w_qv = [(sS, s_xn)]
        if j >= 2:
            w_qv.append(R.at(f"a1_xqfree_{j - 2}"))
        w_qv += R.wtap(f"tf_xq_{j - 2}")
        rq_t = quant_chain(qpA, "A", xnt[:, :], C, idx, xqh_b[idx][:, :],
                           xql_b[idx][:, :], w_qv, mode="adds")
        R.mark(f"a1_xnfree_{j}", sV)
        w_fq = []
        if j >= 2:
            w_fq = [R.at(f"a1_fqfree_{j - 2}")]
        dma("sync", fqh[idx][:, :, :], xqh_b[idx][:, :], sTr,
            waits=[rq_t] + w_fq, transpose=True)
        dma("sync", fql[idx][:, :, :], xql_b[idx][:, :], sTr, transpose=True)
        tr_fq = sTr.n
        R.ck[f"a1_xqfree_{j}"] = (sTr, tr_fq)
        dma("gpsimd", xqh_sp[j, :, :, :], fqh[idx][:, :, :], sSt,
            waits=[(sTr, tr_fq)])
        dma("gpsimd", xql_sp[j, :, :, :], fql[idx][:, :, :], sSt)
        R.mark(f"a1_fqfree_{j}", sSt)
        if taps and j < 5:
            dma("gpsimd", tap["t_xqh"][j * P:(j + 1) * P, :], xqh_b[idx][:, :],
                sSt, waits=[rq_t])
            dma("gpsimd", tap["t_xql"][j * P:(j + 1) * P, :], xql_b[idx][:, :],
                sSt)
            R.mark(f"tf_xq_{j}", sSt)

    ln_nmr = {}

    def ln_front(j):
        idx = j % 2
        xt = x2[j % 3]
        w_st = [(sLd, ld_tab.pop(j))]
        if j >= 2:
            w_st.append(R.at(f"ln_free_{j - 2}"))
        R.op("scalar", lambda e, o=scrA1[:, :], i=xt, a=acc1[idx]: e.activation(
            o, i[:, :], AF.Copy, accum_out=a[:, :]), waits=w_st, incs=[(sS, 1)])
        R.op("scalar", lambda e, o=scrA1[:, :], i=xt, a=acc2[idx]: e.activation(
            o, i[:, :], AF.Square, accum_out=a[:, :]), waits=[(sS, sS.n)],
            incs=[(sS, 1)], fence=True)
        s_stats = sS.n
        R.op("vector", lambda e, o=tmean[idx], i=acc1[idx]: e.tensor_scalar(
            o[:, :], i[:, :], 1.0 / C, None, ALU.mult), waits=[(sS, s_stats)],
            incs=[(sV, 1)], pad=True, prepad=True)
        R.op("vector", lambda e, o=ln_b[idx], a=tmean[idx], b=tmean[idx]:
             e.tensor_tensor(o[:, :], a[:, :], b[:, :], ALU.mult),
             waits=[(sV, sV.n)], incs=[(sV, 1)], pad=True)
        R.op("vector", lambda e, o=ln_a[idx], i=acc2[idx]: e.tensor_scalar(
            o[:, :], i[:, :], 1.0 / C, None, ALU.mult), waits=[(sV, sV.n)],
            incs=[(sV, 1)], pad=True)
        R.op("vector", lambda e, o=tvar[idx], a=ln_a[idx], b=ln_b[idx]:
             e.tensor_tensor(o[:, :], a[:, :], b[:, :], ALU.subtract),
             waits=[(sV, sV.n)], incs=[(sV, 1)], pad=True)
        R.op("vector", lambda e, o=tvar[idx], i=tvar[idx]: e.tensor_scalar(
            o[:, :], i[:, :], 1e-5, None, ALU.add), waits=[(sV, sV.n)],
            incs=[(sV, 1)], pad=True)
        v_var = sV.n
        if taps and j < 5:
            dma("gpsimd", tap[f"t_dbg{j}"][:, :], acc2[idx][:, :], sSt,
                waits=[(sS, s_stats)])
            dma("gpsimd", tap[f"t_dbg{5 + j}"][:, :], tvar[idx][:, :], sSt,
                waits=[(sV, v_var)])
        R.op("scalar", lambda e, o=ln_a[idx], i=tvar[idx]: e.activation(
            o[:, :], i[:, :], AF.Sqrt), waits=[(sV, v_var)], incs=[(sS, 1)])
        s_rstd = sS.n
        R.op("vector", lambda e, o=ln_b[idx], i=ln_a[idx]: e.reciprocal(
            o[:, :], i[:, :]), waits=[(sS, s_rstd)], incs=[(sV, 1)])
        R.op("vector", lambda e, o=ln_b[idx], a=tvar[idx], b=ln_b[idx]:
             e.tensor_tensor(o[:, :], a[:, :], b[:, :], ALU.mult),
             waits=[(sV, sV.n)], incs=[(sV, 1)], pad=True)
        R.op("vector", lambda e, o=ln_b[idx], a=ln_a[idx], b=ln_b[idx]:
             e.tensor_tensor(o[:, :], a[:, :], b[:, :], ALU.add),
             waits=[(sV, sV.n)], incs=[(sV, 1)], pad=True)
        R.op("vector", lambda e, o=ln_b[idx], i=ln_b[idx]: e.tensor_scalar(
            o[:, :], i[:, :], 0.5, None, ALU.mult), waits=[(sV, sV.n)],
            incs=[(sV, 1)], pad=True)
        R.op("vector", lambda e, o=trstd[idx], i=ln_b[idx]: e.reciprocal(
            o[:, :], i[:, :]), waits=[(sV, sV.n)], incs=[(sV, 1)])
        R.op("vector", lambda e, o=tnmr[idx], a=tmean[idx], b=trstd[idx]:
             e.scalar_tensor_tensor(o[:, :], a[:, :], -1.0, b[:, :],
                                    ALU.mult, ALU.mult), waits=[(sV, sV.n)],
             incs=[(sV, 1)], pad=True)
        ln_nmr[j] = sV.n
        if taps and j < 5:
            dma("gpsimd", tap[f"t_dbg{10 + j}"][:, :], trstd[idx][:, :], sSt,
                waits=[(sV, ln_nmr[j])])
        R.mark(f"ln_free_{j}", sV)

    a1_xload(0)
    a1_xload(1)
    ln_front(0)
    a1_prev = None
    for i in range(NTIL):
        idx = i % 2
        xt = x2[i % 3]
        xnt = xn3[i % 3]
        gbt = g2[idx]

        if i + 2 < NTIL:
            a1_xload(i + 2)

        w_xn = [(sV, ln_nmr.pop(i))]
        if i >= 3:
            w_xn.append(R.at(f"a1_xnfree_{i - 3}"))
        w_xn += R.wtap(f"tf_xn_{i - 3}")
        R.op("scalar", lambda e, o=xnt, i=xt, sc=trstd[idx], b=tnmr[idx]:
             e.activation(o[:, :], i[:, :], AF.Identity, bias=b[:, :], scale=sc[:, :]),
             waits=w_xn, incs=[(sS, 1)])
        s_xn = sS.n
        R.mark(f"a1_xfree_{i}", sS)

        w_sp = [R.at(f"a1_xnhfree_{i - 2}")] if i >= 2 else []
        R.op("vector", lambda e, o=xnh_b[idx], i=xnt: e.tensor_copy(o[:, :], i[:, :]),
             waits=[(sS, s_xn)] + w_sp, incs=[(sV, 1)])
        R.op("vector", lambda e, o=xnl_b[idx], a=xnt, b=xnh_b[idx]: e.tensor_tensor(
            o[:, :], a[:, :], b[:, :], ALU.subtract), waits=[(sV, sV.n)],
            incs=[(sV, 1)], fence=True)
        v_split = sV.n
        w_fm = []
        if i >= 2:
            w_fm = [R.at(f"a1_fmfreeT_{i - 2}"), R.at(f"a1_fmfreeS_{i - 2}")]
        dma("sync", fmh[idx][:, :, :], xnh_b[idx][:, :], sTr,
            waits=[(sV, v_split)] + w_fm, transpose=True)
        dma("sync", fml[idx][:, :, :], xnl_b[idx][:, :], sTr, transpose=True)
        tr_fm = sTr.n
        R.ck[f"a1_xnhfree_{i}"] = (sTr, tr_fm)
        dma("gpsimd", xnh_sp[i, :, :, :], fmh[idx][:, :, :], sSt, waits=[(sTr, tr_fm)])
        dma("gpsimd", xnl_sp[i, :, :, :], fml[idx][:, :, :], sSt)
        R.mark(f"a1_fmfreeS_{i}", sSt)

        t_gs = []
        for nchv in range(2):
            bk = (2 * i + nchv) % 4
            t_g = proj3(a1banks[bk], f"g{bk}", fmh[idx], fml[idx], wgh, wgl, 512,
                        np.s_[nchv * 512:(nchv + 1) * 512],
                        [R.at("wg_ready"), (sTr, tr_fm)])
            t_gs.append(t_g)
        R.mark(f"a1_fmfreeT_{i}", sT)

        if i + 1 < NTIL:
            ln_front(i + 1)

        st_here = (idx, xnt, s_xn, t_gs, gbt)
        if a1_prev is not None:
            a1_tail(i - 1, a1_prev)
        a1_prev = st_here

    a1_tail(NTIL - 1, a1_prev)
    R.bankprev[0] = R.bankprev["g0"]
    R.bankprev[1] = R.bankprev["g1"]

    R.mark("a1_T_done", sT)
    R.mark("a1_V_done", sV)
    R.mark("a1_S_done", sS)
    a1_spills = sSt.n

    prep_descs = []
    for w_d_ap, nchunks, kwidth, sp_h, sp_l in (
            (wgate_d, 22, C, wgh_sp, wgl_sp),
            (wup_d, 22, C, wuh_sp, wul_sp),
            (wdown_d, 8, I, wdh_sp, wdl_sp)):
        for j in range(nchunks):
            for u in range((kwidth + PW - 1) // PW):
                k0 = u * PW
                k1 = min(kwidth, k0 + PW)
                prep_descs.append((w_d_ap, j, k0, k1, k0 // P, sp_h, sp_l))

    prep_state = {"clock": 0, "ld": {}, "st": {}}

    def prep_load(k):
        w_d_ap, j, k0, k1, kofs, sp_h, sp_l = prep_descs[k]
        pidx = k % 3
        wseg = k1 - k0
        w_stage = [R.at(f"prep_stg_free_{pidx}")] if f"prep_stg_free_{pidx}" in R.ck else []
        dma("sync", pstg[pidx][:, 0:wseg], w_d_ap[j * P:(j + 1) * P, k0:k1], sLd,
            waits=w_stage)
        prep_state["ld"][k] = sLd.n

    def prep_stage1(k):
        w_d_ap, j, k0, k1, kofs, sp_h, sp_l = prep_descs[k]
        pidx = k % 3
        wseg = k1 - k0
        st = quant_stage1(qpP, "P", pstg[pidx][:, 0:wseg], wseg, k % 2,
                          [(sLd, prep_state["ld"].pop(k))])
        R.ck[f"prep_stg_free_{pidx}"] = (sV, sV.n)
        prep_state["st"][k] = st

    def prep_stage2(k):
        w_d_ap, j, k0, k1, kofs, sp_h, sp_l = prep_descs[k]
        st = prep_state["st"].pop(k)
        wseg = st["width"]
        ksub = wseg // P
        w2 = [R.at("prep_sp_free")] if "prep_sp_free" in R.ck else []
        rq_t = quant_stage2(qpP, "P", st, psp_h[:, 0:wseg], psp_l[:, 0:wseg],
                            waits2=w2, use_pool=False)
        w_tr = [R.at("prep_wtr_free_1")] if "prep_wtr_free_1" in R.ck else []
        dma("sync", pwtr[0][:, 0:ksub, :], psp_h[:, 0:wseg], sTr,
            waits=[rq_t] + w_tr, transpose=True)
        tr_h = sTr.n
        dma("gpsimd", sp_h[:, kofs:kofs + ksub, j * P:(j + 1) * P],
            pwtr[0][:, 0:ksub, :], sSt, waits=[(sTr, tr_h)])
        R.mark("prep_wtr_free_0", sSt)
        dma("sync", pwtr[1][:, 0:ksub, :], psp_l[:, 0:wseg], sTr,
            waits=[R.at("prep_wtr_free_0")], transpose=True)
        tr_l = sTr.n
        R.ck["prep_sp_free"] = (sTr, tr_l)
        dma("gpsimd", sp_l[:, kofs:kofs + ksub, j * P:(j + 1) * P],
            pwtr[1][:, 0:ksub, :], sSt, waits=[(sTr, tr_l)])
        R.mark("prep_wtr_free_1", sSt)

    NPREP = len(prep_descs)
    prep_pt = {"n": 0}

    def prep_pump():
        prep_pt["n"] += 1
        want = (prep_pt["n"] * (NPREP + 2)) // (4 * NTIL)
        while prep_state["clock"] < want:
            prep_advance()

    def prep_advance():
        k = prep_state["clock"]
        if k < NPREP:
            prep_load(k)
        if 1 <= k < NPREP + 1:
            prep_stage1(k - 1)
        if 2 <= k < NPREP + 2:
            prep_stage2(k - 2)
        prep_state["clock"] += 1

    scope_exit(scA1)

    scA2 = []
    wqh = scope_alloc(scA2, "wqh", [P, 8, 3 * C], BF16)
    wql = scope_alloc(scA2, "wql", [P, 8, 3 * C], BF16)
    qstg = [scope_alloc(scA2, f"qstg{j}", [P, 512], F32) for j in range(2)]
    lfh = [scope_alloc(scA2, f"lfh{j}", [P, 8, P], BF16) for j in range(2)]
    lfl = [scope_alloc(scA2, f"lfl{j}", [P, 8, P], BF16) for j in range(2)]
    zb = [scope_alloc(scA2, "zb0", [P, 512], F32)] * 2
    ebz = [scope_alloc(scA2, "ebz0", [P, 512], F32)] * 2
    k1b = [scope_alloc(scA2, f"k1b{j}", [P, C], F32) for j in range(2)]
    vxb = [scope_alloc(scA2, "vxb0", [P, H, 65], F32)] * 2

    a1_gate = [R.at("a1_T_done"), R.at("a1_V_done"), R.at("a1_S_done"),
               (sSt, a1_spills), (sTr, sTr.n)]

    def wq_piece(k8):
        hf, cch = divmod(k8, 8)
        cs = np.s_[hf * 512:(hf + 1) * 512]
        qsg = qstg[k8 % 2]
        w_stg = [R.at(f"wqsp_{k8 - 2}")] if k8 >= 2 else list(a1_gate)
        dma("sync", qsg[:, :], wqkv_d[cch * P:(cch + 1) * P, cs], sLd,
            waits=w_stg)
        ldv = sLd.n
        R.op("vector", lambda e, o=wqh[:, cch, cs], i=qsg[:, :]:
             e.tensor_copy(o, i), waits=[(sLd, ldv)], incs=[(sV, 1)])
        R.op("vector", lambda e, o=wql[:, cch, cs], a=qsg[:, :],
             b=wqh[:, cch, cs]: e.tensor_tensor(o, a, b, ALU.subtract),
             waits=[(sV, sV.n)], incs=[(sV, 1)], fence=True)
        R.mark(f"wqsp_{k8}", sV)
        if cch == 7:
            R.mark(f"wq_col_{hf}", sV)
        if k8 == 47:
            R.mark("wq_ready", sV)
            R.op("vector", lambda e, o=vxb[0][:, :, 64:65]: e.memset(o, 1.0),
                 incs=[(sV, 1)], fence=True)
            R.mark("vones", sV)

    a2_ldt = {}

    def a2_loads(i):
        idx = i % 2
        w_g = [R.at(f"a1_gatesfree_{i}")]
        if i >= 2:
            w_g.append(R.at(f"a2_gatesbuf_free_{i - 2}"))
        else:
            w_g.append(R.at(f"a1_gatesfree_{NTIL - 2 + i}"))
        dma("sync", g2[idx][:, :], gates_sp[i * P:(i + 1) * P, :], sLd, waits=w_g)
        w_l = [R.at(f"a2_lffree_{i - 2}")] if i >= 2 else \
            list(a1_gate) + [R.at(f"a1_fqfree_{i}")]
        dma("sync", lfh[idx][:, :, :], xnh_sp[i, :, :, :], sLd, waits=w_l)
        dma("sync", lfl[idx][:, :, :], xnl_sp[i, :, :, :], sLd)
        a2_ldt[i] = sLd.n

    a2_loads(0)
    for k8 in range(16):
        wq_piece(k8)
    for i in range(NTIL):
        idx = i % 2
        gt = g2[idx]
        q1t = f2[idx]
        ld_i = a2_ldt.pop(i)
        if i + 1 < NTIL:
            a2_loads(i + 1)

        for nch in range(6):
            if i == 0 and nch >= 1:
                for k8 in range(8 + 8 * nch, min(48, 16 + 8 * nch)):
                    wq_piece(k8)
            bank = pp[nch % 2]
            wq_gate = R.at(f"wq_col_{nch}") if i == 0 else R.at("wq_ready")
            t_mm = proj3(bank, nch % 2, lfh[idx], lfl[idx], wqh, wql, 512,
                         np.s_[nch * 512:(nch + 1) * 512],
                         [wq_gate, (sLd, ld_i)])
            if nch < 4:
                dsts = q1t if nch < 2 else k1b[idx]
                dcol = (nch % 2) * 512
                w_z = [(sT, t_mm)]
                prev_er = s_er_tab.get((i, nch - 1), s_er_tab.get((i - 1, 3)))
                if prev_er is not None and nch <= 3:
                    w_z.append(prev_er)
                if nch < 2 and i >= 2:
                    w_z.append(R.at(f"a2_q1buf_free_{i - 2}"))
                if nch >= 2 and i >= 2:
                    w_z.append(R.at(f"a2_einsum_{i - 2}"))
                    w_z += R.wtap(f"tf_k1_{i - 2}")
                R.op("vector", lambda e, o=zb[nch % 2][:, :], a=bank[:, :],
                     b=gt[:, dcol:dcol + 512]:
                     e.tensor_tensor(o, a, b, ALU.mult), waits=w_z, incs=[(sV, 1)],
                     prepad=True)
                bank_release(nch % 2, sV, sV.n)
                v_z = sV.n
                R.op("vector", lambda e, o=ebz[nch % 2][:, :], i=zb[nch % 2][:, :]:
                     e.tensor_scalar(o, i, 0.0, None, ALU.min), waits=[(sV, v_z)],
                     incs=[(sV, 1)])
                v_zm = sV.n
                R.op("scalar", lambda e, o=ebz[nch % 2][:, :], i=ebz[nch % 2][:, :]:
                     e.activation(o, i, AF.Exp), waits=[(sV, v_zm)])
                R.op("scalar", lambda e, o=zb[nch % 2][:, :], i=zb[nch % 2][:, :]:
                     e.activation(o, i, AF.Relu), incs=[(sS, 1)])
                s_er = sS.n
                s_er_tab[(i, nch)] = (sS, s_er)
                R.op("vector", lambda e, o=dsts[:, dcol:dcol + 512],
                     a=ebz[nch % 2][:, :], b=zb[nch % 2][:, :]:
                     e.tensor_tensor(o, a, b, ALU.add), waits=[(sS, s_er)],
                     incs=[(sV, 1)], fence=(nch >= 2))
                if nch == 1:
                    R.mark(f"a2_q1done_{i}", sV)
                if nch == 3:
                    R.mark(f"a2_k1done_{i}", sV)
                    R.mark(f"a2_gatesbuf_free_{i}", sV)
                if nch in (1, 3):
                    prep_pump()
            else:
                h0 = (nch - 4) * 8
                w_v = [(sT, t_mm)]
                if nch == 4 and i >= 1:
                    w_v.append(R.at(f"a2_vfree_{i - 1}"))
                R.op("scalar", lambda e, o=vxb[idx][:, h0:h0 + 8, 0:64], a=bank[:, :]:
                     e.activation(o, a.rearrange("p (h d) -> p h d", d=64), AF.Copy),
                     waits=w_v, incs=[(sS, 1)], prepad=True, fence=(nch == 5))
                bank_release(nch % 2, sS, sS.n)
                if nch == 5:
                    R.mark(f"a2_vdone_{i}", sS)

        prep_pump()

        dma("gpsimd", q1_sp[i * P:(i + 1) * P, :], q1t[:, :], sSt,
            waits=[R.at(f"a2_q1done_{i}")])
        R.mark(f"a2_q1buf_free_{i}", sSt)
        if taps and i < 5:
            dma("gpsimd", tap["t_q1"][i * P:(i + 1) * P, :], q1t[:, :], sSt)
            R.mark(f"a2_q1buf_free_{i}", sSt)
            dma("gpsimd", tap["t_k1"][i * P:(i + 1) * P, :], k1b[idx][:, :], sSt,
                waits=[R.at(f"a2_k1done_{i}")])
            R.mark(f"tf_k1_{i}", sSt)

        w_kv = [R.at(f"a2_k1done_{i}"), R.at(f"a2_vdone_{i}"), R.at("vones")]
        if i >= 1:
            w_kv.append(R.at(f"a2_kvadd_{i - 1}"))
        for h in range(H):
            R.op("tensor", lambda e, o=ps_kv[0:65, h * 64:(h + 1) * 64],
                 l=vxb[idx][:, h, :], r=k1b[idx][:, h * 64:(h + 1) * 64]:
                 e.matmul(o, l, r, start=True, stop=True),
                 waits=w_kv if h == 0 else [],
                 incs=[(sT, 1)] if h == H - 1 else [])
        t_ein = sT.n
        R.mark(f"a2_einsum_{i}", sT)
        R.mark(f"a2_vfree_{i}", sT)
        R.mark(f"a2_lffree_{i}", sT)
        R.op("vector", lambda e, o=kvacc[:, :], a=ps_kv[0:65, :], b=kvacc[:, :]:
             e.scalar_tensor_tensor(o, a, 1.0, b, ALU.mult, ALU.add),
             waits=[(sT, t_ein)], incs=[(sV, 1)], prepad=True, fence=True)
        R.mark(f"a2_kvadd_{i}", sV)

        prep_pump()

    while prep_state["clock"] < NPREP + 2:
        prep_advance()
    R.mark("wprep_done", sSt)
    R.mark("prep_tr_done", sTr)
    R.mark("a2_S_done", sS)
    R.mark("a2_T_done", sT)
    R.mark("a2_kv_done", sV)
    scope_exit(scA2)
    scope_exit(scPREP)

    def post_a2_gate():
        return [R.at("a2_T_done"), R.at("a2_kv_done"), R.at("wprep_done"),
                R.at("prep_tr_done"), R.at("a2_S_done")]

    scCW = []
    x2 = [scope_alloc(scCW, f"x2c{j}", [P, C], F32) for j in range(2)]
    e2 = [scope_alloc(scCW, f"e2_{j}", [P, C], F32) for j in range(4)]
    cgh = scope_alloc(scCW, "cgh", [P, 8, 1024], BF16)
    cgl = scope_alloc(scCW, "cgl", [P, 8, 1024], BF16)
    cuh = scope_alloc(scCW, "cuh", [P, 8, 1024], BF16)
    cul = scope_alloc(scCW, "cul", [P, 8, 1024], BF16)
    cdh = scope_alloc(scCW, "cdh", [P, 8, C], BF16)
    cdl = scope_alloc(scCW, "cdl", [P, 8, C], BF16)

    scIL = []
    kv_all = scope_alloc(scIL, "kv_all", [65, C], F32)
    KSb = scope_alloc(scIL, "KSb", [P, C], F32)
    w2h = scope_alloc(scIL, "w2h", [P, 8, C], BF16)
    w2l = scope_alloc(scIL, "w2l", [P, 8, C], BF16)
    scILb = []
    wpst = [scope_alloc(scILb, f"wpst{j}", [64, C], F32) for j in range(2)]

    if taps:
        dma("gpsimd", tap["t_kvpart"][:, :], kvacc[:, :], sSt,
            waits=[R.at("a2_kv_done")])
    dma("gpsimd", cc_in[:, :], kvacc[:, :], sSt, waits=[R.at("a2_kv_done")])
    st_cc = sSt.n
    R.op("gpsimd", lambda e: e.collective_compute(
        "AllReduce", ALU.add, replica_groups=[[0, 1, 2, 3], [4, 5, 6, 7]],
        ins=[cc_in[:, :]], outs=[cc_out[:, :]]),
        waits=[(sSt, st_cc)], incs=[(sCC, 1)])
    dma("gpsimd", kv_all[:, :], cc_out[:, :], sSt,
        waits=[(sCC, 1)] + post_a2_gate())
    st_kv = sSt.n
    if taps:
        dma("gpsimd", tap["t_kvall"][:, :], kv_all[:, :], sSt, waits=[(sSt, st_kv)])

    dma("sync", KSb[0:1, :], kv_all[64:65, :], sTr,
        waits=[(sSt, st_kv)] + post_a2_gate())
    kk = 1
    while kk < P:
        step = min(kk, P - kk)
        dma("sync", KSb[kk:kk + step, :], KSb[0:step, :], sTr,
            waits=[(sTr, sTr.n)])
        kk += step
    tr_ks = sTr.n
    if taps:
        dma("gpsimd", tap["t_ksb"][:, :], KSb[:, :], sSt, waits=[(sTr, tr_ks)])

    for pr in range(8):
        for hh in range(2):
            h = 2 * pr + hh
            w_wp = []
            if pr >= 1 and hh == 0:
                w_wp.append(R.at(f"w2evac_{pr - 1}"))
            if pr == 0 and hh == 0:
                w_wp.append((sSt, st_kv))
                w_wp.extend(post_a2_gate())
            dma("sync", wpst[hh][:, :], wp_d[h * 64:(h + 1) * 64, :], sLd,
                waits=w_wp)
            ld_wp = sLd.n
            for nch2 in range(2):
                w_mm = [(sLd, ld_wp)]
                if pr >= 1 and hh == 0 and nch2 == 0:
                    w_mm.append(R.at(f"w2evac_{pr - 1}"))
                R.op("tensor", lambda e,
                     o=psW[hh * 64:hh * 64 + 64, nch2 * 512:(nch2 + 1) * 512],
                     l=kv_all[0:64, h * 64:(h + 1) * 64],
                     r=wpst[hh][:, nch2 * 512:(nch2 + 1) * 512]:
                     e.matmul(o, l, r, start=True, stop=True),
                     waits=w_mm,
                     incs=[(sT, 1)] if (hh == 1 and nch2 == 1) else [])
        t_w2 = sT.n
        R.op("vector", lambda e, o=w2h[:, pr, :], i=psW[:, :]: e.tensor_copy(o, i),
             waits=[(sT, t_w2)], prepad=True, incs=[(sV, 1)])
        R.op("vector", lambda e, o=w2l[:, pr, :], a=psW[:, :], b=w2h[:, pr, :]:
             e.tensor_tensor(o, a, b, ALU.subtract), waits=[(sV, sV.n)],
             incs=[(sV, 1)], fence=True)
        R.mark(f"w2evac_{pr}", sV)
        if taps and pr == 0:
            dma("gpsimd", tap["t_w2h"][:, :], w2h[:, 0, :], sSt,
                waits=[R.at("w2evac_0")])
            dma("gpsimd", tap["t_w2l"][:, :], w2l[:, 0, :], sSt)
    R.mark("w2_ready", sV)
    R.bankprev["c2"] = R.at("w2_ready")
    R.bankprev["c3"] = R.at("w2_ready")
    scope_exit(scILb)

    scB = []
    BP_t = scope_alloc(scB, "BPt", [P, C], F32)
    nmb = scope_alloc(scB, "nmb", [P, C], F32)
    n16 = [scope_alloc(scB, f"n16_{j}", [P, H], F32) for j in range(2)]
    n16r = [scope_alloc(scB, f"n16r_{j}", [P, H], F32) for j in range(2)]
    qnh_b = [scope_alloc(scB, f"qnh{j}", [P, C], BF16) for j in range(2)]
    qnl_b = [scope_alloc(scB, f"qnl{j}", [P, C], BF16) for j in range(2)]
    bfh = [scope_alloc(scB, f"bfh{j}", [P, 8, P], BF16) for j in range(2)]
    bfl = [scope_alloc(scB, f"bfl{j}", [P, 8, P], BF16) for j in range(2)]

    dma("sync", BP_t[:, :], bpt_d[:, :], sLd, waits=post_a2_gate())
    ld_bp = sLd.n

    def load_cw(third, waits):
        n0, n1 = N3[third]
        w = list(waits)
        last = 0
        for buf, spl in ((cgh, wgh_sp), (cgl, wgl_sp), (cuh, wuh_sp), (cul, wul_sp)):
            for kc in range(8):
                dma("sync", buf[:, kc, 0:n1 - n0], spl[:, kc, n0:n1], sLd, waits=w)
                w = []
        for buf, spl in ((cdh, wdh_sp), (cdl, wdl_sp)):
            for j in range((n1 - n0) // P):
                kc = n0 // P + j
                dma("sync", buf[:, j, :], spl[:, kc, :], sLd)
        return sLd.n

    ld_cw0 = load_cw(0, post_a2_gate())
    if taps:
        dma("gpsimd", tap["t_cgh"][:, :, :], cgh[:, :, :], sSt, waits=[(sLd, ld_cw0)])
        dma("gpsimd", tap["t_cgl"][:, :, :], cgl[:, :, :], sSt)
        dma("gpsimd", tap["t_cdh"][:, :, :], cdh[:, :, :], sSt)
        dma("gpsimd", tap["t_cdl"][:, :, :], cdl[:, :, :], sSt)

    b_ldt = {}
    bbanks = [pp[0], pp[1], psD[0], psD[1]]
    R.bankprev["b0"] = R.bankprev.pop(0)
    R.bankprev["b1"] = R.bankprev.pop(1)

    def b_loads(i):
        if i >= 2:
            w_q1 = [R.at(f"b_q1free_{i - 2}")]
        else:
            w_q1 = post_a2_gate()
        dma("sync", f2[i % 2][:, :], q1_sp[i * P:(i + 1) * P, :], sLd, waits=w_q1)
        w_x = [(sLd, ld_bp)]
        if i >= 2:
            w_x.append(R.at(f"b_xfree_{i - 2}"))
        dma("sync", x2[i % 2][:, :], x_d[i * P:(i + 1) * P, :], sLd, waits=w_x)
        b_ldt[i] = sLd.n

    def b_tail(j, st):
        xt, pout, t_bs, v_xbp = st
        for nch in range(2):
            bk = (2 * j + nch) % 4
            w_ev = [(sT, t_bs[nch]), (sV, v_xbp)]
            if nch == 0 and j >= 2:
                w_ev.append(R.at(f"b_poutfree_{j - 2}"))
            R.op("vector", lambda e, o=pout[:, nch * 512:(nch + 1) * 512],
                 a=bbanks[bk][:, :], b=xt[:, nch * 512:(nch + 1) * 512]:
                 e.scalar_tensor_tensor(o, a, 1.0, b, ALU.mult, ALU.add),
                 waits=w_ev, incs=[(sV, 1)], prepad=True)
            bank_release(f"b{bk}", sV, sV.n)
        v_pout = sV.n
        R.mark(f"b_xfree_{j}", sV)
        dma("gpsimd", part_sp[j * P:(j + 1) * P, :], pout[:, :], sSt,
            waits=[(sV, v_pout)])
        R.mark(f"b_poutfree_{j}", sSt)
        if taps and j < 2:
            dma("gpsimd", tap["t_part"][j * P:(j + 1) * P, :], pout[:, :], sSt)
            R.mark(f"b_poutfree_{j}", sSt)

    b_loads(0)
    b_prev = None
    for i in range(NTIL):
        idx = i % 2
        q1t = f2[idx]
        xt = x2[i % 2]
        pout = e2[idx]
        ld_b = b_ldt.pop(i)

        w_nm = [(sLd, ld_b), (sTr, tr_ks)]
        if i == 1:
            w_nm += R.wtap("tf_qn")
        R.op("vector", lambda e, o=nmb[:, :], a=q1t, b=KSb: e.tensor_tensor(
            o, a[:, :], b[:, :], ALU.mult), waits=w_nm, incs=[(sV, 1)])
        R.op("vector", lambda e, o=n16[idx][:, :], i=nmb[:, :]: e.tensor_reduce(
            o, i.rearrange("p (h d) -> p h d", d=64), AX.X, ALU.add),
            waits=[(sV, sV.n)], incs=[(sV, 1)], pad=True)
        R.op("vector", lambda e, o=n16[idx][:, :], i=n16[idx][:, :]: e.tensor_scalar(
            o, i, 1e-8, None, ALU.add), waits=[(sV, sV.n)], incs=[(sV, 1)],
            pad=True)
        R.op("vector", lambda e, o=n16r[idx][:, :], i=n16[idx][:, :]: e.reciprocal(o, i),
             waits=[(sV, sV.n)], incs=[(sV, 1)])
        if taps and i == 0:
            dma("gpsimd", tap["t_norm"][:, :], n16r[idx][:, :], sSt,
                waits=[(sV, sV.n)])
        R.op("vector", lambda e, o=nmb, a=q1t, b=n16r[idx]: e.tensor_tensor(
            o[:, :].rearrange("p (h d) -> p h d", d=64),
            a[:, :].rearrange("p (h d) -> p h d", d=64),
            b[:, :].broadcast_to((P, H, 64)), ALU.mult), waits=[(sV, sV.n)],
            incs=[(sV, 1)])
        R.mark(f"b_q1free_{i}", sV)
        if taps and i == 0:
            dma("gpsimd", tap["t_qn"][:, :], nmb[:, :], sSt,
                waits=[R.at("b_q1free_0")])
            R.mark("tf_qn", sSt)
        w_qn = [R.at(f"b_qnfree_{i - 2}")] if i >= 2 else []
        R.op("vector", lambda e, o=qnh_b[idx], i=nmb: e.tensor_copy(o[:, :], i[:, :]),
             waits=w_qn + [R.at(f"b_q1free_{i}")], incs=[(sV, 1)])
        R.op("vector", lambda e, o=qnl_b[idx], a=nmb, b=qnh_b[idx]: e.tensor_tensor(
            o[:, :], a[:, :], b[:, :], ALU.subtract), waits=[(sV, sV.n)],
            incs=[(sV, 1)], fence=True)
        v_qn = sV.n
        w_bf = [R.at(f"b_bffree_{i - 2}")] if i >= 2 else []
        dma("sync", bfh[idx][:, :, :], qnh_b[idx][:, :], sTr,
            waits=[(sV, v_qn)] + w_bf, transpose=True)
        dma("sync", bfl[idx][:, :, :], qnl_b[idx][:, :], sTr, transpose=True)
        tr_b = sTr.n
        R.ck[f"b_qnfree_{i}"] = (sTr, tr_b)
        if taps and i == 0:
            dma("gpsimd", tap["t_qnh"][:, :], qnh_b[idx][:, :], sSt,
                waits=[(sV, v_qn)])
            dma("gpsimd", tap["t_qnl"][:, :], qnl_b[idx][:, :], sSt)
            dma("gpsimd", tap["t_bfh"][:, :, :], bfh[idx][:, :, :], sSt,
                waits=[(sTr, tr_b)])
            R.ck[f"b_qnfree_{i}"] = (sSt, sSt.n)

        w_px = [(sLd, ld_b)]
        R.op("vector", lambda e, o=xt, a=xt, b=BP_t: e.tensor_tensor(
            o[:, :], a[:, :], b[:, :], ALU.add), waits=w_px, incs=[(sV, 1)])
        v_xbp = sV.n

        t_bs = []
        for nch in range(2):
            bk = (2 * i + nch) % 4
            t_b = proj3(bbanks[bk], f"b{bk}", bfh[idx], bfl[idx], w2h, w2l, 512,
                        np.s_[nch * 512:(nch + 1) * 512],
                        [R.at("w2_ready"), (sTr, tr_b)])
            t_bs.append(t_b)
        R.mark(f"b_bffree_{i}", sT)

        st_here = (xt, pout, t_bs, v_xbp)
        if b_prev is not None:
            b_tail(i - 1, b_prev)
        b_prev = st_here
        if i + 1 < NTIL:
            b_loads(i + 1)

    b_tail(NTIL - 1, b_prev)
    R.bankprev[0] = R.bankprev["b0"]
    R.bankprev[1] = R.bankprev["b1"]
    R.bankprev["D0_0"] = R.bankprev["b2"]
    R.bankprev["D1_0"] = R.bankprev["b3"]

    R.mark("b_T_done", sT)
    R.mark("b_V_done", sV)
    scope_exit(scB)
    scope_exit(scIL)

    scC = []
    qpC = {
        "yq": [scope_alloc(scC, f"yqC{j}", [P, 1024], F32) for j in range(2)],
        "sb": [scope_alloc(scC, f"sbC{j}", [P, 1024], BF16) for j in range(7)],
        "ub": [scope_alloc(scC, f"ubC{j}", [P, 1024], BF16) for j in range(2)],
        "sg": [scope_alloc(scC, "sgC0", [P, 1024], BF16)] * 2,
    }
    qpC["aa"] = qpC["yq"]
    xfh = [scope_alloc(scC, f"xfh{j}", [P, 8, P], BF16) for j in range(2)]
    xfl = [scope_alloc(scC, f"xfl{j}", [P, 8, P], BF16) for j in range(2)]
    hqh_b = [scope_alloc(scC, f"hqh{j}", [P, 1024], BF16) for j in range(2)]
    hql_b = [scope_alloc(scC, f"hql{j}", [P, 1024], BF16) for j in range(2)]
    hfh = [scope_alloc(scC, f"hfh{j}", [P, 8, P], BF16) for j in range(4)]
    hfl = [scope_alloc(scC, f"hfl{j}", [P, 8, P], BF16) for j in range(4)]

    R.bankprev["D0_1"] = R.at("a2_kv_done")
    R.bankprev["D1_1"] = R.at("a2_kv_done")
    R.bankprev["c0"] = R.bankprev.get(0)
    R.bankprev["c1"] = R.bankprev.get(1)
    if R.bankprev["c0"] is None:
        del R.bankprev["c0"]
    if R.bankprev["c1"] is None:
        del R.bankprev["c1"]

    def c_down(third, i, st_c):
        TN, kch, idx4, tr_h, pin, ld_pin = st_c
        dbanks = [psD[0], psD[1], ps_kv[:, 0:512], ps_kv[:, 512:1024]]
        for mch in range(2):
            bank = dbanks[mch + 2 * (i % 2)]
            triples = []
            for kc in range(kch):
                triples.append((bank[:, :], hfh[idx4][:, kc, :],
                                cdh[:, kc, mch * 512:(mch + 1) * 512]))
                triples.append((bank[:, :], hfh[idx4][:, kc, :],
                                cdl[:, kc, mch * 512:(mch + 1) * 512]))
                triples.append((bank[:, :], hfl[idx4][:, kc, :],
                                cdh[:, kc, mch * 512:(mch + 1) * 512]))
            mm_group(bank, f"D{mch}_{i % 2}", triples, [(sTr, tr_h)])
            t_d = sT.n
            R.op("vector", lambda e, o=pin[:, mch * 512:(mch + 1) * 512],
                 a=bank[:, :], b=pin[:, mch * 512:(mch + 1) * 512]:
                 e.scalar_tensor_tensor(o, a, 1.0, b, ALU.mult, ALU.add),
                 waits=[(sT, t_d), (sLd, ld_pin)], incs=[(sV, 1)], prepad=True)
            bank_release(f"D{mch}_{i % 2}", sV, sV.n)
        v_p = sV.n
        R.mark(f"c_hffree_{third}_{i}", sT)
        if third < 2:
            dma("gpsimd", part_sp[i * P:(i + 1) * P, :], pin[:, :], sSt,
                waits=[(sV, v_p)])
        else:
            dma("gpsimd", y_d[i * P:(i + 1) * P, :], pin[:, :], sSt,
                waits=[(sV, v_p)])
        R.mark(f"c_pdone_{third}_{i}", sSt)
        R.mark(f"c_pinfree_{third}_{i}", sSt)

    for third in range(3):
        n0, n1 = N3[third]
        TN = n1 - n0
        kch = TN // P
        if third > 0:
            ld_cw = load_cw(third, [R.at(f"c_T_done_{third - 1}")])
        else:
            ld_cw = ld_cw0
        nck = [(j * 512, min(TN, (j + 1) * 512)) for j in range((TN + 511) // 512)]
        nnc = len(nck)
        c_pending = []
        ldt = {}

        def c_loads(i):
            idx = i % 2
            idx4 = i % 4
            if i >= 2:
                w_xf = [R.at(f"c_xffree_{third}_{i - 2}")]
            elif third > 0:
                w_xf = [R.at(f"c_xffree_{third - 1}_{NTIL - 2 + i}")]
            else:
                w_xf = [R.at("b_V_done"), R.at("b_T_done")]
            if i == 0:
                w_xf.append((sLd, ld_cw))
            dma("sync", xfh[idx][:, :, :], xqh_sp[i, :, :, :], sLd, waits=w_xf)
            dma("sync", xfl[idx][:, :, :], xql_sp[i, :, :, :], sLd)
            ld_c = sLd.n
            if third == 0:
                w_pin = [R.at(f"b_poutfree_{max(i, NTIL - 2 + (i % 2))}")]
            else:
                w_pin = [R.at(f"c_pdone_{third - 1}_{i}")]
            if i >= 4:
                w_pin.append(R.at(f"c_pinfree_{third}_{i - 4}"))
            elif third > 0:
                w_pin.append(R.at(f"c_pinfree_{third - 1}_{NTIL - 4 + i}"))
            dma("sync", e2[idx4][:, :], part_sp[i * P:(i + 1) * P, :], sLd,
                waits=w_pin)
            ldt[i] = (ld_c, sLd.n)

        c_loads(0)
        for i in range(NTIL):
            idx = i % 2
            idx4 = i % 4
            hbuf = g2[idx]
            sgt = x2[idx]
            upt = f2[idx]
            pin = e2[idx4]
            ld_c, ld_pin = ldt.pop(i)
            if i + 1 < NTIL:
                c_loads(i + 1)

            cbanks = [pp[0], pp[1], psW[:, 0:512], psW[:, 512:1024]]
            for gu, (wbh, wbl) in enumerate(((cgh, cgl), (cuh, cul))):
                dst = sgt if gu == 0 else upt
                for ncj, (c0, c1) in enumerate(nck):
                    seq = gu * nnc + ncj
                    bank = cbanks[seq % 4]
                    t_c = proj3(bank, f"c{seq % 4}", xfh[idx], xfl[idx], wbh, wbl,
                                c1 - c0, np.s_[c0:c1], [(sLd, ld_c)])
                    w_ev = [(sT, t_c)]
                    if ncj == 0:
                        if third == 0 and i < 2:
                            w_ev.append(R.at("b_V_done"))
                        elif i >= 2:
                            w_ev.append(R.at(f"c_sufree_{third}_{i - 2}"))
                        elif third > 0:
                            w_ev.append(R.at(f"c_sufree_{third - 1}_{NTIL - 2 + i}"))
                    R.op("scalar", lambda e, o=dst[:, c0:c1], i=bank[:, 0:c1 - c0],
                         f=(AF.Silu if gu == 0 else AF.Copy):
                         e.activation(o, i, f), waits=w_ev, incs=[(sS, 1)],
                         prepad=True)
                    bank_release(f"c{seq % 4}", sS, sS.n)
            s_gu = sS.n
            R.mark(f"c_xffree_{third}_{i}", sT)

            if len(c_pending) >= 2:
                c_down(third, i - 2, c_pending.pop(0))

            w_h = []
            if i >= 1:
                w_h = [R.at(f"c_hfree_{third}_{i - 1}")]
            elif third > 0:
                w_h = [R.at(f"c_hfree_{third - 1}_{NTIL - 1}")]
            if third == 0 and i == 1:
                w_h += R.wtap("tf_h")
            R.op("vector", lambda e, o=hbuf[:, 0:TN], a=sgt[:, 0:TN], b=upt[:, 0:TN]:
                 e.tensor_tensor(o, a, b, ALU.mult), waits=[(sS, s_gu)] + w_h,
                 incs=[(sV, 1)])
            R.mark(f"c_sufree_{third}_{i}", sV)
            v_h = sV.n
            if taps and third == 0 and i == 0:
                dma("gpsimd", tap["t_h"][:, 0:TN], hbuf[:, 0:TN], sSt,
                    waits=[(sV, v_h)])
                R.mark("tf_h", sSt)

            w_qv = [(sV, v_h)]
            if third == 0 and i < 2:
                w_qv.append(R.at("b_V_done"))
                w_qv.append(R.at("b_T_done"))
            rq_t = quant_chain(qpC, "C", hbuf[:, 0:TN], TN, idx,
                               hqh_b[idx][:, 0:TN], hql_b[idx][:, 0:TN], w_qv)
            R.mark(f"c_hfree_{third}_{i}", sV)

            w_hf = []
            if i >= 4:
                w_hf = [R.at(f"c_hffree_{third}_{i - 4}")]
            elif third > 0:
                w_hf = [R.at(f"c_hffree_{third - 1}_{NTIL - 4 + i}")]
            else:
                w_hf = [R.at("b_T_done")]
            dma("sync", hfh[idx4][:, 0:kch, :], hqh_b[idx][:, 0:TN], sTr,
                waits=[rq_t] + w_hf, transpose=True)
            dma("sync", hfl[idx4][:, 0:kch, :], hql_b[idx][:, 0:TN], sTr,
                transpose=True)
            tr_h = sTr.n
            c_pending.append((TN, kch, idx4, tr_h, pin, ld_pin))

        for joff, st_c in enumerate(c_pending):
            c_down(third, NTIL - len(c_pending) + joff, st_c)
        c_pending = []
        R.mark(f"c_T_done_{third}", sT)

    scope_exit(scC)
    scope_exit(scCW)

    final_st = sSt.n
    R.op("gpsimd", None, waits=[(sSt, final_st)])

    with ExitStack() as es:
        for s in R.sems:
            s.h = es.enter_context(nc.semaphore(s.name))
        with nc.Block() as block:
            @block.sync
            def _(e):
                R.replay(e, "sync")

            @block.scalar
            def _(e):
                R.replay(e, "scalar")

            @block.vector
            def _(e):
                R.replay(e, "vector")

            @block.tensor
            def _(e):
                R.replay(e, "tensor")

            @block.gpsimd
            def _(e):
                R.replay(e, "gpsimd")

    return nc


_NC_CACHE = {}


def _get_nc(taps=False):
    key = bool(taps)
    if key not in _NC_CACHE:
        _NC_CACHE[key] = build_program(taps=key)
    return _NC_CACHE[key]


def make_inputs(L0_attn_x, L0_attn_Wqkv, L0_attn_Wg, L0_attn_bg, L0_attn_Wp,
                L0_attn_bp, L0_ffn_w_gate, L0_ffn_w_up, L0_ffn_w_down):
    x = np.ascontiguousarray(np.asarray(L0_attn_x, dtype=np.float32))
    wqkv = np.ascontiguousarray(np.asarray(L0_attn_Wqkv, dtype=np.float32))
    wg = np.ascontiguousarray(np.asarray(L0_attn_Wg, dtype=np.float32))
    bg = np.asarray(L0_attn_bg, dtype=np.float32)
    wp = np.ascontiguousarray(np.asarray(L0_attn_Wp, dtype=np.float32))
    bp = np.asarray(L0_attn_bp, dtype=np.float32)
    wgt = np.ascontiguousarray(np.asarray(L0_ffn_w_gate, dtype=np.float32))
    wup = np.ascontiguousarray(np.asarray(L0_ffn_w_up, dtype=np.float32))
    wdn = np.ascontiguousarray(np.asarray(L0_ffn_w_down, dtype=np.float32))

    bgt = np.ascontiguousarray(np.tile(bg[None, :], (P, 1)))
    bpt = np.ascontiguousarray(np.tile(bp[None, :], (P, 1)))
    csts = np.zeros((P, 8), dtype=np.float32)
    csts[:, 0] = 1e-5
    for j, t in enumerate(THR):
        csts[:, 1 + j] = -t

    in_maps = []
    for c in range(8):
        b, qt = divmod(c, 4)
        xs = np.ascontiguousarray(x[b, qt * NT:(qt + 1) * NT, :])
        in_maps.append({
            "x": xs, "wqkv": wqkv, "wg": wg, "wp": wp,
            "wgate": wgt, "wup": wup, "wdown": wdn,
            "bgt": bgt, "bpt": bpt, "csts_d": csts,
        })
    return in_maps


def kernel(L0_attn_x, L0_attn_Wqkv, L0_attn_Wg, L0_attn_bg, L0_attn_Wp,
           L0_attn_bp, L0_ffn_w_gate, L0_ffn_w_up, L0_ffn_w_down):
    in_maps = make_inputs(L0_attn_x, L0_attn_Wqkv, L0_attn_Wg, L0_attn_bg,
                          L0_attn_Wp, L0_attn_bp, L0_ffn_w_gate, L0_ffn_w_up,
                          L0_ffn_w_down)
    nc = _get_nc(taps=False)
    res = run_bass_kernel_spmd(nc, in_maps, core_ids=list(range(8)))
    y = np.empty((2, 4 * NT, C), dtype=np.float32)
    for c in range(8):
        b, qt = divmod(c, 4)
        y[b, qt * NT:(qt + 1) * NT, :] = res.results[c]["y"]
    return y


# version 20
# speedup vs baseline: 1.0421x; 1.0421x over previous
import numpy as np
from contextlib import ExitStack

import concourse.bass as bass
import concourse.mybir as mybir
from concourse.bass_utils import run_bass_kernel_spmd

F32 = mybir.dt.float32
BF16 = mybir.dt.bfloat16
AF = mybir.ActivationFunctionType
ALU = mybir.AluOpType
AX = mybir.AxisListType

P = 128
C = 1024
NT = 4096
NTIL = NT // P
H = 16
Dh = 64
I = 2816
N3 = [(0, 1024), (1024, 2048), (2048, 2816)]
THR = [0.25, 0.75, 1.25, 1.75, 2.5, 3.5, 5.0]

ENGS = ("sync", "scalar", "vector", "tensor", "gpsimd")


class Sem:
    __slots__ = ("name", "n", "h")

    def __init__(self, name):
        self.name, self.n, self.h = name, 0, None


class Rec:

    def __init__(self):
        self.ops = {e: [] for e in ENGS}
        self.sems = []
        self.ck = {}
        self.bankprev = {}
        self.jnk = None
        self.jnkS = None

    def sem(self, name):
        s = Sem(name)
        self.sems.append(s)
        return s

    def _junk(self, eng):
        if eng == "scalar":
            j = self.jnkS
            return lambda e, o=j: e.memzero(o)
        j = self.jnk
        return lambda e, o=j: e.memset(o, 1.0)

    def op(self, eng, fn, waits=(), incs=(), pad=False, prepad=False, fence=False):
        w = tuple((s, v) for s, v in waits if v > 0)
        if prepad:
            self.ops[eng].append((w, self._junk(eng), ()))
        if fence:
            self.ops[eng].append((w, fn, ()))
            for s, k in incs:
                s.n += k
            self.ops[eng].append(((), self._junk(eng), tuple(incs)))
        else:
            for s, k in incs:
                s.n += k
            self.ops[eng].append((w, fn, tuple(incs)))
        if pad:
            self.ops[eng].append(((), self._junk(eng), ()))

    def mark(self, name, sem):
        self.ck[name] = (sem, sem.n)

    def at(self, name):
        return self.ck[name]

    def wtap(self, name):
        return [self.ck[name]] if name in self.ck else []

    def replay(self, e, eng):
        lastw = {}
        for waits, fn, incs in self.ops[eng]:
            for s, v in waits:
                if lastw.get(id(s), 0) >= v:
                    continue
                e.wait_ge(s.h, v)
                lastw[id(s)] = v
            ins = fn(e) if fn is not None else None
            for s, k in incs:
                ins.then_inc(s.h, k)


def build_program(taps=False):
    nc = bass.Bass("TRN2", target_bir_lowering=False)

    x_d = nc.dram_tensor("x", [NT, C], F32, kind="ExternalInput").ap()
    wqkv_d = nc.dram_tensor("wqkv", [C, 3 * C], F32, kind="ExternalInput").ap()
    wg_d = nc.dram_tensor("wg", [C, C], F32, kind="ExternalInput").ap()
    wp_d = nc.dram_tensor("wp", [C, C], F32, kind="ExternalInput").ap()
    wgate_d = nc.dram_tensor("wgate", [I, C], F32, kind="ExternalInput").ap()
    wup_d = nc.dram_tensor("wup", [I, C], F32, kind="ExternalInput").ap()
    wdown_d = nc.dram_tensor("wdown", [C, I], F32, kind="ExternalInput").ap()
    bgt_d = nc.dram_tensor("bgt", [P, C], F32, kind="ExternalInput").ap()
    bpt_d = nc.dram_tensor("bpt", [P, C], F32, kind="ExternalInput").ap()
    csts_d = nc.dram_tensor("csts_d", [P, 8], F32, kind="ExternalInput").ap()
    y_d = nc.dram_tensor("y", [NT, C], F32, kind="ExternalOutput").ap()

    gates_sp = nc.dram_tensor("gates_sp", [NT, C], F32, kind="Internal").ap()
    q1_sp = nc.dram_tensor("q1_sp", [NT, C], F32, kind="Internal").ap()
    part_sp = nc.dram_tensor("part_sp", [NT, C], F32, kind="Internal").ap()
    xnh_sp = nc.dram_tensor("xnh_sp", [NTIL, P, 8, P], BF16, kind="Internal").ap()
    xnl_sp = nc.dram_tensor("xnl_sp", [NTIL, P, 8, P], BF16, kind="Internal").ap()
    xqh_sp = nc.dram_tensor("xqh_sp", [NTIL, P, 8, P], BF16, kind="Internal").ap()
    xql_sp = nc.dram_tensor("xql_sp", [NTIL, P, 8, P], BF16, kind="Internal").ap()
    wgh_sp = nc.dram_tensor("wgh_sp", [P, 8, I], BF16, kind="Internal").ap()
    wgl_sp = nc.dram_tensor("wgl_sp", [P, 8, I], BF16, kind="Internal").ap()
    wuh_sp = nc.dram_tensor("wuh_sp", [P, 8, I], BF16, kind="Internal").ap()
    wul_sp = nc.dram_tensor("wul_sp", [P, 8, I], BF16, kind="Internal").ap()
    wdh_sp = nc.dram_tensor("wdh_sp", [P, 22, C], BF16, kind="Internal").ap()
    wdl_sp = nc.dram_tensor("wdl_sp", [P, 22, C], BF16, kind="Internal").ap()
    cc_in = nc.dram_tensor("cc_in", [65, C], F32, kind="Internal").ap()
    cc_out = nc.dram_tensor("cc_out", [65, C], F32, kind="Internal").ap()

    tap = {}
    if taps:
        for nm, shp, dt in [("t_xn", [640, C], F32), ("t_gates", [640, C], F32),
                            ("t_q1", [640, C], F32), ("t_k1", [640, C], F32),
                            ("t_kvpart", [65, C], F32), ("t_kvall", [65, C], F32),
                            ("t_part", [256, C], F32), ("t_h", [P, 1024], F32),
                            ("t_xqh", [640, C], BF16), ("t_xql", [640, C], BF16),
                            ("t_ksb", [P, C], F32), ("t_norm", [P, H], F32),
                            ("t_qn", [P, C], F32), ("t_w2h", [P, C], BF16),
                            ("t_w2l", [P, C], BF16), ("t_qnh", [P, C], BF16),
                            ("t_qnl", [P, C], BF16), ("t_bfh", [P, 8, P], BF16),
                            ("t_cgh", [P, 8, 1024], BF16), ("t_cgl", [P, 8, 1024], BF16),
                            ("t_cdh", [P, 8, C], BF16), ("t_cdl", [P, 8, C], BF16),
                            *[(f"t_dbg{k}", [P, 1], F32) for k in range(15)],
                            ("t_psph", [P, 1408], BF16), ("t_pspl", [P, 1408], BF16),
                            ("t_pyq", [P, 1408], F32), ("t_paa", [P, 1408], F32)]:
            tap[nm] = nc.dram_tensor(nm, shp, dt, kind="ExternalOutput").ap()

    R = Rec()
    sLd = R.sem("sLd")
    sTr = R.sem("sTr")
    sSt = R.sem("sSt")
    sS = R.sem("sS")
    sV = R.sem("sV")
    sT = R.sem("sT")
    sCC = R.sem("sCC")
    sP = R.sem("sP")

    def sb(name, shape, dt):
        g = nc.sbuf_tensor(name, shape, dt)
        t = g.__enter__()
        return t.ap(), g

    def scope_alloc(scope, name, shape, dt):
        ap, g = sb(name, shape, dt)
        scope.append(g)
        return ap

    def scope_exit(scope):
        for g in reversed(scope):
            g.__exit__(None, None, None)
        scope.clear()

    csts, _ = sb("csts", [P, 8], F32)
    f2 = [sb(f"f2_{j}", [P, C], F32)[0] for j in range(2)]
    g2 = [sb(f"g2_{j}", [P, C], F32)[0] for j in range(2)]
    acc1 = [sb(f"acc1_{j}", [P, 1], F32)[0] for j in range(2)]
    acc2 = [sb(f"acc2_{j}", [P, 1], F32)[0] for j in range(2)]
    tmean = [sb(f"tmean_{j}", [P, 1], F32)[0] for j in range(2)]
    tvar = [sb(f"tvar_{j}", [P, 1], F32)[0] for j in range(2)]
    trstd = [sb(f"trstd_{j}", [P, 1], F32)[0] for j in range(2)]
    tnmr = [sb(f"tnmr_{j}", [P, 1], F32)[0] for j in range(2)]
    ln_a = [sb(f"ln_a{j}", [P, 1], F32)[0] for j in range(2)]
    ln_b = [sb(f"ln_b{j}", [P, 1], F32)[0] for j in range(2)]
    red2 = [sb(f"red2_{j}", [P, 88], F32)[0] for j in range(2)]
    sq2 = [sb(f"sq2_{j}", [P, 88], F32)[0] for j in range(2)]
    rq2 = [sb(f"rq2_{j}", [P, 88], F32)[0] for j in range(2)]
    kvacc, _ = sb("kvacc", [65, C], F32)
    jnkt, _ = sb("jnkt", [P, 8], F32)
    jnkS, _ = sb("jnkS", [P, 8], F32)
    R.jnk = jnkt[:, :]
    R.jnkS = jnkS[:, :]

    eps_c = csts[:, 0:1]
    thr_c = [csts[:, 1 + j:2 + j] for j in range(7)]

    ps_kv = nc.psum_tensor("ps_kv", [P, C], F32).__enter__().ap()
    pp = [nc.psum_tensor(f"pp{j}", [P, 512], F32).__enter__().ap() for j in range(2)]
    psW = nc.psum_tensor("psW", [P, C], F32).__enter__().ap()
    psD = [nc.psum_tensor(f"psD{j}", [P, 512], F32).__enter__().ap() for j in range(2)]

    def dma(eng, out, in_, sem, waits=(), transpose=False):
        if transpose:
            R.op(eng, lambda e, o=out, i=in_: e.dma_start(out=o, in_=i, transpose=True),
                 waits=waits, incs=[(sem, 16)])
        else:
            R.op(eng, lambda e, o=out, i=in_: e.dma_start(out=o, in_=i),
                 waits=waits, incs=[(sem, 16)])

    def mm_group(bank, bankkey, triples, waits_first):
        w0 = list(waits_first)
        prev = R.bankprev.get(bankkey)
        if prev is not None:
            w0.append(prev)
        n = len(triples)
        for t, (o, l, r) in enumerate(triples):
            R.op("tensor",
                 lambda e, o=o, l=l, r=r, s=(t == 0), p=(t == n - 1):
                 e.matmul(o, l, r, start=s, stop=p),
                 waits=w0 if t == 0 else [],
                 incs=[(sT, 1)] if t == n - 1 else [])

    def bank_release(bankkey, sem, val):
        R.bankprev[bankkey] = (sem, val)

    def proj3(bank, bankkey, lh, ll, wh, wl, ncols, csl, waits_first):
        triples = []
        for cch in range(8):
            triples.append((bank[:, 0:ncols], lh[:, cch, :], wh[:, cch, csl]))
            triples.append((bank[:, 0:ncols], lh[:, cch, :], wl[:, cch, csl]))
            triples.append((bank[:, 0:ncols], ll[:, cch, :], wh[:, cch, csl]))
        mm_group(bank, bankkey, triples, waits_first)
        return sT.n

    qprev = {}

    def quant_stage1(pool, key, src, width, idx, waits_v):
        nb = width // 16
        red, sq, rq = red2[idx][:, 0:nb], sq2[idx][:, 0:nb], rq2[idx][:, 0:nb]
        yq = pool["yq"][idx][:, 0:width]
        prev = [qprev[key]] if key in qprev else []
        R.op("vector", lambda e, r=red, s=src: e.tensor_reduce(
            r, s.rearrange("p (b k) -> p b k", k=16), AX.X, ALU.max,
            apply_absolute_value=True), waits=list(waits_v) + prev, pad=True)
        R.op("vector", lambda e, o=sq, i=red: e.tensor_scalar(
            o, i, 1.0 / 6.0, 1e-12, ALU.mult, ALU.max), incs=[(sV, 1)])
        v_sq = sV.n
        R.op("vector", lambda e, o=rq, i=sq: e.reciprocal(o, i),
             waits=[(sV, v_sq)], incs=[(sV, 1)])
        v_rq = sV.n
        R.op("vector", lambda e, o=yq, s=src, r=rq, k=nb: e.tensor_tensor(
            o.rearrange("p (b k) -> p b k", k=16),
            s.rearrange("p (b k) -> p b k", k=16),
            r.broadcast_to((P, k, 16)), ALU.mult), waits=[(sV, v_rq)],
            incs=[(sV, 1)])
        return {"v_yq": sV.n, "v_sq": v_sq, "sq": sq, "yq": yq,
                "idx": idx, "width": width, "nb": nb}

    def quant_stage2(pool, key, st, dst_h, dst_l, waits2=(), use_pool=True):
        width, nb, idx = st["width"], st["nb"], st["idx"]
        yq, sq = st["yq"], st["sq"]
        aa = pool["aa"][idx][:, 0:width]
        sb7 = [b[:, 0:width] for b in pool["sb"]]
        ub = [b[:, 0:width] for b in pool["ub"]]
        sgb = pool["sg"][idx][:, 0:width]
        prev = [qprev[key]] if key in qprev else []
        R.op("scalar", lambda e, o=sgb, i=yq: e.activation(o, i, AF.Sign),
             waits=[(sV, st["v_yq"])] + prev)
        R.op("scalar", lambda e, o=aa, i=yq: e.activation(o, i, AF.Abs))
        for t in range(4):
            R.op("scalar", lambda e, o=sb7[t], i=aa, b=thr_c[t]: e.activation(
                o, i, AF.Sign, bias=b), incs=[(sS, 1)] if t == 3 else [])
        s_ab4 = sS.n
        for t in range(4, 7):
            R.op("scalar", lambda e, o=sb7[t], i=aa, b=thr_c[t]: e.activation(
                o, i, AF.Sign, bias=b), incs=[(sS, 1)] if t == 6 else [])
        s_all = sS.n
        if use_pool:
            R.op("gpsimd", lambda e, o=ub[0], a=sb7[0], b=sb7[1]: e.tensor_tensor(
                o, a, b, ALU.add), waits=[(sS, s_ab4)] + list(waits2))
            R.op("gpsimd", lambda e, o=ub[1], a=sb7[2], b=sb7[3]: e.tensor_tensor(
                o, a, b, ALU.add))
            R.op("gpsimd", lambda e, o=ub[0], a=ub[0], b=ub[1]: e.tensor_tensor(
                o, a, b, ALU.add))
            R.op("gpsimd", lambda e, o=ub[1], a=sb7[4], b=sb7[5]: e.tensor_tensor(
                o, a, b, ALU.add), waits=[(sS, s_all)], incs=[(sP, 1)],
                fence=True)
            p_tree = sP.n
            if use_pool == "adds":
                R.op("vector", lambda e, o=ub[1], a=sb7[6], b=ub[1]:
                     e.scalar_tensor_tensor(o, a, 2.0, b, ALU.mult, ALU.add),
                     waits=[(sS, s_all), (sP, p_tree)], prepad=True)
                R.op("vector", lambda e, o=ub[0], a=ub[1], b=ub[0]:
                     e.scalar_tensor_tensor(o, a, 2.0, b, ALU.mult, ALU.add))
                R.op("vector", lambda e, o=ub[1], i=ub[0]: e.tensor_scalar(
                    o, i, 0.25, 3.0, ALU.mult, ALU.add))
                R.op("vector", lambda e, o=ub[0], a=ub[1], b=sgb: e.tensor_tensor(
                    o, a, b, ALU.mult))
                R.op("vector", lambda e, o=yq, a=ub[0], s=sq, k=nb: e.tensor_tensor(
                    o.rearrange("p (b k) -> p b k", k=16),
                    a.rearrange("p (b k) -> p b k", k=16),
                    s.broadcast_to((P, k, 16)), ALU.mult),
                    waits=[(sV, st["v_sq"])])
                R.op("vector", lambda e, o=dst_h, i=yq: e.tensor_copy(o, i))
                R.op("vector", lambda e, o=dst_l, a=yq, b=dst_h: e.tensor_tensor(
                    o, a, b, ALU.subtract), incs=[(sV, 1)], fence=True)
                qprev[key] = (sV, sV.n)
                return qprev[key]
            R.op("vector", lambda e, o=ub[1], a=sb7[6], b=ub[1]:
                 e.scalar_tensor_tensor(o, a, 2.0, b, ALU.mult, ALU.add),
                 waits=[(sS, s_all), (sP, p_tree)], prepad=True)
            R.op("vector", lambda e, o=ub[0], a=ub[1], b=ub[0]:
                 e.scalar_tensor_tensor(o, a, 2.0, b, ALU.mult, ALU.add))
            R.op("vector", lambda e, o=ub[1], i=ub[0]: e.tensor_scalar(
                o, i, 0.25, 3.0, ALU.mult, ALU.add))
            R.op("vector", lambda e, o=ub[0], a=ub[1], b=sgb: e.tensor_tensor(
                o, a, b, ALU.mult), incs=[(sV, 1)], fence=True)
            v_g = sV.n
            R.op("gpsimd", lambda e, o=yq, a=ub[0], s=sq, k=nb: e.tensor_tensor(
                o.rearrange("p (b k) -> p b k", k=16),
                a.rearrange("p (b k) -> p b k", k=16),
                s.broadcast_to((P, k, 16)), ALU.mult),
                waits=[(sV, v_g), (sV, st["v_sq"])])
            R.op("gpsimd", lambda e, o=dst_h, i=yq: e.tensor_copy(o, i))
            R.op("gpsimd", lambda e, o=dst_l, a=yq, b=dst_h: e.tensor_tensor(
                o, a, b, ALU.subtract), incs=[(sP, 1)], fence=True)
            qprev[key] = (sP, sP.n)
            return qprev[key]
        R.op("vector", lambda e, o=ub[0], a=sb7[0], b=sb7[1]: e.tensor_tensor(
            o, a, b, ALU.add), waits=[(sS, s_ab4)] + list(waits2))
        R.op("vector", lambda e, o=ub[1], a=sb7[2], b=sb7[3]: e.tensor_tensor(
            o, a, b, ALU.add))
        R.op("vector", lambda e, o=ub[0], a=ub[0], b=ub[1]: e.tensor_tensor(
            o, a, b, ALU.add))
        R.op("vector", lambda e, o=ub[1], a=sb7[4], b=sb7[5]: e.tensor_tensor(
            o, a, b, ALU.add), waits=[(sS, s_all)])
        R.op("vector", lambda e, o=ub[1], a=sb7[6], b=ub[1]:
             e.scalar_tensor_tensor(o, a, 2.0, b, ALU.mult, ALU.add))
        R.op("vector", lambda e, o=ub[0], a=ub[1], b=ub[0]:
             e.scalar_tensor_tensor(o, a, 2.0, b, ALU.mult, ALU.add))
        R.op("vector", lambda e, o=ub[1], i=ub[0]: e.tensor_scalar(
            o, i, 0.25, 3.0, ALU.mult, ALU.add))
        R.op("vector", lambda e, o=ub[0], a=ub[1], b=sgb: e.tensor_tensor(
            o, a, b, ALU.mult))
        R.op("vector", lambda e, o=yq, a=ub[0], s=sq, k=nb: e.tensor_tensor(
            o.rearrange("p (b k) -> p b k", k=16),
            a.rearrange("p (b k) -> p b k", k=16),
            s.broadcast_to((P, k, 16)), ALU.mult), waits=[(sV, st["v_sq"])])
        R.op("vector", lambda e, o=dst_h, i=yq: e.tensor_copy(o, i))
        R.op("vector", lambda e, o=dst_l, a=yq, b=dst_h: e.tensor_tensor(
            o, a, b, ALU.subtract), incs=[(sV, 1)], fence=True)
        qprev[key] = (sV, sV.n)
        return qprev[key]

    def quant_chain(pool, key, src, width, idx, dst_h, dst_l, waits_v,
                    mode=False):
        st = quant_stage1(pool, key, src, width, idx, waits_v)
        return quant_stage2(pool, key, st, dst_h, dst_l, use_pool=mode)

    scPREP = []
    PW = 1024
    pstg = [scope_alloc(scPREP, f"pstg{j}", [P, PW], F32) for j in range(3)]
    psp_h = scope_alloc(scPREP, "psp_h", [P, PW], BF16)
    psp_l = scope_alloc(scPREP, "psp_l", [P, PW], BF16)
    pwtr = [scope_alloc(scPREP, "pwtr0", [P, 8, P], BF16)] * 2
    qpP = {
        "yq": [scope_alloc(scPREP, f"yqP{j}", [P, PW], F32) for j in range(2)],
    }
    qpP["aa"] = qpP["yq"]
    qpP.update({
        "sb": [scope_alloc(scPREP, f"sbP{j}", [P, PW], BF16) for j in range(7)],
        "ub": [scope_alloc(scPREP, f"ubP{j}", [P, PW], BF16) for j in range(2)],
        "sg": [scope_alloc(scPREP, "sgP0", [P, PW], BF16)] * 2,
    })

    scA1 = []
    x2 = [scope_alloc(scA1, f"x2a{j}", [P, C], F32) for j in range(3)]
    xn3 = [scope_alloc(scA1, f"xn3_{j}", [P, C], F32) for j in range(3)]
    scrA1 = scope_alloc(scA1, "scrA1", [P, C], F32)
    wgh = scope_alloc(scA1, "wgh", [P, 8, C], BF16)
    wgl = scope_alloc(scA1, "wgl", [P, 8, C], BF16)
    BG = scope_alloc(scA1, "BG", [P, C], F32)
    qpA = {
        "yq": [b[:, 0:C] for b in qpP["yq"]],
        "aa": [b[:, 0:C] for b in qpP["aa"]],
        "sb": qpP["sb"],
        "ub": qpP["ub"],
        "sg": [scope_alloc(scA1, f"sgA{j}", [P, C], BF16) for j in range(2)],
    }
    xnh_b = [scope_alloc(scA1, f"xnhb{j}", [P, C], BF16) for j in range(2)]
    xnl_b = [scope_alloc(scA1, f"xnlb{j}", [P, C], BF16) for j in range(2)]
    xqh_b = [scope_alloc(scA1, f"xqhb{j}", [P, C], BF16) for j in range(2)]
    xql_b = [scope_alloc(scA1, f"xqlb{j}", [P, C], BF16) for j in range(2)]
    fmh = [scope_alloc(scA1, f"fmh{j}", [P, 8, P], BF16) for j in range(2)]
    fml = [scope_alloc(scA1, f"fml{j}", [P, 8, P], BF16) for j in range(2)]
    fqh = [scope_alloc(scA1, f"fqh{j}", [P, 8, P], BF16) for j in range(2)]
    fql = [scope_alloc(scA1, f"fql{j}", [P, 8, P], BF16) for j in range(2)]

    dma("sync", csts[:, :], csts_d[:, :], sLd)
    dma("sync", BG[:, :], bgt_d[:, :], sLd)
    for cch in range(8):
        w_stg = [R.at(f"wgsp_{cch - 1}")] if cch >= 1 else []
        dma("sync", pstg[cch % 2][:, 0:C], wg_d[cch * P:(cch + 1) * P, :], sLd,
            waits=w_stg)
        ldv = sLd.n
        R.op("vector", lambda e, o=wgh[:, cch, :], i=pstg[cch % 2][:, 0:C]:
             e.tensor_copy(o, i), waits=[(sLd, ldv)])
        R.op("vector", lambda e, o=wgl[:, cch, :], a=pstg[cch % 2][:, 0:C],
             b=wgh[:, cch, :]: e.tensor_tensor(o, a, b, ALU.subtract),
             incs=[(sV, 1)], fence=True)
        R.mark(f"wgsp_{cch}", sV)
    R.mark("wg_ready", sV)

    R.op("vector", lambda e, o=kvacc[:, :]: e.memset(o, 0.0), incs=[(sV, 1)])

    s_er_tab = {}
    a1banks = [pp[0], pp[1], psW[:, 0:512], psW[:, 512:1024]]

    ld_tab = {}

    def a1_xload(L):
        w_x = [R.at(f"a1_xfree_{L - 3}")] if L >= 3 else []
        dma("sync", x2[L % 3][:, :], x_d[L * P:(L + 1) * P, :], sLd, waits=w_x)
        ld_tab[L] = sLd.n

    def a1_tail(j, st):
        idx, xnt, s_xn, t_gs, gbt = st
        for nchv in range(2):
            bk = (2 * j + nchv) % 4
            w_gb = [(sT, t_gs[nchv])]
            if nchv == 0 and j >= 2:
                w_gb.append(R.at(f"a1_gatesfree_{j - 2}"))
            if nchv == 0:
                w_gb += R.wtap(f"tf_gates_{j - 2}")
            R.op("vector", lambda e, o=gbt[:, nchv * 512:(nchv + 1) * 512],
                 a=a1banks[bk][:, :], b=BG[:, nchv * 512:(nchv + 1) * 512]:
                 e.scalar_tensor_tensor(o, a, 1.0, b, ALU.mult, ALU.add),
                 waits=w_gb, incs=[(sV, 1)], prepad=True)
            bank_release(f"g{bk}", sV, sV.n)
        v_gb = sV.n
        R.op("scalar", lambda e, o=gbt, i=gbt: e.activation(o[:, :], i[:, :],
             AF.Sigmoid), waits=[(sV, v_gb)], incs=[(sS, 1)])
        s_gates = sS.n
        dma("gpsimd", gates_sp[j * P:(j + 1) * P, :], gbt[:, :], sSt,
            waits=[(sS, s_gates)])
        R.mark(f"a1_gatesfree_{j}", sSt)
        if taps and j < 5:
            dma("gpsimd", tap["t_xn"][j * P:(j + 1) * P, :], xnt[:, :], sSt,
                waits=[(sS, s_xn)])
            R.mark(f"tf_xn_{j}", sSt)
            dma("gpsimd", tap["t_gates"][j * P:(j + 1) * P, :], gbt[:, :], sSt,
                waits=[(sS, s_gates)])
            R.mark(f"tf_gates_{j}", sSt)

        w_qv = [(sS, s_xn)]
        if j >= 2:
            w_qv.append(R.at(f"a1_xqfree_{j - 2}"))
        w_qv += R.wtap(f"tf_xq_{j - 2}")
        rq_t = quant_chain(qpA, "A", xnt[:, :], C, idx, xqh_b[idx][:, :],
                           xql_b[idx][:, :], w_qv, mode="adds")
        R.mark(f"a1_xnfree_{j}", sV)
        w_fq = []
        if j >= 2:
            w_fq = [R.at(f"a1_fqfree_{j - 2}")]
        dma("sync", fqh[idx][:, :, :], xqh_b[idx][:, :], sTr,
            waits=[rq_t] + w_fq, transpose=True)
        dma("sync", fql[idx][:, :, :], xql_b[idx][:, :], sTr, transpose=True)
        tr_fq = sTr.n
        R.ck[f"a1_xqfree_{j}"] = (sTr, tr_fq)
        dma("gpsimd", xqh_sp[j, :, :, :], fqh[idx][:, :, :], sSt,
            waits=[(sTr, tr_fq)])
        dma("gpsimd", xql_sp[j, :, :, :], fql[idx][:, :, :], sSt)
        R.mark(f"a1_fqfree_{j}", sSt)
        if taps and j < 5:
            dma("gpsimd", tap["t_xqh"][j * P:(j + 1) * P, :], xqh_b[idx][:, :],
                sSt, waits=[rq_t])
            dma("gpsimd", tap["t_xql"][j * P:(j + 1) * P, :], xql_b[idx][:, :],
                sSt)
            R.mark(f"tf_xq_{j}", sSt)

    ln_nmr = {}

    def ln_front(j):
        idx = j % 2
        xt = x2[j % 3]
        w_st = [(sLd, ld_tab.pop(j))]
        if j >= 2:
            w_st.append(R.at(f"ln_free_{j - 2}"))
        R.op("scalar", lambda e, o=scrA1[:, :], i=xt, a=acc1[idx]: e.activation(
            o, i[:, :], AF.Copy, accum_out=a[:, :]), waits=w_st)
        R.op("scalar", lambda e, o=scrA1[:, :], i=xt, a=acc2[idx]: e.activation(
            o, i[:, :], AF.Square, accum_out=a[:, :]), incs=[(sS, 1)],
            fence=True)
        s_stats = sS.n
        R.op("vector", lambda e, o=tmean[idx], i=acc1[idx]: e.tensor_scalar(
            o[:, :], i[:, :], 1.0 / C, None, ALU.mult), waits=[(sS, s_stats)],
            incs=[(sV, 1)], pad=True, prepad=True)
        R.op("vector", lambda e, o=ln_b[idx], a=tmean[idx], b=tmean[idx]:
             e.tensor_tensor(o[:, :], a[:, :], b[:, :], ALU.mult),
             waits=[(sV, sV.n)], incs=[(sV, 1)], pad=True)
        R.op("vector", lambda e, o=ln_a[idx], i=acc2[idx]: e.tensor_scalar(
            o[:, :], i[:, :], 1.0 / C, None, ALU.mult), waits=[(sV, sV.n)],
            incs=[(sV, 1)], pad=True)
        R.op("vector", lambda e, o=tvar[idx], a=ln_a[idx], b=ln_b[idx]:
             e.tensor_tensor(o[:, :], a[:, :], b[:, :], ALU.subtract),
             waits=[(sV, sV.n)], incs=[(sV, 1)], pad=True)
        R.op("vector", lambda e, o=tvar[idx], i=tvar[idx]: e.tensor_scalar(
            o[:, :], i[:, :], 1e-5, None, ALU.add), waits=[(sV, sV.n)],
            incs=[(sV, 1)], pad=True)
        v_var = sV.n
        if taps and j < 5:
            dma("gpsimd", tap[f"t_dbg{j}"][:, :], acc2[idx][:, :], sSt,
                waits=[(sS, s_stats)])
            dma("gpsimd", tap[f"t_dbg{5 + j}"][:, :], tvar[idx][:, :], sSt,
                waits=[(sV, v_var)])
        R.op("scalar", lambda e, o=ln_a[idx], i=tvar[idx]: e.activation(
            o[:, :], i[:, :], AF.Sqrt), waits=[(sV, v_var)], incs=[(sS, 1)])
        s_rstd = sS.n
        R.op("vector", lambda e, o=ln_b[idx], i=ln_a[idx]: e.reciprocal(
            o[:, :], i[:, :]), waits=[(sS, s_rstd)], incs=[(sV, 1)])
        R.op("vector", lambda e, o=ln_b[idx], a=tvar[idx], b=ln_b[idx]:
             e.tensor_tensor(o[:, :], a[:, :], b[:, :], ALU.mult),
             waits=[(sV, sV.n)], incs=[(sV, 1)], pad=True)
        R.op("vector", lambda e, o=ln_b[idx], a=ln_a[idx], b=ln_b[idx]:
             e.tensor_tensor(o[:, :], a[:, :], b[:, :], ALU.add),
             waits=[(sV, sV.n)], incs=[(sV, 1)], pad=True)
        R.op("vector", lambda e, o=ln_b[idx], i=ln_b[idx]: e.tensor_scalar(
            o[:, :], i[:, :], 0.5, None, ALU.mult), waits=[(sV, sV.n)],
            incs=[(sV, 1)], pad=True)
        R.op("vector", lambda e, o=trstd[idx], i=ln_b[idx]: e.reciprocal(
            o[:, :], i[:, :]), waits=[(sV, sV.n)], incs=[(sV, 1)])
        R.op("vector", lambda e, o=tnmr[idx], a=tmean[idx], b=trstd[idx]:
             e.scalar_tensor_tensor(o[:, :], a[:, :], -1.0, b[:, :],
                                    ALU.mult, ALU.mult), waits=[(sV, sV.n)],
             incs=[(sV, 1)], pad=True)
        ln_nmr[j] = sV.n
        if taps and j < 5:
            dma("gpsimd", tap[f"t_dbg{10 + j}"][:, :], trstd[idx][:, :], sSt,
                waits=[(sV, ln_nmr[j])])
        R.mark(f"ln_free_{j}", sV)

    a1_xload(0)
    a1_xload(1)
    ln_front(0)
    a1_prev = None
    for i in range(NTIL):
        idx = i % 2
        xt = x2[i % 3]
        xnt = xn3[i % 3]
        gbt = g2[idx]

        if i + 2 < NTIL:
            a1_xload(i + 2)

        w_xn = [(sV, ln_nmr.pop(i))]
        if i >= 3:
            w_xn.append(R.at(f"a1_xnfree_{i - 3}"))
        w_xn += R.wtap(f"tf_xn_{i - 3}")
        R.op("scalar", lambda e, o=xnt, i=xt, sc=trstd[idx], b=tnmr[idx]:
             e.activation(o[:, :], i[:, :], AF.Identity, bias=b[:, :], scale=sc[:, :]),
             waits=w_xn, incs=[(sS, 1)])
        s_xn = sS.n
        R.mark(f"a1_xfree_{i}", sS)

        w_sp = [R.at(f"a1_xnhfree_{i - 2}")] if i >= 2 else []
        R.op("vector", lambda e, o=xnh_b[idx], i=xnt: e.tensor_copy(o[:, :], i[:, :]),
             waits=[(sS, s_xn)] + w_sp)
        R.op("vector", lambda e, o=xnl_b[idx], a=xnt, b=xnh_b[idx]: e.tensor_tensor(
            o[:, :], a[:, :], b[:, :], ALU.subtract), incs=[(sV, 1)], fence=True)
        v_split = sV.n
        w_fm = []
        if i >= 2:
            w_fm = [R.at(f"a1_fmfreeT_{i - 2}"), R.at(f"a1_fmfreeS_{i - 2}")]
        dma("sync", fmh[idx][:, :, :], xnh_b[idx][:, :], sTr,
            waits=[(sV, v_split)] + w_fm, transpose=True)
        dma("sync", fml[idx][:, :, :], xnl_b[idx][:, :], sTr, transpose=True)
        tr_fm = sTr.n
        R.ck[f"a1_xnhfree_{i}"] = (sTr, tr_fm)
        dma("gpsimd", xnh_sp[i, :, :, :], fmh[idx][:, :, :], sSt, waits=[(sTr, tr_fm)])
        dma("gpsimd", xnl_sp[i, :, :, :], fml[idx][:, :, :], sSt)
        R.mark(f"a1_fmfreeS_{i}", sSt)

        t_gs = []
        for nchv in range(2):
            bk = (2 * i + nchv) % 4
            t_g = proj3(a1banks[bk], f"g{bk}", fmh[idx], fml[idx], wgh, wgl, 512,
                        np.s_[nchv * 512:(nchv + 1) * 512],
                        [R.at("wg_ready"), (sTr, tr_fm)])
            t_gs.append(t_g)
        R.mark(f"a1_fmfreeT_{i}", sT)

        if i + 1 < NTIL:
            ln_front(i + 1)

        st_here = (idx, xnt, s_xn, t_gs, gbt)
        if a1_prev is not None:
            a1_tail(i - 1, a1_prev)
        a1_prev = st_here

    a1_tail(NTIL - 1, a1_prev)
    R.bankprev[0] = R.bankprev["g0"]
    R.bankprev[1] = R.bankprev["g1"]

    R.mark("a1_T_done", sT)
    R.mark("a1_V_done", sV)
    R.mark("a1_S_done", sS)
    a1_spills = sSt.n

    prep_descs = []
    for w_d_ap, nchunks, kwidth, sp_h, sp_l in (
            (wgate_d, 22, C, wgh_sp, wgl_sp),
            (wup_d, 22, C, wuh_sp, wul_sp),
            (wdown_d, 8, I, wdh_sp, wdl_sp)):
        for j in range(nchunks):
            for u in range((kwidth + PW - 1) // PW):
                k0 = u * PW
                k1 = min(kwidth, k0 + PW)
                prep_descs.append((w_d_ap, j, k0, k1, k0 // P, sp_h, sp_l))

    prep_state = {"clock": 0, "ld": {}, "st": {}}

    def prep_load(k):
        w_d_ap, j, k0, k1, kofs, sp_h, sp_l = prep_descs[k]
        pidx = k % 3
        wseg = k1 - k0
        w_stage = [R.at(f"prep_stg_free_{pidx}")] if f"prep_stg_free_{pidx}" in R.ck else []
        dma("sync", pstg[pidx][:, 0:wseg], w_d_ap[j * P:(j + 1) * P, k0:k1], sLd,
            waits=w_stage)
        prep_state["ld"][k] = sLd.n

    def prep_stage1(k):
        w_d_ap, j, k0, k1, kofs, sp_h, sp_l = prep_descs[k]
        pidx = k % 3
        wseg = k1 - k0
        st = quant_stage1(qpP, "P", pstg[pidx][:, 0:wseg], wseg, k % 2,
                          [(sLd, prep_state["ld"].pop(k))])
        R.ck[f"prep_stg_free_{pidx}"] = (sV, sV.n)
        prep_state["st"][k] = st

    def prep_stage2(k):
        w_d_ap, j, k0, k1, kofs, sp_h, sp_l = prep_descs[k]
        st = prep_state["st"].pop(k)
        wseg = st["width"]
        ksub = wseg // P
        w2 = [R.at("prep_sp_free")] if "prep_sp_free" in R.ck else []
        rq_t = quant_stage2(qpP, "P", st, psp_h[:, 0:wseg], psp_l[:, 0:wseg],
                            waits2=w2, use_pool=False)
        w_tr = [R.at("prep_wtr_free_1")] if "prep_wtr_free_1" in R.ck else []
        dma("sync", pwtr[0][:, 0:ksub, :], psp_h[:, 0:wseg], sTr,
            waits=[rq_t] + w_tr, transpose=True)
        tr_h = sTr.n
        dma("gpsimd", sp_h[:, kofs:kofs + ksub, j * P:(j + 1) * P],
            pwtr[0][:, 0:ksub, :], sSt, waits=[(sTr, tr_h)])
        R.mark("prep_wtr_free_0", sSt)
        dma("sync", pwtr[1][:, 0:ksub, :], psp_l[:, 0:wseg], sTr,
            waits=[R.at("prep_wtr_free_0")], transpose=True)
        tr_l = sTr.n
        R.ck["prep_sp_free"] = (sTr, tr_l)
        dma("gpsimd", sp_l[:, kofs:kofs + ksub, j * P:(j + 1) * P],
            pwtr[1][:, 0:ksub, :], sSt, waits=[(sTr, tr_l)])
        R.mark("prep_wtr_free_1", sSt)

    NPREP = len(prep_descs)
    prep_pt = {"n": 0}

    def prep_pump():
        prep_pt["n"] += 1
        want = (prep_pt["n"] * (NPREP + 2)) // (4 * NTIL)
        while prep_state["clock"] < want:
            prep_advance()

    def prep_advance():
        k = prep_state["clock"]
        if k < NPREP:
            prep_load(k)
        if 1 <= k < NPREP + 1:
            prep_stage1(k - 1)
        if 2 <= k < NPREP + 2:
            prep_stage2(k - 2)
        prep_state["clock"] += 1

    scope_exit(scA1)

    scA2 = []
    wqh = scope_alloc(scA2, "wqh", [P, 8, 3 * C], BF16)
    wql = scope_alloc(scA2, "wql", [P, 8, 3 * C], BF16)
    qstg = [scope_alloc(scA2, f"qstg{j}", [P, 512], F32) for j in range(2)]
    lfh = [scope_alloc(scA2, f"lfh{j}", [P, 8, P], BF16) for j in range(2)]
    lfl = [scope_alloc(scA2, f"lfl{j}", [P, 8, P], BF16) for j in range(2)]
    zb = [scope_alloc(scA2, "zb0", [P, 512], F32)] * 2
    ebz = [scope_alloc(scA2, "ebz0", [P, 512], F32)] * 2
    k1b = [scope_alloc(scA2, f"k1b{j}", [P, C], F32) for j in range(2)]
    vxb = [scope_alloc(scA2, "vxb0", [P, H, 65], F32)] * 2

    a1_gate = [R.at("a1_T_done"), R.at("a1_V_done"), R.at("a1_S_done"),
               (sSt, a1_spills), (sTr, sTr.n)]

    def wq_piece(k8):
        hf, cch = divmod(k8, 8)
        cs = np.s_[hf * 512:(hf + 1) * 512]
        qsg = qstg[k8 % 2]
        w_stg = [R.at(f"wqsp_{k8 - 2}")] if k8 >= 2 else list(a1_gate)
        dma("sync", qsg[:, :], wqkv_d[cch * P:(cch + 1) * P, cs], sLd,
            waits=w_stg)
        ldv = sLd.n
        R.op("vector", lambda e, o=wqh[:, cch, cs], i=qsg[:, :]:
             e.tensor_copy(o, i), waits=[(sLd, ldv)])
        R.op("vector", lambda e, o=wql[:, cch, cs], a=qsg[:, :],
             b=wqh[:, cch, cs]: e.tensor_tensor(o, a, b, ALU.subtract),
             incs=[(sV, 1)], fence=True)
        R.mark(f"wqsp_{k8}", sV)
        if cch == 7:
            R.mark(f"wq_col_{hf}", sV)
        if k8 == 47:
            R.mark("wq_ready", sV)
            R.op("vector", lambda e, o=vxb[0][:, :, 64:65]: e.memset(o, 1.0),
                 incs=[(sV, 1)], fence=True)
            R.mark("vones", sV)

    a2_ldt = {}

    def a2_loads(i):
        idx = i % 2
        w_g = [R.at(f"a1_gatesfree_{i}")]
        if i >= 2:
            w_g.append(R.at(f"a2_gatesbuf_free_{i - 2}"))
        else:
            w_g.append(R.at(f"a1_gatesfree_{NTIL - 2 + i}"))
        dma("sync", g2[idx][:, :], gates_sp[i * P:(i + 1) * P, :], sLd, waits=w_g)
        w_l = [R.at(f"a2_lffree_{i - 2}")] if i >= 2 else \
            list(a1_gate) + [R.at(f"a1_fqfree_{i}")]
        dma("sync", lfh[idx][:, :, :], xnh_sp[i, :, :, :], sLd, waits=w_l)
        dma("sync", lfl[idx][:, :, :], xnl_sp[i, :, :, :], sLd)
        a2_ldt[i] = sLd.n

    a2_loads(0)
    for k8 in range(16):
        wq_piece(k8)
    for i in range(NTIL):
        idx = i % 2
        gt = g2[idx]
        q1t = f2[idx]
        ld_i = a2_ldt.pop(i)
        if i + 1 < NTIL:
            a2_loads(i + 1)

        for nch in range(6):
            if i == 0 and nch >= 1:
                for k8 in range(8 + 8 * nch, min(48, 16 + 8 * nch)):
                    wq_piece(k8)
            bank = pp[nch % 2]
            wq_gate = R.at(f"wq_col_{nch}") if i == 0 else R.at("wq_ready")
            t_mm = proj3(bank, nch % 2, lfh[idx], lfl[idx], wqh, wql, 512,
                         np.s_[nch * 512:(nch + 1) * 512],
                         [wq_gate, (sLd, ld_i)])
            if nch < 4:
                dsts = q1t if nch < 2 else k1b[idx]
                dcol = (nch % 2) * 512
                w_z = [(sT, t_mm)]
                prev_er = s_er_tab.get((i, nch - 1), s_er_tab.get((i - 1, 3)))
                if prev_er is not None and nch <= 3:
                    w_z.append(prev_er)
                if nch < 2 and i >= 2:
                    w_z.append(R.at(f"a2_q1buf_free_{i - 2}"))
                if nch >= 2 and i >= 2:
                    w_z.append(R.at(f"a2_einsum_{i - 2}"))
                    w_z += R.wtap(f"tf_k1_{i - 2}")
                R.op("vector", lambda e, o=zb[nch % 2][:, :], a=bank[:, :],
                     b=gt[:, dcol:dcol + 512]:
                     e.tensor_tensor(o, a, b, ALU.mult), waits=w_z, incs=[(sV, 1)],
                     prepad=True)
                bank_release(nch % 2, sV, sV.n)
                v_z = sV.n
                R.op("vector", lambda e, o=ebz[nch % 2][:, :], i=zb[nch % 2][:, :]:
                     e.tensor_scalar(o, i, 0.0, None, ALU.min), incs=[(sV, 1)])
                v_zm = sV.n
                R.op("scalar", lambda e, o=ebz[nch % 2][:, :], i=ebz[nch % 2][:, :]:
                     e.activation(o, i, AF.Exp), waits=[(sV, v_zm)])
                R.op("scalar", lambda e, o=zb[nch % 2][:, :], i=zb[nch % 2][:, :]:
                     e.activation(o, i, AF.Relu), incs=[(sS, 1)])
                s_er = sS.n
                s_er_tab[(i, nch)] = (sS, s_er)
                R.op("vector", lambda e, o=dsts[:, dcol:dcol + 512],
                     a=ebz[nch % 2][:, :], b=zb[nch % 2][:, :]:
                     e.tensor_tensor(o, a, b, ALU.add), waits=[(sS, s_er)],
                     incs=[(sV, 1)], fence=(nch >= 2))
                if nch == 1:
                    R.mark(f"a2_q1done_{i}", sV)
                if nch == 3:
                    R.mark(f"a2_k1done_{i}", sV)
                    R.mark(f"a2_gatesbuf_free_{i}", sV)
                if nch in (1, 3):
                    prep_pump()
            else:
                h0 = (nch - 4) * 8
                w_v = [(sT, t_mm)]
                if nch == 4 and i >= 1:
                    w_v.append(R.at(f"a2_vfree_{i - 1}"))
                R.op("scalar", lambda e, o=vxb[idx][:, h0:h0 + 8, 0:64], a=bank[:, :]:
                     e.activation(o, a.rearrange("p (h d) -> p h d", d=64), AF.Copy),
                     waits=w_v, incs=[(sS, 1)], prepad=True, fence=(nch == 5))
                bank_release(nch % 2, sS, sS.n)
                if nch == 5:
                    R.mark(f"a2_vdone_{i}", sS)

        prep_pump()

        dma("gpsimd", q1_sp[i * P:(i + 1) * P, :], q1t[:, :], sSt,
            waits=[R.at(f"a2_q1done_{i}")])
        R.mark(f"a2_q1buf_free_{i}", sSt)
        if taps and i < 5:
            dma("gpsimd", tap["t_q1"][i * P:(i + 1) * P, :], q1t[:, :], sSt)
            R.mark(f"a2_q1buf_free_{i}", sSt)
            dma("gpsimd", tap["t_k1"][i * P:(i + 1) * P, :], k1b[idx][:, :], sSt,
                waits=[R.at(f"a2_k1done_{i}")])
            R.mark(f"tf_k1_{i}", sSt)

        w_kv = [R.at(f"a2_k1done_{i}"), R.at(f"a2_vdone_{i}"), R.at("vones")]
        if i >= 1:
            w_kv.append(R.at(f"a2_kvadd_{i - 1}"))
        for h in range(H):
            R.op("tensor", lambda e, o=ps_kv[0:65, h * 64:(h + 1) * 64],
                 l=vxb[idx][:, h, :], r=k1b[idx][:, h * 64:(h + 1) * 64]:
                 e.matmul(o, l, r, start=True, stop=True),
                 waits=w_kv if h == 0 else [],
                 incs=[(sT, 1)] if h == H - 1 else [])
        t_ein = sT.n
        R.mark(f"a2_einsum_{i}", sT)
        R.mark(f"a2_vfree_{i}", sT)
        R.mark(f"a2_lffree_{i}", sT)
        R.op("vector", lambda e, o=kvacc[:, :], a=ps_kv[0:65, :], b=kvacc[:, :]:
             e.scalar_tensor_tensor(o, a, 1.0, b, ALU.mult, ALU.add),
             waits=[(sT, t_ein)], incs=[(sV, 1)], prepad=True, fence=True)
        R.mark(f"a2_kvadd_{i}", sV)

        prep_pump()

    while prep_state["clock"] < NPREP + 2:
        prep_advance()
    R.mark("wprep_done", sSt)
    R.mark("prep_tr_done", sTr)
    R.mark("a2_S_done", sS)
    R.mark("a2_T_done", sT)
    R.mark("a2_kv_done", sV)
    scope_exit(scA2)
    scope_exit(scPREP)

    def post_a2_gate():
        return [R.at("a2_T_done"), R.at("a2_kv_done"), R.at("wprep_done"),
                R.at("prep_tr_done"), R.at("a2_S_done")]

    scCW = []
    x2 = [scope_alloc(scCW, f"x2c{j}", [P, C], F32) for j in range(2)]
    e2 = [scope_alloc(scCW, f"e2_{j}", [P, C], F32) for j in range(4)]
    cgh = scope_alloc(scCW, "cgh", [P, 8, 1024], BF16)
    cgl = scope_alloc(scCW, "cgl", [P, 8, 1024], BF16)
    cuh = scope_alloc(scCW, "cuh", [P, 8, 1024], BF16)
    cul = scope_alloc(scCW, "cul", [P, 8, 1024], BF16)
    cdh = scope_alloc(scCW, "cdh", [P, 8, C], BF16)
    cdl = scope_alloc(scCW, "cdl", [P, 8, C], BF16)

    scIL = []
    kv_all = scope_alloc(scIL, "kv_all", [65, C], F32)
    KSb = scope_alloc(scIL, "KSb", [P, C], F32)
    w2h = scope_alloc(scIL, "w2h", [P, 8, C], BF16)
    w2l = scope_alloc(scIL, "w2l", [P, 8, C], BF16)
    scILb = []
    wpst = [scope_alloc(scILb, f"wpst{j}", [64, C], F32) for j in range(4)]

    if taps:
        dma("gpsimd", tap["t_kvpart"][:, :], kvacc[:, :], sSt,
            waits=[R.at("a2_kv_done")])
    dma("gpsimd", cc_in[:, :], kvacc[:, :], sSt, waits=[R.at("a2_kv_done")])
    st_cc = sSt.n
    R.op("gpsimd", lambda e: e.collective_compute(
        "AllReduce", ALU.add, replica_groups=[[0, 1, 2, 3], [4, 5, 6, 7]],
        ins=[cc_in[:, :]], outs=[cc_out[:, :]]),
        waits=[(sSt, st_cc)], incs=[(sCC, 1)])
    dma("gpsimd", kv_all[:, :], cc_out[:, :], sSt,
        waits=[(sCC, 1)] + post_a2_gate())
    st_kv = sSt.n
    if taps:
        dma("gpsimd", tap["t_kvall"][:, :], kv_all[:, :], sSt, waits=[(sSt, st_kv)])

    dma("sync", KSb[0:1, :], kv_all[64:65, :], sTr,
        waits=[(sSt, st_kv)] + post_a2_gate())
    kk = 1
    while kk < P:
        step = min(kk, P - kk)
        dma("sync", KSb[kk:kk + step, :], KSb[0:step, :], sTr,
            waits=[(sTr, sTr.n)])
        kk += step
    tr_ks = sTr.n
    if taps:
        dma("gpsimd", tap["t_ksb"][:, :], KSb[:, :], sSt, waits=[(sTr, tr_ks)])

    w2_ldt, w2_mmt = {}, {}

    def w2_load(h):
        w_wp = []
        if h == 0:
            w_wp.append((sSt, st_kv))
            w_wp.extend(post_a2_gate())
        if h >= 4:
            w_wp.append((sT, w2_mmt[h - 4]))
        dma("sync", wpst[h % 4][:, :], wp_d[h * 64:(h + 1) * 64, :], sLd,
            waits=w_wp)
        w2_ldt[h] = sLd.n

    for h in range(4):
        w2_load(h)
    for pr in range(8):
        for hh in range(2):
            h = 2 * pr + hh
            for nch2 in range(2):
                w_mm = [(sLd, w2_ldt.pop(h))] if nch2 == 0 else []
                if pr >= 1 and hh == 0 and nch2 == 0:
                    w_mm.append(R.at(f"w2evac_{pr - 1}"))
                R.op("tensor", lambda e,
                     o=psW[hh * 64:hh * 64 + 64, nch2 * 512:(nch2 + 1) * 512],
                     l=kv_all[0:64, h * 64:(h + 1) * 64],
                     r=wpst[h % 4][:, nch2 * 512:(nch2 + 1) * 512]:
                     e.matmul(o, l, r, start=True, stop=True),
                     waits=w_mm,
                     incs=[(sT, 1)] if nch2 == 1 else [])
            w2_mmt[h] = sT.n
            if h + 4 < 16:
                w2_load(h + 4)
        t_w2 = sT.n
        R.op("vector", lambda e, o=w2h[:, pr, :], i=psW[:, :]: e.tensor_copy(o, i),
             waits=[(sT, t_w2)], prepad=True)
        R.op("vector", lambda e, o=w2l[:, pr, :], a=psW[:, :], b=w2h[:, pr, :]:
             e.tensor_tensor(o, a, b, ALU.subtract), incs=[(sV, 1)], fence=True)
        R.mark(f"w2evac_{pr}", sV)
        if taps and pr == 0:
            dma("gpsimd", tap["t_w2h"][:, :], w2h[:, 0, :], sSt,
                waits=[R.at("w2evac_0")])
            dma("gpsimd", tap["t_w2l"][:, :], w2l[:, 0, :], sSt)
    R.mark("w2_ready", sV)
    R.bankprev["c2"] = R.at("w2_ready")
    R.bankprev["c3"] = R.at("w2_ready")
    scope_exit(scILb)

    scB = []
    BP_t = scope_alloc(scB, "BPt", [P, C], F32)
    nmb = scope_alloc(scB, "nmb", [P, C], F32)
    n16 = [scope_alloc(scB, f"n16_{j}", [P, H], F32) for j in range(2)]
    n16r = [scope_alloc(scB, f"n16r_{j}", [P, H], F32) for j in range(2)]
    qnh_b = [scope_alloc(scB, f"qnh{j}", [P, C], BF16) for j in range(2)]
    qnl_b = [scope_alloc(scB, f"qnl{j}", [P, C], BF16) for j in range(2)]
    bfh = [scope_alloc(scB, f"bfh{j}", [P, 8, P], BF16) for j in range(2)]
    bfl = [scope_alloc(scB, f"bfl{j}", [P, 8, P], BF16) for j in range(2)]

    dma("sync", BP_t[:, :], bpt_d[:, :], sLd, waits=post_a2_gate())
    ld_bp = sLd.n

    def load_cw(third, waits):
        n0, n1 = N3[third]
        w = list(waits)
        last = 0
        for buf, spl in ((cgh, wgh_sp), (cgl, wgl_sp), (cuh, wuh_sp), (cul, wul_sp)):
            for kc in range(8):
                dma("sync", buf[:, kc, 0:n1 - n0], spl[:, kc, n0:n1], sLd, waits=w)
                w = []
        for buf, spl in ((cdh, wdh_sp), (cdl, wdl_sp)):
            for j in range((n1 - n0) // P):
                kc = n0 // P + j
                dma("sync", buf[:, j, :], spl[:, kc, :], sLd)
        return sLd.n

    ld_cw0 = load_cw(0, post_a2_gate())
    if taps:
        dma("gpsimd", tap["t_cgh"][:, :, :], cgh[:, :, :], sSt, waits=[(sLd, ld_cw0)])
        dma("gpsimd", tap["t_cgl"][:, :, :], cgl[:, :, :], sSt)
        dma("gpsimd", tap["t_cdh"][:, :, :], cdh[:, :, :], sSt)
        dma("gpsimd", tap["t_cdl"][:, :, :], cdl[:, :, :], sSt)

    b_ldt = {}
    bbanks = [pp[0], pp[1], psD[0], psD[1]]
    R.bankprev["b0"] = R.bankprev.pop(0)
    R.bankprev["b1"] = R.bankprev.pop(1)

    def b_loads(i):
        if i >= 2:
            w_q1 = [R.at(f"b_q1free_{i - 2}")]
        else:
            w_q1 = post_a2_gate()
        dma("sync", f2[i % 2][:, :], q1_sp[i * P:(i + 1) * P, :], sLd, waits=w_q1)
        w_x = [(sLd, ld_bp)]
        if i >= 2:
            w_x.append(R.at(f"b_xfree_{i - 2}"))
        dma("sync", x2[i % 2][:, :], x_d[i * P:(i + 1) * P, :], sLd, waits=w_x)
        b_ldt[i] = sLd.n

    def b_tail(j, st):
        xt, pout, t_bs = st
        for nch in range(2):
            bk = (2 * j + nch) % 4
            w_ev = [(sT, t_bs[nch])]
            if nch == 0 and j >= 2:
                w_ev.append(R.at(f"b_poutfree_{j - 2}"))
            R.op("vector", lambda e, o=pout[:, nch * 512:(nch + 1) * 512],
                 a=bbanks[bk][:, :], b=xt[:, nch * 512:(nch + 1) * 512]:
                 e.scalar_tensor_tensor(o, a, 1.0, b, ALU.mult, ALU.add),
                 waits=w_ev, incs=[(sV, 1)], prepad=True)
            bank_release(f"b{bk}", sV, sV.n)
        v_pout = sV.n
        R.mark(f"b_xfree_{j}", sV)
        dma("gpsimd", part_sp[j * P:(j + 1) * P, :], pout[:, :], sSt,
            waits=[(sV, v_pout)])
        R.mark(f"b_poutfree_{j}", sSt)
        if taps and j < 2:
            dma("gpsimd", tap["t_part"][j * P:(j + 1) * P, :], pout[:, :], sSt)
            R.mark(f"b_poutfree_{j}", sSt)

    b_loads(0)
    b_prev = None
    for i in range(NTIL):
        idx = i % 2
        q1t = f2[idx]
        xt = x2[i % 2]
        pout = e2[idx]
        ld_b = b_ldt.pop(i)

        w_nm = [(sLd, ld_b), (sTr, tr_ks)]
        if i == 1:
            w_nm += R.wtap("tf_qn")
        R.op("vector", lambda e, o=nmb[:, :], a=q1t, b=KSb: e.tensor_tensor(
            o, a[:, :], b[:, :], ALU.mult), waits=w_nm)
        R.op("vector", lambda e, o=n16[idx][:, :], i=nmb[:, :]: e.tensor_reduce(
            o, i.rearrange("p (h d) -> p h d", d=64), AX.X, ALU.add),
            incs=[(sV, 1)], pad=True)
        R.op("vector", lambda e, o=n16[idx][:, :], i=n16[idx][:, :]: e.tensor_scalar(
            o, i, 1e-8, None, ALU.add), waits=[(sV, sV.n)], incs=[(sV, 1)],
            pad=True)
        R.op("vector", lambda e, o=n16r[idx][:, :], i=n16[idx][:, :]: e.reciprocal(o, i),
             waits=[(sV, sV.n)], incs=[(sV, 1)])
        if taps and i == 0:
            dma("gpsimd", tap["t_norm"][:, :], n16r[idx][:, :], sSt,
                waits=[(sV, sV.n)])
        R.op("vector", lambda e, o=nmb, a=q1t, b=n16r[idx]: e.tensor_tensor(
            o[:, :].rearrange("p (h d) -> p h d", d=64),
            a[:, :].rearrange("p (h d) -> p h d", d=64),
            b[:, :].broadcast_to((P, H, 64)), ALU.mult), waits=[(sV, sV.n)],
            incs=[(sV, 1)])
        R.mark(f"b_q1free_{i}", sV)
        if taps and i == 0:
            dma("gpsimd", tap["t_qn"][:, :], nmb[:, :], sSt,
                waits=[R.at("b_q1free_0")])
            R.mark("tf_qn", sSt)
        w_qn = [R.at(f"b_qnfree_{i - 2}")] if i >= 2 else []
        R.op("vector", lambda e, o=qnh_b[idx], i=nmb: e.tensor_copy(o[:, :], i[:, :]),
             waits=w_qn)
        R.op("vector", lambda e, o=qnl_b[idx], a=nmb, b=qnh_b[idx]: e.tensor_tensor(
            o[:, :], a[:, :], b[:, :], ALU.subtract), incs=[(sV, 1)], fence=True)
        v_qn = sV.n
        w_bf = [R.at(f"b_bffree_{i - 2}")] if i >= 2 else []
        dma("sync", bfh[idx][:, :, :], qnh_b[idx][:, :], sTr,
            waits=[(sV, v_qn)] + w_bf, transpose=True)
        dma("sync", bfl[idx][:, :, :], qnl_b[idx][:, :], sTr, transpose=True)
        tr_b = sTr.n
        R.ck[f"b_qnfree_{i}"] = (sTr, tr_b)
        if taps and i == 0:
            dma("gpsimd", tap["t_qnh"][:, :], qnh_b[idx][:, :], sSt,
                waits=[(sV, v_qn)])
            dma("gpsimd", tap["t_qnl"][:, :], qnl_b[idx][:, :], sSt)
            dma("gpsimd", tap["t_bfh"][:, :, :], bfh[idx][:, :, :], sSt,
                waits=[(sTr, tr_b)])
            R.ck[f"b_qnfree_{i}"] = (sSt, sSt.n)

        w_px = [(sLd, ld_b)]
        R.op("vector", lambda e, o=xt, a=xt, b=BP_t: e.tensor_tensor(
            o[:, :], a[:, :], b[:, :], ALU.add), waits=w_px, incs=[(sV, 1)])

        t_bs = []
        for nch in range(2):
            bk = (2 * i + nch) % 4
            t_b = proj3(bbanks[bk], f"b{bk}", bfh[idx], bfl[idx], w2h, w2l, 512,
                        np.s_[nch * 512:(nch + 1) * 512],
                        [R.at("w2_ready"), (sTr, tr_b)])
            t_bs.append(t_b)
        R.mark(f"b_bffree_{i}", sT)

        st_here = (xt, pout, t_bs)
        if b_prev is not None:
            b_tail(i - 1, b_prev)
        b_prev = st_here
        if i + 1 < NTIL:
            b_loads(i + 1)

    b_tail(NTIL - 1, b_prev)
    R.bankprev[0] = R.bankprev["b0"]
    R.bankprev[1] = R.bankprev["b1"]
    R.bankprev["D0_0"] = R.bankprev["b2"]
    R.bankprev["D1_0"] = R.bankprev["b3"]

    R.mark("b_T_done", sT)
    R.mark("b_V_done", sV)
    scope_exit(scB)
    scope_exit(scIL)

    scC = []
    qpC = {
        "yq": [scope_alloc(scC, f"yqC{j}", [P, 1024], F32) for j in range(2)],
        "sb": [scope_alloc(scC, f"sbC{j}", [P, 1024], BF16) for j in range(7)],
        "ub": [scope_alloc(scC, f"ubC{j}", [P, 1024], BF16) for j in range(2)],
        "sg": [scope_alloc(scC, "sgC0", [P, 1024], BF16)] * 2,
    }
    qpC["aa"] = qpC["yq"]
    xfh = [scope_alloc(scC, f"xfh{j}", [P, 8, P], BF16) for j in range(2)]
    xfl = [scope_alloc(scC, f"xfl{j}", [P, 8, P], BF16) for j in range(2)]
    hqh_b = [scope_alloc(scC, f"hqh{j}", [P, 1024], BF16) for j in range(2)]
    hql_b = [scope_alloc(scC, f"hql{j}", [P, 1024], BF16) for j in range(2)]
    hfh = [scope_alloc(scC, f"hfh{j}", [P, 8, P], BF16) for j in range(4)]
    hfl = [scope_alloc(scC, f"hfl{j}", [P, 8, P], BF16) for j in range(4)]

    R.bankprev["D0_1"] = R.at("a2_kv_done")
    R.bankprev["D1_1"] = R.at("a2_kv_done")
    R.bankprev["c0"] = R.bankprev.get(0)
    R.bankprev["c1"] = R.bankprev.get(1)
    if R.bankprev["c0"] is None:
        del R.bankprev["c0"]
    if R.bankprev["c1"] is None:
        del R.bankprev["c1"]

    def c_down(third, i, st_c):
        TN, kch, idx4, tr_h, pin, ld_pin = st_c
        dbanks = [psD[0], psD[1], ps_kv[:, 0:512], ps_kv[:, 512:1024]]
        for mch in range(2):
            bank = dbanks[mch + 2 * (i % 2)]
            triples = []
            for kc in range(kch):
                triples.append((bank[:, :], hfh[idx4][:, kc, :],
                                cdh[:, kc, mch * 512:(mch + 1) * 512]))
                triples.append((bank[:, :], hfh[idx4][:, kc, :],
                                cdl[:, kc, mch * 512:(mch + 1) * 512]))
                triples.append((bank[:, :], hfl[idx4][:, kc, :],
                                cdh[:, kc, mch * 512:(mch + 1) * 512]))
            mm_group(bank, f"D{mch}_{i % 2}", triples, [(sTr, tr_h)])
            t_d = sT.n
            R.op("vector", lambda e, o=pin[:, mch * 512:(mch + 1) * 512],
                 a=bank[:, :], b=pin[:, mch * 512:(mch + 1) * 512]:
                 e.scalar_tensor_tensor(o, a, 1.0, b, ALU.mult, ALU.add),
                 waits=[(sT, t_d), (sLd, ld_pin)], incs=[(sV, 1)], prepad=True)
            bank_release(f"D{mch}_{i % 2}", sV, sV.n)
        v_p = sV.n
        R.mark(f"c_hffree_{third}_{i}", sT)
        if third < 2:
            dma("gpsimd", part_sp[i * P:(i + 1) * P, :], pin[:, :], sSt,
                waits=[(sV, v_p)])
        else:
            dma("gpsimd", y_d[i * P:(i + 1) * P, :], pin[:, :], sSt,
                waits=[(sV, v_p)])
        R.mark(f"c_pdone_{third}_{i}", sSt)
        R.mark(f"c_pinfree_{third}_{i}", sSt)

    for third in range(3):
        n0, n1 = N3[third]
        TN = n1 - n0
        kch = TN // P
        if third > 0:
            ld_cw = load_cw(third, [R.at(f"c_T_done_{third - 1}")])
        else:
            ld_cw = ld_cw0
        nck = [(j * 512, min(TN, (j + 1) * 512)) for j in range((TN + 511) // 512)]
        nnc = len(nck)
        c_pending = []
        ldt = {}

        def c_loads(i):
            idx = i % 2
            idx4 = i % 4
            if i >= 2:
                w_xf = [R.at(f"c_xffree_{third}_{i - 2}")]
            elif third > 0:
                w_xf = [R.at(f"c_xffree_{third - 1}_{NTIL - 2 + i}")]
            else:
                w_xf = [R.at("b_V_done"), R.at("b_T_done")]
            if i == 0:
                w_xf.append((sLd, ld_cw))
            dma("sync", xfh[idx][:, :, :], xqh_sp[i, :, :, :], sLd, waits=w_xf)
            dma("sync", xfl[idx][:, :, :], xql_sp[i, :, :, :], sLd)
            ld_c = sLd.n
            if third == 0:
                w_pin = [R.at(f"b_poutfree_{max(i, NTIL - 2 + (i % 2))}")]
            else:
                w_pin = [R.at(f"c_pdone_{third - 1}_{i}")]
            if i >= 4:
                w_pin.append(R.at(f"c_pinfree_{third}_{i - 4}"))
            elif third > 0:
                w_pin.append(R.at(f"c_pinfree_{third - 1}_{NTIL - 4 + i}"))
            dma("sync", e2[idx4][:, :], part_sp[i * P:(i + 1) * P, :], sLd,
                waits=w_pin)
            ldt[i] = (ld_c, sLd.n)

        c_loads(0)
        for i in range(NTIL):
            idx = i % 2
            idx4 = i % 4
            hbuf = g2[idx]
            sgt = x2[idx]
            upt = f2[idx]
            pin = e2[idx4]
            ld_c, ld_pin = ldt.pop(i)
            if i + 1 < NTIL:
                c_loads(i + 1)

            cbanks = [pp[0], pp[1], psW[:, 0:512], psW[:, 512:1024]]
            for gu, (wbh, wbl) in enumerate(((cgh, cgl), (cuh, cul))):
                dst = sgt if gu == 0 else upt
                for ncj, (c0, c1) in enumerate(nck):
                    seq = gu * nnc + ncj
                    bank = cbanks[seq % 4]
                    t_c = proj3(bank, f"c{seq % 4}", xfh[idx], xfl[idx], wbh, wbl,
                                c1 - c0, np.s_[c0:c1], [(sLd, ld_c)])
                    w_ev = [(sT, t_c)]
                    if ncj == 0:
                        if third == 0 and i < 2:
                            w_ev.append(R.at("b_V_done"))
                        elif i >= 2:
                            w_ev.append(R.at(f"c_sufree_{third}_{i - 2}"))
                        elif third > 0:
                            w_ev.append(R.at(f"c_sufree_{third - 1}_{NTIL - 2 + i}"))
                    R.op("scalar", lambda e, o=dst[:, c0:c1], i=bank[:, 0:c1 - c0],
                         f=(AF.Silu if gu == 0 else AF.Copy):
                         e.activation(o, i, f), waits=w_ev, incs=[(sS, 1)],
                         prepad=True)
                    bank_release(f"c{seq % 4}", sS, sS.n)
            s_gu = sS.n
            R.mark(f"c_xffree_{third}_{i}", sT)

            if len(c_pending) >= 2:
                c_down(third, i - 2, c_pending.pop(0))

            w_h = []
            if i >= 1:
                w_h = [R.at(f"c_hfree_{third}_{i - 1}")]
            elif third > 0:
                w_h = [R.at(f"c_hfree_{third - 1}_{NTIL - 1}")]
            if third == 0 and i == 1:
                w_h += R.wtap("tf_h")
            R.op("vector", lambda e, o=hbuf[:, 0:TN], a=sgt[:, 0:TN], b=upt[:, 0:TN]:
                 e.tensor_tensor(o, a, b, ALU.mult), waits=[(sS, s_gu)] + w_h,
                 incs=[(sV, 1)])
            R.mark(f"c_sufree_{third}_{i}", sV)
            v_h = sV.n
            if taps and third == 0 and i == 0:
                dma("gpsimd", tap["t_h"][:, 0:TN], hbuf[:, 0:TN], sSt,
                    waits=[(sV, v_h)])
                R.mark("tf_h", sSt)

            w_qv = [(sV, v_h)]
            if third == 0 and i < 2:
                w_qv.append(R.at("b_V_done"))
                w_qv.append(R.at("b_T_done"))
            rq_t = quant_chain(qpC, "C", hbuf[:, 0:TN], TN, idx,
                               hqh_b[idx][:, 0:TN], hql_b[idx][:, 0:TN], w_qv)
            R.mark(f"c_hfree_{third}_{i}", sV)

            w_hf = []
            if i >= 4:
                w_hf = [R.at(f"c_hffree_{third}_{i - 4}")]
            elif third > 0:
                w_hf = [R.at(f"c_hffree_{third - 1}_{NTIL - 4 + i}")]
            else:
                w_hf = [R.at("b_T_done")]
            dma("sync", hfh[idx4][:, 0:kch, :], hqh_b[idx][:, 0:TN], sTr,
                waits=[rq_t] + w_hf, transpose=True)
            dma("sync", hfl[idx4][:, 0:kch, :], hql_b[idx][:, 0:TN], sTr,
                transpose=True)
            tr_h = sTr.n
            c_pending.append((TN, kch, idx4, tr_h, pin, ld_pin))

        for joff, st_c in enumerate(c_pending):
            c_down(third, NTIL - len(c_pending) + joff, st_c)
        c_pending = []
        R.mark(f"c_T_done_{third}", sT)

    scope_exit(scC)
    scope_exit(scCW)

    final_st = sSt.n
    R.op("gpsimd", None, waits=[(sSt, final_st)])

    with ExitStack() as es:
        for s in R.sems:
            s.h = es.enter_context(nc.semaphore(s.name))
        with nc.Block() as block:
            @block.sync
            def _(e):
                R.replay(e, "sync")

            @block.scalar
            def _(e):
                R.replay(e, "scalar")

            @block.vector
            def _(e):
                R.replay(e, "vector")

            @block.tensor
            def _(e):
                R.replay(e, "tensor")

            @block.gpsimd
            def _(e):
                R.replay(e, "gpsimd")

    return nc


_NC_CACHE = {}


def _get_nc(taps=False):
    key = bool(taps)
    if key not in _NC_CACHE:
        _NC_CACHE[key] = build_program(taps=key)
    return _NC_CACHE[key]


def make_inputs(L0_attn_x, L0_attn_Wqkv, L0_attn_Wg, L0_attn_bg, L0_attn_Wp,
                L0_attn_bp, L0_ffn_w_gate, L0_ffn_w_up, L0_ffn_w_down):
    x = np.ascontiguousarray(np.asarray(L0_attn_x, dtype=np.float32))
    wqkv = np.ascontiguousarray(np.asarray(L0_attn_Wqkv, dtype=np.float32))
    wg = np.ascontiguousarray(np.asarray(L0_attn_Wg, dtype=np.float32))
    bg = np.asarray(L0_attn_bg, dtype=np.float32)
    wp = np.ascontiguousarray(np.asarray(L0_attn_Wp, dtype=np.float32))
    bp = np.asarray(L0_attn_bp, dtype=np.float32)
    wgt = np.ascontiguousarray(np.asarray(L0_ffn_w_gate, dtype=np.float32))
    wup = np.ascontiguousarray(np.asarray(L0_ffn_w_up, dtype=np.float32))
    wdn = np.ascontiguousarray(np.asarray(L0_ffn_w_down, dtype=np.float32))

    bgt = np.ascontiguousarray(np.tile(bg[None, :], (P, 1)))
    bpt = np.ascontiguousarray(np.tile(bp[None, :], (P, 1)))
    csts = np.zeros((P, 8), dtype=np.float32)
    csts[:, 0] = 1e-5
    for j, t in enumerate(THR):
        csts[:, 1 + j] = -t

    in_maps = []
    for c in range(8):
        b, qt = divmod(c, 4)
        xs = np.ascontiguousarray(x[b, qt * NT:(qt + 1) * NT, :])
        in_maps.append({
            "x": xs, "wqkv": wqkv, "wg": wg, "wp": wp,
            "wgate": wgt, "wup": wup, "wdown": wdn,
            "bgt": bgt, "bpt": bpt, "csts_d": csts,
        })
    return in_maps


def kernel(L0_attn_x, L0_attn_Wqkv, L0_attn_Wg, L0_attn_bg, L0_attn_Wp,
           L0_attn_bp, L0_ffn_w_gate, L0_ffn_w_up, L0_ffn_w_down):
    in_maps = make_inputs(L0_attn_x, L0_attn_Wqkv, L0_attn_Wg, L0_attn_bg,
                          L0_attn_Wp, L0_attn_bp, L0_ffn_w_gate, L0_ffn_w_up,
                          L0_ffn_w_down)
    nc = _get_nc(taps=False)
    res = run_bass_kernel_spmd(nc, in_maps, core_ids=list(range(8)))
    y = np.empty((2, 4 * NT, C), dtype=np.float32)
    for c in range(8):
        b, qt = divmod(c, 4)
        y[b, qt * NT:(qt + 1) * NT, :] = res.results[c]["y"]
    return y


# version 21
# speedup vs baseline: 1.0428x; 1.0007x over previous
import numpy as np
from contextlib import ExitStack

import concourse.bass as bass
import concourse.mybir as mybir
from concourse.bass_utils import run_bass_kernel_spmd

F32 = mybir.dt.float32
BF16 = mybir.dt.bfloat16
AF = mybir.ActivationFunctionType
ALU = mybir.AluOpType
AX = mybir.AxisListType

P = 128
C = 1024
NT = 4096
NTIL = NT // P
H = 16
Dh = 64
I = 2816
N3 = [(0, 1024), (1024, 2048), (2048, 2816)]
THR = [0.25, 0.75, 1.25, 1.75, 2.5, 3.5, 5.0]

ENGS = ("sync", "scalar", "vector", "tensor", "gpsimd")


class Sem:
    __slots__ = ("name", "n", "h")

    def __init__(self, name):
        self.name, self.n, self.h = name, 0, None


class Rec:

    def __init__(self):
        self.ops = {e: [] for e in ENGS}
        self.sems = []
        self.ck = {}
        self.bankprev = {}
        self.jnk = None
        self.jnkS = None

    def sem(self, name):
        s = Sem(name)
        self.sems.append(s)
        return s

    def _junk(self, eng):
        if eng == "scalar":
            j = self.jnkS
            return lambda e, o=j: e.memzero(o)
        j = self.jnk
        return lambda e, o=j: e.memset(o, 1.0)

    def op(self, eng, fn, waits=(), incs=(), pad=False, prepad=False, fence=False):
        w = tuple((s, v) for s, v in waits if v > 0)
        if prepad:
            self.ops[eng].append((w, self._junk(eng), ()))
        if fence:
            self.ops[eng].append((w, fn, ()))
            for s, k in incs:
                s.n += k
            self.ops[eng].append(((), self._junk(eng), tuple(incs)))
        else:
            for s, k in incs:
                s.n += k
            self.ops[eng].append((w, fn, tuple(incs)))
        if pad:
            self.ops[eng].append(((), self._junk(eng), ()))

    def mark(self, name, sem):
        self.ck[name] = (sem, sem.n)

    def at(self, name):
        return self.ck[name]

    def wtap(self, name):
        return [self.ck[name]] if name in self.ck else []

    def replay(self, e, eng):
        lastw = {}
        for waits, fn, incs in self.ops[eng]:
            for s, v in waits:
                if lastw.get(id(s), 0) >= v:
                    continue
                e.wait_ge(s.h, v)
                lastw[id(s)] = v
            ins = fn(e) if fn is not None else None
            for s, k in incs:
                ins.then_inc(s.h, k)


def build_program(taps=False):
    nc = bass.Bass("TRN2", target_bir_lowering=False)

    x_d = nc.dram_tensor("x", [NT, C], F32, kind="ExternalInput").ap()
    wqkv_d = nc.dram_tensor("wqkv", [C, 3 * C], F32, kind="ExternalInput").ap()
    wg_d = nc.dram_tensor("wg", [C, C], F32, kind="ExternalInput").ap()
    wp_d = nc.dram_tensor("wp", [C, C], F32, kind="ExternalInput").ap()
    wgate_d = nc.dram_tensor("wgate", [I, C], F32, kind="ExternalInput").ap()
    wup_d = nc.dram_tensor("wup", [I, C], F32, kind="ExternalInput").ap()
    wdown_d = nc.dram_tensor("wdown", [C, I], F32, kind="ExternalInput").ap()
    bgt_d = nc.dram_tensor("bgt", [P, C], F32, kind="ExternalInput").ap()
    bpt_d = nc.dram_tensor("bpt", [P, C], F32, kind="ExternalInput").ap()
    csts_d = nc.dram_tensor("csts_d", [P, 8], F32, kind="ExternalInput").ap()
    y_d = nc.dram_tensor("y", [NT, C], F32, kind="ExternalOutput").ap()

    gates_sp = nc.dram_tensor("gates_sp", [NT, C], F32, kind="Internal").ap()
    q1_sp = nc.dram_tensor("q1_sp", [NT, C], F32, kind="Internal").ap()
    part_sp = nc.dram_tensor("part_sp", [NT, C], F32, kind="Internal").ap()
    xnh_sp = nc.dram_tensor("xnh_sp", [NTIL, P, 8, P], BF16, kind="Internal").ap()
    xnl_sp = nc.dram_tensor("xnl_sp", [NTIL, P, 8, P], BF16, kind="Internal").ap()
    xqh_sp = nc.dram_tensor("xqh_sp", [NTIL, P, 8, P], BF16, kind="Internal").ap()
    xql_sp = nc.dram_tensor("xql_sp", [NTIL, P, 8, P], BF16, kind="Internal").ap()
    wgh_sp = nc.dram_tensor("wgh_sp", [P, 8, I], BF16, kind="Internal").ap()
    wgl_sp = nc.dram_tensor("wgl_sp", [P, 8, I], BF16, kind="Internal").ap()
    wuh_sp = nc.dram_tensor("wuh_sp", [P, 8, I], BF16, kind="Internal").ap()
    wul_sp = nc.dram_tensor("wul_sp", [P, 8, I], BF16, kind="Internal").ap()
    wdh_sp = nc.dram_tensor("wdh_sp", [P, 22, C], BF16, kind="Internal").ap()
    wdl_sp = nc.dram_tensor("wdl_sp", [P, 22, C], BF16, kind="Internal").ap()
    cc_in = nc.dram_tensor("cc_in", [65, C], F32, kind="Internal").ap()
    cc_out = nc.dram_tensor("cc_out", [65, C], F32, kind="Internal").ap()

    tap = {}
    if taps:
        for nm, shp, dt in [("t_xn", [640, C], F32), ("t_gates", [640, C], F32),
                            ("t_q1", [640, C], F32), ("t_k1", [640, C], F32),
                            ("t_kvpart", [65, C], F32), ("t_kvall", [65, C], F32),
                            ("t_part", [256, C], F32), ("t_h", [P, 1024], F32),
                            ("t_xqh", [640, C], BF16), ("t_xql", [640, C], BF16),
                            ("t_ksb", [P, C], F32), ("t_norm", [P, H], F32),
                            ("t_qn", [P, C], F32), ("t_w2h", [P, C], BF16),
                            ("t_w2l", [P, C], BF16), ("t_qnh", [P, C], BF16),
                            ("t_qnl", [P, C], BF16), ("t_bfh", [P, 8, P], BF16),
                            ("t_cgh", [P, 8, 1024], BF16), ("t_cgl", [P, 8, 1024], BF16),
                            ("t_cdh", [P, 8, C], BF16), ("t_cdl", [P, 8, C], BF16),
                            *[(f"t_dbg{k}", [P, 1], F32) for k in range(15)],
                            ("t_psph", [P, 1408], BF16), ("t_pspl", [P, 1408], BF16),
                            ("t_pyq", [P, 1408], F32), ("t_paa", [P, 1408], F32)]:
            tap[nm] = nc.dram_tensor(nm, shp, dt, kind="ExternalOutput").ap()

    R = Rec()
    sLd = R.sem("sLd")
    sTr = R.sem("sTr")
    sSt = R.sem("sSt")
    sS = R.sem("sS")
    sV = R.sem("sV")
    sT = R.sem("sT")
    sCC = R.sem("sCC")
    sP = R.sem("sP")

    def sb(name, shape, dt):
        g = nc.sbuf_tensor(name, shape, dt)
        t = g.__enter__()
        return t.ap(), g

    def scope_alloc(scope, name, shape, dt):
        ap, g = sb(name, shape, dt)
        scope.append(g)
        return ap

    def scope_exit(scope):
        for g in reversed(scope):
            g.__exit__(None, None, None)
        scope.clear()

    csts, _ = sb("csts", [P, 8], F32)
    f2 = [sb(f"f2_{j}", [P, C], F32)[0] for j in range(2)]
    g2 = [sb(f"g2_{j}", [P, C], F32)[0] for j in range(2)]
    acc1 = [sb(f"acc1_{j}", [P, 1], F32)[0] for j in range(2)]
    acc2 = [sb(f"acc2_{j}", [P, 1], F32)[0] for j in range(2)]
    tmean = [sb(f"tmean_{j}", [P, 1], F32)[0] for j in range(2)]
    tvar = [sb(f"tvar_{j}", [P, 1], F32)[0] for j in range(2)]
    trstd = [sb(f"trstd_{j}", [P, 1], F32)[0] for j in range(2)]
    tnmr = [sb(f"tnmr_{j}", [P, 1], F32)[0] for j in range(2)]
    ln_a = [sb(f"ln_a{j}", [P, 1], F32)[0] for j in range(2)]
    ln_b = [sb(f"ln_b{j}", [P, 1], F32)[0] for j in range(2)]
    red2 = [sb(f"red2_{j}", [P, 88], F32)[0] for j in range(2)]
    sq2 = [sb(f"sq2_{j}", [P, 88], F32)[0] for j in range(2)]
    rq2 = [sb(f"rq2_{j}", [P, 88], F32)[0] for j in range(2)]
    kvacc, _ = sb("kvacc", [65, C], F32)
    jnkt, _ = sb("jnkt", [P, 8], F32)
    jnkS, _ = sb("jnkS", [P, 8], F32)
    R.jnk = jnkt[:, :]
    R.jnkS = jnkS[:, :]

    eps_c = csts[:, 0:1]
    thr_c = [csts[:, 1 + j:2 + j] for j in range(7)]

    ps_kv = nc.psum_tensor("ps_kv", [P, C], F32).__enter__().ap()
    pp = [nc.psum_tensor(f"pp{j}", [P, 512], F32).__enter__().ap() for j in range(2)]
    psW = nc.psum_tensor("psW", [P, C], F32).__enter__().ap()
    psD = [nc.psum_tensor(f"psD{j}", [P, 512], F32).__enter__().ap() for j in range(2)]

    def dma(eng, out, in_, sem, waits=(), transpose=False):
        if transpose:
            R.op(eng, lambda e, o=out, i=in_: e.dma_start(out=o, in_=i, transpose=True),
                 waits=waits, incs=[(sem, 16)])
        else:
            R.op(eng, lambda e, o=out, i=in_: e.dma_start(out=o, in_=i),
                 waits=waits, incs=[(sem, 16)])

    def mm_group(bank, bankkey, triples, waits_first):
        w0 = list(waits_first)
        prev = R.bankprev.get(bankkey)
        if prev is not None:
            w0.append(prev)
        n = len(triples)
        for t, (o, l, r) in enumerate(triples):
            R.op("tensor",
                 lambda e, o=o, l=l, r=r, s=(t == 0), p=(t == n - 1):
                 e.matmul(o, l, r, start=s, stop=p),
                 waits=w0 if t == 0 else [],
                 incs=[(sT, 1)] if t == n - 1 else [])

    def bank_release(bankkey, sem, val):
        R.bankprev[bankkey] = (sem, val)

    def proj3(bank, bankkey, lh, ll, wh, wl, ncols, csl, waits_first):
        triples = []
        for cch in range(8):
            triples.append((bank[:, 0:ncols], lh[:, cch, :], wh[:, cch, csl]))
            triples.append((bank[:, 0:ncols], lh[:, cch, :], wl[:, cch, csl]))
            triples.append((bank[:, 0:ncols], ll[:, cch, :], wh[:, cch, csl]))
        mm_group(bank, bankkey, triples, waits_first)
        return sT.n

    qprev = {}

    def quant_stage1(pool, key, src, width, idx, waits_v):
        nb = width // 16
        red, sq, rq = red2[idx][:, 0:nb], sq2[idx][:, 0:nb], rq2[idx][:, 0:nb]
        yq = pool["yq"][idx][:, 0:width]
        prev = [qprev[key]] if key in qprev else []
        R.op("vector", lambda e, r=red, s=src: e.tensor_reduce(
            r, s.rearrange("p (b k) -> p b k", k=16), AX.X, ALU.max,
            apply_absolute_value=True), waits=list(waits_v) + prev, pad=True)
        R.op("vector", lambda e, o=sq, i=red: e.tensor_scalar(
            o, i, 1.0 / 6.0, 1e-12, ALU.mult, ALU.max), incs=[(sV, 1)])
        v_sq = sV.n
        R.op("vector", lambda e, o=rq, i=sq: e.reciprocal(o, i),
             waits=[(sV, v_sq)], incs=[(sV, 1)])
        v_rq = sV.n
        R.op("vector", lambda e, o=yq, s=src, r=rq, k=nb: e.tensor_tensor(
            o.rearrange("p (b k) -> p b k", k=16),
            s.rearrange("p (b k) -> p b k", k=16),
            r.broadcast_to((P, k, 16)), ALU.mult), waits=[(sV, v_rq)],
            incs=[(sV, 1)])
        return {"v_yq": sV.n, "v_sq": v_sq, "sq": sq, "yq": yq,
                "idx": idx, "width": width, "nb": nb}

    def quant_stage2(pool, key, st, dst_h, dst_l, waits2=(), use_pool=True):
        width, nb, idx = st["width"], st["nb"], st["idx"]
        yq, sq = st["yq"], st["sq"]
        aa = pool["aa"][idx][:, 0:width]
        sb7 = [b[:, 0:width] for b in pool["sb"]]
        ub = [b[:, 0:width] for b in pool["ub"]]
        sgb = pool["sg"][idx][:, 0:width]
        prev = [qprev[key]] if key in qprev else []
        R.op("scalar", lambda e, o=sgb, i=yq: e.activation(o, i, AF.Sign),
             waits=[(sV, st["v_yq"])] + prev)
        R.op("scalar", lambda e, o=aa, i=yq: e.activation(o, i, AF.Abs))
        for t in range(4):
            R.op("scalar", lambda e, o=sb7[t], i=aa, b=thr_c[t]: e.activation(
                o, i, AF.Sign, bias=b), incs=[(sS, 1)] if t == 3 else [])
        s_ab4 = sS.n
        for t in range(4, 7):
            R.op("scalar", lambda e, o=sb7[t], i=aa, b=thr_c[t]: e.activation(
                o, i, AF.Sign, bias=b), incs=[(sS, 1)] if t == 6 else [])
        s_all = sS.n
        if use_pool:
            R.op("gpsimd", lambda e, o=ub[0], a=sb7[0], b=sb7[1]: e.tensor_tensor(
                o, a, b, ALU.add), waits=[(sS, s_ab4)] + list(waits2))
            R.op("gpsimd", lambda e, o=ub[1], a=sb7[2], b=sb7[3]: e.tensor_tensor(
                o, a, b, ALU.add))
            R.op("gpsimd", lambda e, o=ub[0], a=ub[0], b=ub[1]: e.tensor_tensor(
                o, a, b, ALU.add))
            R.op("gpsimd", lambda e, o=ub[1], a=sb7[4], b=sb7[5]: e.tensor_tensor(
                o, a, b, ALU.add), waits=[(sS, s_all)], incs=[(sP, 1)],
                fence=True)
            p_tree = sP.n
            if use_pool == "adds":
                R.op("vector", lambda e, o=ub[1], a=sb7[6], b=ub[1]:
                     e.scalar_tensor_tensor(o, a, 2.0, b, ALU.mult, ALU.add),
                     waits=[(sS, s_all), (sP, p_tree)], prepad=True)
                R.op("vector", lambda e, o=ub[0], a=ub[1], b=ub[0]:
                     e.scalar_tensor_tensor(o, a, 2.0, b, ALU.mult, ALU.add))
                R.op("vector", lambda e, o=ub[1], i=ub[0]: e.tensor_scalar(
                    o, i, 0.25, 3.0, ALU.mult, ALU.add))
                R.op("vector", lambda e, o=ub[0], a=ub[1], b=sgb: e.tensor_tensor(
                    o, a, b, ALU.mult))
                R.op("vector", lambda e, o=yq, a=ub[0], s=sq, k=nb: e.tensor_tensor(
                    o.rearrange("p (b k) -> p b k", k=16),
                    a.rearrange("p (b k) -> p b k", k=16),
                    s.broadcast_to((P, k, 16)), ALU.mult),
                    waits=[(sV, st["v_sq"])])
                R.op("vector", lambda e, o=dst_h, i=yq: e.tensor_copy(o, i))
                R.op("vector", lambda e, o=dst_l, a=yq, b=dst_h: e.tensor_tensor(
                    o, a, b, ALU.subtract), incs=[(sV, 1)], fence=True)
                qprev[key] = (sV, sV.n)
                return qprev[key]
            R.op("vector", lambda e, o=ub[1], a=sb7[6], b=ub[1]:
                 e.scalar_tensor_tensor(o, a, 2.0, b, ALU.mult, ALU.add),
                 waits=[(sS, s_all), (sP, p_tree)], prepad=True)
            R.op("vector", lambda e, o=ub[0], a=ub[1], b=ub[0]:
                 e.scalar_tensor_tensor(o, a, 2.0, b, ALU.mult, ALU.add))
            R.op("vector", lambda e, o=ub[1], i=ub[0]: e.tensor_scalar(
                o, i, 0.25, 3.0, ALU.mult, ALU.add))
            R.op("vector", lambda e, o=ub[0], a=ub[1], b=sgb: e.tensor_tensor(
                o, a, b, ALU.mult), incs=[(sV, 1)], fence=True)
            v_g = sV.n
            R.op("gpsimd", lambda e, o=yq, a=ub[0], s=sq, k=nb: e.tensor_tensor(
                o.rearrange("p (b k) -> p b k", k=16),
                a.rearrange("p (b k) -> p b k", k=16),
                s.broadcast_to((P, k, 16)), ALU.mult),
                waits=[(sV, v_g), (sV, st["v_sq"])])
            R.op("gpsimd", lambda e, o=dst_h, i=yq: e.tensor_copy(o, i))
            R.op("gpsimd", lambda e, o=dst_l, a=yq, b=dst_h: e.tensor_tensor(
                o, a, b, ALU.subtract), incs=[(sP, 1)], fence=True)
            qprev[key] = (sP, sP.n)
            return qprev[key]
        R.op("vector", lambda e, o=ub[0], a=sb7[0], b=sb7[1]: e.tensor_tensor(
            o, a, b, ALU.add), waits=[(sS, s_ab4)] + list(waits2))
        R.op("vector", lambda e, o=ub[1], a=sb7[2], b=sb7[3]: e.tensor_tensor(
            o, a, b, ALU.add))
        R.op("vector", lambda e, o=ub[0], a=ub[0], b=ub[1]: e.tensor_tensor(
            o, a, b, ALU.add))
        R.op("vector", lambda e, o=ub[1], a=sb7[4], b=sb7[5]: e.tensor_tensor(
            o, a, b, ALU.add), waits=[(sS, s_all)])
        R.op("vector", lambda e, o=ub[1], a=sb7[6], b=ub[1]:
             e.scalar_tensor_tensor(o, a, 2.0, b, ALU.mult, ALU.add))
        R.op("vector", lambda e, o=ub[0], a=ub[1], b=ub[0]:
             e.scalar_tensor_tensor(o, a, 2.0, b, ALU.mult, ALU.add))
        R.op("vector", lambda e, o=ub[1], i=ub[0]: e.tensor_scalar(
            o, i, 0.25, 3.0, ALU.mult, ALU.add))
        R.op("vector", lambda e, o=ub[0], a=ub[1], b=sgb: e.tensor_tensor(
            o, a, b, ALU.mult))
        R.op("vector", lambda e, o=yq, a=ub[0], s=sq, k=nb: e.tensor_tensor(
            o.rearrange("p (b k) -> p b k", k=16),
            a.rearrange("p (b k) -> p b k", k=16),
            s.broadcast_to((P, k, 16)), ALU.mult), waits=[(sV, st["v_sq"])])
        R.op("vector", lambda e, o=dst_h, i=yq: e.tensor_copy(o, i))
        R.op("vector", lambda e, o=dst_l, a=yq, b=dst_h: e.tensor_tensor(
            o, a, b, ALU.subtract), incs=[(sV, 1)], fence=True)
        qprev[key] = (sV, sV.n)
        return qprev[key]

    def quant_chain(pool, key, src, width, idx, dst_h, dst_l, waits_v,
                    mode=False):
        st = quant_stage1(pool, key, src, width, idx, waits_v)
        return quant_stage2(pool, key, st, dst_h, dst_l, use_pool=mode)

    scPREP = []
    PW = 1024
    pstg = [scope_alloc(scPREP, f"pstg{j}", [P, PW], F32) for j in range(3)]
    psp_h = scope_alloc(scPREP, "psp_h", [P, PW], BF16)
    psp_l = scope_alloc(scPREP, "psp_l", [P, PW], BF16)
    pwtr = [scope_alloc(scPREP, "pwtr0", [P, 8, P], BF16)] * 2
    qpP = {
        "yq": [scope_alloc(scPREP, f"yqP{j}", [P, PW], F32) for j in range(2)],
    }
    qpP["aa"] = qpP["yq"]
    qpP.update({
        "sb": [scope_alloc(scPREP, f"sbP{j}", [P, PW], BF16) for j in range(7)],
        "ub": [scope_alloc(scPREP, f"ubP{j}", [P, PW], BF16) for j in range(2)],
        "sg": [scope_alloc(scPREP, "sgP0", [P, PW], BF16)] * 2,
    })

    scA1 = []
    x2 = [scope_alloc(scA1, f"x2a{j}", [P, C], F32) for j in range(3)]
    xn3 = [scope_alloc(scA1, f"xn3_{j}", [P, C], F32) for j in range(3)]
    scrA1 = scope_alloc(scA1, "scrA1", [P, C], F32)
    wgh = scope_alloc(scA1, "wgh", [P, 8, C], BF16)
    wgl = scope_alloc(scA1, "wgl", [P, 8, C], BF16)
    BG = scope_alloc(scA1, "BG", [P, C], F32)
    qpA = {
        "yq": [b[:, 0:C] for b in qpP["yq"]],
        "aa": [b[:, 0:C] for b in qpP["aa"]],
        "sb": qpP["sb"],
        "ub": qpP["ub"],
        "sg": [scope_alloc(scA1, f"sgA{j}", [P, C], BF16) for j in range(2)],
    }
    xnh_b = [scope_alloc(scA1, f"xnhb{j}", [P, C], BF16) for j in range(2)]
    xnl_b = [scope_alloc(scA1, f"xnlb{j}", [P, C], BF16) for j in range(2)]
    xqh_b = [scope_alloc(scA1, f"xqhb{j}", [P, C], BF16) for j in range(2)]
    xql_b = [scope_alloc(scA1, f"xqlb{j}", [P, C], BF16) for j in range(2)]
    fmh = [scope_alloc(scA1, f"fmh{j}", [P, 8, P], BF16) for j in range(2)]
    fml = [scope_alloc(scA1, f"fml{j}", [P, 8, P], BF16) for j in range(2)]
    fqh = [scope_alloc(scA1, f"fqh{j}", [P, 8, P], BF16) for j in range(2)]
    fql = [scope_alloc(scA1, f"fql{j}", [P, 8, P], BF16) for j in range(2)]

    dma("sync", csts[:, :], csts_d[:, :], sLd)
    dma("sync", BG[:, :], bgt_d[:, :], sLd)
    for cch in range(8):
        w_stg = [R.at(f"wgsp_{cch - 1}")] if cch >= 1 else []
        dma("sync", pstg[cch % 2][:, 0:C], wg_d[cch * P:(cch + 1) * P, :], sLd,
            waits=w_stg)
        ldv = sLd.n
        R.op("vector", lambda e, o=wgh[:, cch, :], i=pstg[cch % 2][:, 0:C]:
             e.tensor_copy(o, i), waits=[(sLd, ldv)])
        R.op("vector", lambda e, o=wgl[:, cch, :], a=pstg[cch % 2][:, 0:C],
             b=wgh[:, cch, :]: e.tensor_tensor(o, a, b, ALU.subtract),
             incs=[(sV, 1)], fence=True)
        R.mark(f"wgsp_{cch}", sV)
    R.mark("wg_ready", sV)

    R.op("vector", lambda e, o=kvacc[:, :]: e.memset(o, 0.0), incs=[(sV, 1)])

    s_er_tab = {}
    a1banks = [pp[0], pp[1], psW[:, 0:512], psW[:, 512:1024]]

    ld_tab = {}

    def a1_xload(L):
        w_x = [R.at(f"a1_xfree_{L - 3}")] if L >= 3 else []
        dma("sync", x2[L % 3][:, :], x_d[L * P:(L + 1) * P, :], sLd, waits=w_x)
        ld_tab[L] = sLd.n

    def a1_tail(j, st):
        idx, xnt, s_xn, t_gs, gbt = st
        for nchv in range(2):
            bk = (2 * j + nchv) % 4
            w_gb = [(sT, t_gs[nchv])]
            if nchv == 0 and j >= 2:
                w_gb.append(R.at(f"a1_gatesfree_{j - 2}"))
            if nchv == 0:
                w_gb += R.wtap(f"tf_gates_{j - 2}")
            R.op("vector", lambda e, o=gbt[:, nchv * 512:(nchv + 1) * 512],
                 a=a1banks[bk][:, :], b=BG[:, nchv * 512:(nchv + 1) * 512]:
                 e.scalar_tensor_tensor(o, a, 1.0, b, ALU.mult, ALU.add),
                 waits=w_gb, incs=[(sV, 1)], prepad=True)
            bank_release(f"g{bk}", sV, sV.n)
        v_gb = sV.n
        R.op("scalar", lambda e, o=gbt, i=gbt: e.activation(o[:, :], i[:, :],
             AF.Sigmoid), waits=[(sV, v_gb)], incs=[(sS, 1)])
        s_gates = sS.n
        dma("gpsimd", gates_sp[j * P:(j + 1) * P, :], gbt[:, :], sSt,
            waits=[(sS, s_gates)])
        R.mark(f"a1_gatesfree_{j}", sSt)
        if taps and j < 5:
            dma("gpsimd", tap["t_xn"][j * P:(j + 1) * P, :], xnt[:, :], sSt,
                waits=[(sS, s_xn)])
            R.mark(f"tf_xn_{j}", sSt)
            dma("gpsimd", tap["t_gates"][j * P:(j + 1) * P, :], gbt[:, :], sSt,
                waits=[(sS, s_gates)])
            R.mark(f"tf_gates_{j}", sSt)

        w_qv = [(sS, s_xn)]
        if j >= 2:
            w_qv.append(R.at(f"a1_xqfree_{j - 2}"))
        w_qv += R.wtap(f"tf_xq_{j - 2}")
        rq_t = quant_chain(qpA, "A", xnt[:, :], C, idx, xqh_b[idx][:, :],
                           xql_b[idx][:, :], w_qv, mode="adds")
        R.mark(f"a1_xnfree_{j}", sV)
        w_fq = []
        if j >= 2:
            w_fq = [R.at(f"a1_fqfree_{j - 2}")]
        dma("sync", fqh[idx][:, :, :], xqh_b[idx][:, :], sTr,
            waits=[rq_t] + w_fq, transpose=True)
        dma("sync", fql[idx][:, :, :], xql_b[idx][:, :], sTr, transpose=True)
        tr_fq = sTr.n
        R.ck[f"a1_xqfree_{j}"] = (sTr, tr_fq)
        dma("gpsimd", xqh_sp[j, :, :, :], fqh[idx][:, :, :], sSt,
            waits=[(sTr, tr_fq)])
        dma("gpsimd", xql_sp[j, :, :, :], fql[idx][:, :, :], sSt)
        R.mark(f"a1_fqfree_{j}", sSt)
        if taps and j < 5:
            dma("gpsimd", tap["t_xqh"][j * P:(j + 1) * P, :], xqh_b[idx][:, :],
                sSt, waits=[rq_t])
            dma("gpsimd", tap["t_xql"][j * P:(j + 1) * P, :], xql_b[idx][:, :],
                sSt)
            R.mark(f"tf_xq_{j}", sSt)

    ln_nmr = {}

    def ln_front(j):
        idx = j % 2
        xt = x2[j % 3]
        w_st = [(sLd, ld_tab.pop(j))]
        if j >= 2:
            w_st.append(R.at(f"ln_free_{j - 2}"))
        R.op("scalar", lambda e, o=scrA1[:, :], i=xt, a=acc1[idx]: e.activation(
            o, i[:, :], AF.Copy, accum_out=a[:, :]), waits=w_st)
        R.op("scalar", lambda e, o=scrA1[:, :], i=xt, a=acc2[idx]: e.activation(
            o, i[:, :], AF.Square, accum_out=a[:, :]), incs=[(sS, 1)],
            fence=True)
        s_stats = sS.n
        R.op("vector", lambda e, o=tmean[idx], i=acc1[idx]: e.tensor_scalar(
            o[:, :], i[:, :], 1.0 / C, None, ALU.mult), waits=[(sS, s_stats)],
            incs=[(sV, 1)], pad=True, prepad=True)
        R.op("vector", lambda e, o=ln_b[idx], a=tmean[idx], b=tmean[idx]:
             e.tensor_tensor(o[:, :], a[:, :], b[:, :], ALU.mult),
             waits=[(sV, sV.n)], incs=[(sV, 1)], pad=True)
        R.op("vector", lambda e, o=ln_a[idx], i=acc2[idx]: e.tensor_scalar(
            o[:, :], i[:, :], 1.0 / C, None, ALU.mult), waits=[(sV, sV.n)],
            incs=[(sV, 1)], pad=True)
        R.op("vector", lambda e, o=tvar[idx], a=ln_a[idx], b=ln_b[idx]:
             e.tensor_tensor(o[:, :], a[:, :], b[:, :], ALU.subtract),
             waits=[(sV, sV.n)], incs=[(sV, 1)], pad=True)
        R.op("vector", lambda e, o=tvar[idx], i=tvar[idx]: e.tensor_scalar(
            o[:, :], i[:, :], 1e-5, None, ALU.add), waits=[(sV, sV.n)],
            incs=[(sV, 1)], pad=True)
        v_var = sV.n
        if taps and j < 5:
            dma("gpsimd", tap[f"t_dbg{j}"][:, :], acc2[idx][:, :], sSt,
                waits=[(sS, s_stats)])
            dma("gpsimd", tap[f"t_dbg{5 + j}"][:, :], tvar[idx][:, :], sSt,
                waits=[(sV, v_var)])
        R.op("scalar", lambda e, o=ln_a[idx], i=tvar[idx]: e.activation(
            o[:, :], i[:, :], AF.Sqrt), waits=[(sV, v_var)], incs=[(sS, 1)])
        s_rstd = sS.n
        R.op("vector", lambda e, o=ln_b[idx], i=ln_a[idx]: e.reciprocal(
            o[:, :], i[:, :]), waits=[(sS, s_rstd)], incs=[(sV, 1)])
        R.op("vector", lambda e, o=ln_b[idx], a=tvar[idx], b=ln_b[idx]:
             e.tensor_tensor(o[:, :], a[:, :], b[:, :], ALU.mult),
             waits=[(sV, sV.n)], incs=[(sV, 1)], pad=True)
        R.op("vector", lambda e, o=ln_b[idx], a=ln_a[idx], b=ln_b[idx]:
             e.tensor_tensor(o[:, :], a[:, :], b[:, :], ALU.add),
             waits=[(sV, sV.n)], incs=[(sV, 1)], pad=True)
        R.op("vector", lambda e, o=ln_b[idx], i=ln_b[idx]: e.tensor_scalar(
            o[:, :], i[:, :], 0.5, None, ALU.mult), waits=[(sV, sV.n)],
            incs=[(sV, 1)], pad=True)
        R.op("vector", lambda e, o=trstd[idx], i=ln_b[idx]: e.reciprocal(
            o[:, :], i[:, :]), waits=[(sV, sV.n)], incs=[(sV, 1)])
        R.op("vector", lambda e, o=tnmr[idx], a=tmean[idx], b=trstd[idx]:
             e.scalar_tensor_tensor(o[:, :], a[:, :], -1.0, b[:, :],
                                    ALU.mult, ALU.mult), waits=[(sV, sV.n)],
             incs=[(sV, 1)], pad=True)
        ln_nmr[j] = sV.n
        if taps and j < 5:
            dma("gpsimd", tap[f"t_dbg{10 + j}"][:, :], trstd[idx][:, :], sSt,
                waits=[(sV, ln_nmr[j])])
        R.mark(f"ln_free_{j}", sV)

    a1_xload(0)
    a1_xload(1)
    ln_front(0)
    a1_prev = None
    for i in range(NTIL):
        idx = i % 2
        xt = x2[i % 3]
        xnt = xn3[i % 3]
        gbt = g2[idx]

        if i + 2 < NTIL:
            a1_xload(i + 2)

        w_xn = [(sV, ln_nmr.pop(i))]
        if i >= 3:
            w_xn.append(R.at(f"a1_xnfree_{i - 3}"))
        w_xn += R.wtap(f"tf_xn_{i - 3}")
        R.op("scalar", lambda e, o=xnt, i=xt, sc=trstd[idx], b=tnmr[idx]:
             e.activation(o[:, :], i[:, :], AF.Identity, bias=b[:, :], scale=sc[:, :]),
             waits=w_xn, incs=[(sS, 1)])
        s_xn = sS.n
        R.mark(f"a1_xfree_{i}", sS)

        w_sp = [R.at(f"a1_xnhfree_{i - 2}")] if i >= 2 else []
        R.op("vector", lambda e, o=xnh_b[idx], i=xnt: e.tensor_copy(o[:, :], i[:, :]),
             waits=[(sS, s_xn)] + w_sp)
        R.op("vector", lambda e, o=xnl_b[idx], a=xnt, b=xnh_b[idx]: e.tensor_tensor(
            o[:, :], a[:, :], b[:, :], ALU.subtract), incs=[(sV, 1)], fence=True)
        v_split = sV.n
        w_fm = []
        if i >= 2:
            w_fm = [R.at(f"a1_fmfreeT_{i - 2}"), R.at(f"a1_fmfreeS_{i - 2}")]
        dma("sync", fmh[idx][:, :, :], xnh_b[idx][:, :], sTr,
            waits=[(sV, v_split)] + w_fm, transpose=True)
        dma("sync", fml[idx][:, :, :], xnl_b[idx][:, :], sTr, transpose=True)
        tr_fm = sTr.n
        R.ck[f"a1_xnhfree_{i}"] = (sTr, tr_fm)
        dma("gpsimd", xnh_sp[i, :, :, :], fmh[idx][:, :, :], sSt, waits=[(sTr, tr_fm)])
        dma("gpsimd", xnl_sp[i, :, :, :], fml[idx][:, :, :], sSt)
        R.mark(f"a1_fmfreeS_{i}", sSt)

        t_gs = []
        for nchv in range(2):
            bk = (2 * i + nchv) % 4
            t_g = proj3(a1banks[bk], f"g{bk}", fmh[idx], fml[idx], wgh, wgl, 512,
                        np.s_[nchv * 512:(nchv + 1) * 512],
                        [R.at("wg_ready"), (sTr, tr_fm)])
            t_gs.append(t_g)
        R.mark(f"a1_fmfreeT_{i}", sT)

        if i + 1 < NTIL:
            ln_front(i + 1)

        st_here = (idx, xnt, s_xn, t_gs, gbt)
        if a1_prev is not None:
            a1_tail(i - 1, a1_prev)
        a1_prev = st_here

    a1_tail(NTIL - 1, a1_prev)
    R.bankprev[0] = R.bankprev["g0"]
    R.bankprev[1] = R.bankprev["g1"]

    R.mark("a1_T_done", sT)
    R.mark("a1_V_done", sV)
    R.mark("a1_S_done", sS)
    a1_spills = sSt.n

    prep_descs = []
    for w_d_ap, nchunks, kwidth, sp_h, sp_l in (
            (wgate_d, 22, C, wgh_sp, wgl_sp),
            (wup_d, 22, C, wuh_sp, wul_sp),
            (wdown_d, 8, I, wdh_sp, wdl_sp)):
        for j in range(nchunks):
            for u in range((kwidth + PW - 1) // PW):
                k0 = u * PW
                k1 = min(kwidth, k0 + PW)
                prep_descs.append((w_d_ap, j, k0, k1, k0 // P, sp_h, sp_l))

    prep_state = {"clock": 0, "ld": {}, "st": {}}

    def prep_load(k):
        w_d_ap, j, k0, k1, kofs, sp_h, sp_l = prep_descs[k]
        pidx = k % 3
        wseg = k1 - k0
        w_stage = [R.at(f"prep_stg_free_{pidx}")] if f"prep_stg_free_{pidx}" in R.ck else []
        dma("sync", pstg[pidx][:, 0:wseg], w_d_ap[j * P:(j + 1) * P, k0:k1], sLd,
            waits=w_stage)
        prep_state["ld"][k] = sLd.n

    def prep_stage1(k):
        w_d_ap, j, k0, k1, kofs, sp_h, sp_l = prep_descs[k]
        pidx = k % 3
        wseg = k1 - k0
        st = quant_stage1(qpP, "P", pstg[pidx][:, 0:wseg], wseg, k % 2,
                          [(sLd, prep_state["ld"].pop(k))])
        R.ck[f"prep_stg_free_{pidx}"] = (sV, sV.n)
        prep_state["st"][k] = st

    def prep_stage2(k):
        w_d_ap, j, k0, k1, kofs, sp_h, sp_l = prep_descs[k]
        st = prep_state["st"].pop(k)
        wseg = st["width"]
        ksub = wseg // P
        w2 = [R.at("prep_sp_free")] if "prep_sp_free" in R.ck else []
        rq_t = quant_stage2(qpP, "P", st, psp_h[:, 0:wseg], psp_l[:, 0:wseg],
                            waits2=w2, use_pool=False)
        w_tr = [R.at("prep_wtr_free_1")] if "prep_wtr_free_1" in R.ck else []
        dma("sync", pwtr[0][:, 0:ksub, :], psp_h[:, 0:wseg], sTr,
            waits=[rq_t] + w_tr, transpose=True)
        tr_h = sTr.n
        dma("gpsimd", sp_h[:, kofs:kofs + ksub, j * P:(j + 1) * P],
            pwtr[0][:, 0:ksub, :], sSt, waits=[(sTr, tr_h)])
        R.mark("prep_wtr_free_0", sSt)
        dma("sync", pwtr[1][:, 0:ksub, :], psp_l[:, 0:wseg], sTr,
            waits=[R.at("prep_wtr_free_0")], transpose=True)
        tr_l = sTr.n
        R.ck["prep_sp_free"] = (sTr, tr_l)
        dma("gpsimd", sp_l[:, kofs:kofs + ksub, j * P:(j + 1) * P],
            pwtr[1][:, 0:ksub, :], sSt, waits=[(sTr, tr_l)])
        R.mark("prep_wtr_free_1", sSt)

    NPREP = len(prep_descs)
    prep_pt = {"n": 0}

    def prep_pump():
        prep_pt["n"] += 1
        want = (prep_pt["n"] * (NPREP + 2)) // (4 * NTIL)
        while prep_state["clock"] < want:
            prep_advance()

    def prep_advance():
        k = prep_state["clock"]
        if k < NPREP:
            prep_load(k)
        if 1 <= k < NPREP + 1:
            prep_stage1(k - 1)
        if 2 <= k < NPREP + 2:
            prep_stage2(k - 2)
        prep_state["clock"] += 1

    scope_exit(scA1)

    scA2 = []
    wqh = scope_alloc(scA2, "wqh", [P, 8, 3 * C], BF16)
    wql = scope_alloc(scA2, "wql", [P, 8, 3 * C], BF16)
    qstg = [scope_alloc(scA2, f"qstg{j}", [P, 512], F32) for j in range(2)]
    lfh = [scope_alloc(scA2, f"lfh{j}", [P, 8, P], BF16) for j in range(2)]
    lfl = [scope_alloc(scA2, f"lfl{j}", [P, 8, P], BF16) for j in range(2)]
    zb = [scope_alloc(scA2, "zb0", [P, 512], F32)] * 2
    ebz = [scope_alloc(scA2, "ebz0", [P, 512], F32)] * 2
    k1b = [scope_alloc(scA2, f"k1b{j}", [P, C], F32) for j in range(2)]
    vxb = [scope_alloc(scA2, "vxb0", [P, H, 65], F32)] * 2

    a1_gate = [R.at("a1_T_done"), R.at("a1_V_done"), R.at("a1_S_done"),
               (sSt, a1_spills), (sTr, sTr.n)]

    def wq_piece(k8):
        hf, cch = divmod(k8, 8)
        cs = np.s_[hf * 512:(hf + 1) * 512]
        qsg = qstg[k8 % 2]
        w_stg = [R.at(f"wqsp_{k8 - 2}")] if k8 >= 2 else list(a1_gate)
        dma("sync", qsg[:, :], wqkv_d[cch * P:(cch + 1) * P, cs], sLd,
            waits=w_stg)
        ldv = sLd.n
        R.op("vector", lambda e, o=wqh[:, cch, cs], i=qsg[:, :]:
             e.tensor_copy(o, i), waits=[(sLd, ldv)])
        R.op("vector", lambda e, o=wql[:, cch, cs], a=qsg[:, :],
             b=wqh[:, cch, cs]: e.tensor_tensor(o, a, b, ALU.subtract),
             incs=[(sV, 1)], fence=True)
        R.mark(f"wqsp_{k8}", sV)
        if cch == 7:
            R.mark(f"wq_col_{hf}", sV)
        if k8 == 47:
            R.mark("wq_ready", sV)
            R.op("vector", lambda e, o=vxb[0][:, :, 64:65]: e.memset(o, 1.0),
                 incs=[(sV, 1)], fence=True)
            R.mark("vones", sV)

    a2_ldt = {}

    def a2_loads(i):
        idx = i % 2
        w_g = [R.at(f"a1_gatesfree_{i}")]
        if i >= 2:
            w_g.append(R.at(f"a2_gatesbuf_free_{i - 2}"))
        else:
            w_g.append(R.at(f"a1_gatesfree_{NTIL - 2 + i}"))
        dma("sync", g2[idx][:, :], gates_sp[i * P:(i + 1) * P, :], sLd, waits=w_g)
        w_l = [R.at(f"a2_lffree_{i - 2}")] if i >= 2 else \
            list(a1_gate) + [R.at(f"a1_fqfree_{i}")]
        dma("sync", lfh[idx][:, :, :], xnh_sp[i, :, :, :], sLd, waits=w_l)
        dma("sync", lfl[idx][:, :, :], xnl_sp[i, :, :, :], sLd)
        a2_ldt[i] = sLd.n

    a2_loads(0)
    for k8 in range(16):
        wq_piece(k8)
    for i in range(NTIL):
        idx = i % 2
        gt = g2[idx]
        q1t = f2[idx]
        ld_i = a2_ldt.pop(i)
        if i + 1 < NTIL:
            a2_loads(i + 1)

        for nch in range(6):
            if i == 0 and nch >= 1:
                for k8 in range(8 + 8 * nch, min(48, 16 + 8 * nch)):
                    wq_piece(k8)
            bank = pp[nch % 2]
            wq_gate = R.at(f"wq_col_{nch}") if i == 0 else R.at("wq_ready")
            t_mm = proj3(bank, nch % 2, lfh[idx], lfl[idx], wqh, wql, 512,
                         np.s_[nch * 512:(nch + 1) * 512],
                         [wq_gate, (sLd, ld_i)])
            if nch < 4:
                dsts = q1t if nch < 2 else k1b[idx]
                dcol = (nch % 2) * 512
                w_z = [(sT, t_mm)]
                prev_er = s_er_tab.get((i, nch - 1), s_er_tab.get((i - 1, 3)))
                if prev_er is not None and nch <= 3:
                    w_z.append(prev_er)
                if nch < 2 and i >= 2:
                    w_z.append(R.at(f"a2_q1buf_free_{i - 2}"))
                if nch >= 2 and i >= 2:
                    w_z.append(R.at(f"a2_einsum_{i - 2}"))
                    w_z += R.wtap(f"tf_k1_{i - 2}")
                R.op("vector", lambda e, o=zb[nch % 2][:, :], a=bank[:, :],
                     b=gt[:, dcol:dcol + 512]:
                     e.tensor_tensor(o, a, b, ALU.mult), waits=w_z, incs=[(sV, 1)],
                     prepad=True)
                bank_release(nch % 2, sV, sV.n)
                v_z = sV.n
                R.op("vector", lambda e, o=ebz[nch % 2][:, :], i=zb[nch % 2][:, :]:
                     e.tensor_scalar(o, i, 0.0, None, ALU.min), incs=[(sV, 1)])
                v_zm = sV.n
                R.op("scalar", lambda e, o=ebz[nch % 2][:, :], i=ebz[nch % 2][:, :]:
                     e.activation(o, i, AF.Exp), waits=[(sV, v_zm)])
                R.op("scalar", lambda e, o=zb[nch % 2][:, :], i=zb[nch % 2][:, :]:
                     e.activation(o, i, AF.Relu), incs=[(sS, 1)])
                s_er = sS.n
                s_er_tab[(i, nch)] = (sS, s_er)
                R.op("vector", lambda e, o=dsts[:, dcol:dcol + 512],
                     a=ebz[nch % 2][:, :], b=zb[nch % 2][:, :]:
                     e.tensor_tensor(o, a, b, ALU.add), waits=[(sS, s_er)],
                     incs=[(sV, 1)], fence=(nch >= 2))
                if nch == 1:
                    R.mark(f"a2_q1done_{i}", sV)
                if nch == 3:
                    R.mark(f"a2_k1done_{i}", sV)
                    R.mark(f"a2_gatesbuf_free_{i}", sV)
                if nch in (1, 3):
                    prep_pump()
            else:
                h0 = (nch - 4) * 8
                w_v = [(sT, t_mm)]
                if nch == 4 and i >= 1:
                    w_v.append(R.at(f"a2_vfree_{i - 1}"))
                R.op("scalar", lambda e, o=vxb[idx][:, h0:h0 + 8, 0:64], a=bank[:, :]:
                     e.activation(o, a.rearrange("p (h d) -> p h d", d=64), AF.Copy),
                     waits=w_v, incs=[(sS, 1)], prepad=True, fence=(nch == 5))
                bank_release(nch % 2, sS, sS.n)
                if nch == 5:
                    R.mark(f"a2_vdone_{i}", sS)

        prep_pump()

        dma("gpsimd", q1_sp[i * P:(i + 1) * P, :], q1t[:, :], sSt,
            waits=[R.at(f"a2_q1done_{i}")])
        R.mark(f"a2_q1buf_free_{i}", sSt)
        if taps and i < 5:
            dma("gpsimd", tap["t_q1"][i * P:(i + 1) * P, :], q1t[:, :], sSt)
            R.mark(f"a2_q1buf_free_{i}", sSt)
            dma("gpsimd", tap["t_k1"][i * P:(i + 1) * P, :], k1b[idx][:, :], sSt,
                waits=[R.at(f"a2_k1done_{i}")])
            R.mark(f"tf_k1_{i}", sSt)

        w_kv = [R.at(f"a2_k1done_{i}"), R.at(f"a2_vdone_{i}"), R.at("vones")]
        if i >= 1:
            w_kv.append(R.at(f"a2_kvadd_{i - 1}"))
        for h in range(H):
            R.op("tensor", lambda e, o=ps_kv[0:65, h * 64:(h + 1) * 64],
                 l=vxb[idx][:, h, :], r=k1b[idx][:, h * 64:(h + 1) * 64]:
                 e.matmul(o, l, r, start=True, stop=True),
                 waits=w_kv if h == 0 else [],
                 incs=[(sT, 1)] if h == H - 1 else [])
        t_ein = sT.n
        R.mark(f"a2_einsum_{i}", sT)
        R.mark(f"a2_vfree_{i}", sT)
        R.mark(f"a2_lffree_{i}", sT)
        R.op("vector", lambda e, o=kvacc[:, :], a=ps_kv[0:65, :], b=kvacc[:, :]:
             e.scalar_tensor_tensor(o, a, 1.0, b, ALU.mult, ALU.add),
             waits=[(sT, t_ein)], incs=[(sV, 1)], prepad=True, fence=True)
        R.mark(f"a2_kvadd_{i}", sV)

        prep_pump()

    while prep_state["clock"] < NPREP + 2:
        prep_advance()
    R.mark("wprep_done", sSt)
    R.mark("prep_tr_done", sTr)
    R.mark("a2_S_done", sS)
    R.mark("a2_T_done", sT)
    R.mark("a2_kv_done", sV)
    scope_exit(scA2)
    scope_exit(scPREP)

    def post_a2_gate():
        return [R.at("a2_T_done"), R.at("a2_kv_done"), R.at("wprep_done"),
                R.at("prep_tr_done"), R.at("a2_S_done")]

    scCW = []
    x2 = [scope_alloc(scCW, f"x2c{j}", [P, C], F32) for j in range(2)]
    e2 = [scope_alloc(scCW, f"e2_{j}", [P, C], F32) for j in range(4)]
    cgh = scope_alloc(scCW, "cgh", [P, 8, 1024], BF16)
    cgl = scope_alloc(scCW, "cgl", [P, 8, 1024], BF16)
    cuh = scope_alloc(scCW, "cuh", [P, 8, 1024], BF16)
    cul = scope_alloc(scCW, "cul", [P, 8, 1024], BF16)
    cdh = scope_alloc(scCW, "cdh", [P, 8, C], BF16)
    cdl = scope_alloc(scCW, "cdl", [P, 8, C], BF16)

    scIL = []
    kv_all = scope_alloc(scIL, "kv_all", [65, C], F32)
    KSb = scope_alloc(scIL, "KSb", [P, C], F32)
    w2h = scope_alloc(scIL, "w2h", [P, 8, C], BF16)
    w2l = scope_alloc(scIL, "w2l", [P, 8, C], BF16)
    scILb = []
    wpst = [scope_alloc(scILb, f"wpst{j}", [64, C], F32) for j in range(4)]

    if taps:
        dma("gpsimd", tap["t_kvpart"][:, :], kvacc[:, :], sSt,
            waits=[R.at("a2_kv_done")])
    dma("gpsimd", cc_in[:, :], kvacc[:, :], sSt, waits=[R.at("a2_kv_done")])
    st_cc = sSt.n
    R.op("gpsimd", lambda e: e.collective_compute(
        "AllReduce", ALU.add, replica_groups=[[0, 1, 2, 3], [4, 5, 6, 7]],
        ins=[cc_in[:, :]], outs=[cc_out[:, :]]),
        waits=[(sSt, st_cc)], incs=[(sCC, 1)])
    dma("gpsimd", kv_all[:, :], cc_out[:, :], sSt,
        waits=[(sCC, 1)] + post_a2_gate())
    st_kv = sSt.n
    if taps:
        dma("gpsimd", tap["t_kvall"][:, :], kv_all[:, :], sSt, waits=[(sSt, st_kv)])

    dma("sync", KSb[0:1, :], kv_all[64:65, :], sTr,
        waits=[(sSt, st_kv)] + post_a2_gate())
    kk = 1
    while kk < P:
        step = min(kk, P - kk)
        dma("sync", KSb[kk:kk + step, :], KSb[0:step, :], sTr,
            waits=[(sTr, sTr.n)])
        kk += step
    tr_ks = sTr.n
    if taps:
        dma("gpsimd", tap["t_ksb"][:, :], KSb[:, :], sSt, waits=[(sTr, tr_ks)])

    w2_ldt, w2_mmt = {}, {}

    def w2_load(h):
        w_wp = []
        if h == 0:
            w_wp.append((sSt, st_kv))
            w_wp.extend(post_a2_gate())
        if h >= 4:
            w_wp.append((sT, w2_mmt[h - 4]))
        dma("sync", wpst[h % 4][:, :], wp_d[h * 64:(h + 1) * 64, :], sLd,
            waits=w_wp)
        w2_ldt[h] = sLd.n

    for h in range(4):
        w2_load(h)
    for pr in range(8):
        for hh in range(2):
            h = 2 * pr + hh
            for nch2 in range(2):
                w_mm = [(sLd, w2_ldt.pop(h))] if nch2 == 0 else []
                if pr >= 1 and hh == 0 and nch2 == 0:
                    w_mm.append(R.at(f"w2evac_{pr - 1}"))
                R.op("tensor", lambda e,
                     o=psW[hh * 64:hh * 64 + 64, nch2 * 512:(nch2 + 1) * 512],
                     l=kv_all[0:64, h * 64:(h + 1) * 64],
                     r=wpst[h % 4][:, nch2 * 512:(nch2 + 1) * 512]:
                     e.matmul(o, l, r, start=True, stop=True),
                     waits=w_mm,
                     incs=[(sT, 1)] if nch2 == 1 else [])
            w2_mmt[h] = sT.n
            if h + 4 < 16:
                w2_load(h + 4)
        t_w2 = sT.n
        R.op("vector", lambda e, o=w2h[:, pr, :], i=psW[:, :]: e.tensor_copy(o, i),
             waits=[(sT, t_w2)], prepad=True)
        R.op("vector", lambda e, o=w2l[:, pr, :], a=psW[:, :], b=w2h[:, pr, :]:
             e.tensor_tensor(o, a, b, ALU.subtract), incs=[(sV, 1)], fence=True)
        R.mark(f"w2evac_{pr}", sV)
        if taps and pr == 0:
            dma("gpsimd", tap["t_w2h"][:, :], w2h[:, 0, :], sSt,
                waits=[R.at("w2evac_0")])
            dma("gpsimd", tap["t_w2l"][:, :], w2l[:, 0, :], sSt)
    R.mark("w2_ready", sV)
    R.bankprev["c2"] = R.at("w2_ready")
    R.bankprev["c3"] = R.at("w2_ready")
    scope_exit(scILb)

    scB = []
    BP_t = scope_alloc(scB, "BPt", [P, C], F32)
    nmb = scope_alloc(scB, "nmb", [P, C], F32)
    n16 = [scope_alloc(scB, f"n16_{j}", [P, H], F32) for j in range(2)]
    n16r = [scope_alloc(scB, f"n16r_{j}", [P, H], F32) for j in range(2)]
    qnh_b = [scope_alloc(scB, f"qnh{j}", [P, C], BF16) for j in range(2)]
    qnl_b = [scope_alloc(scB, f"qnl{j}", [P, C], BF16) for j in range(2)]
    bfh = [scope_alloc(scB, f"bfh{j}", [P, 8, P], BF16) for j in range(2)]
    bfl = [scope_alloc(scB, f"bfl{j}", [P, 8, P], BF16) for j in range(2)]

    dma("sync", BP_t[:, :], bpt_d[:, :], sLd, waits=post_a2_gate())
    ld_bp = sLd.n

    def load_cw(third, waits):
        n0, n1 = N3[third]
        w = list(waits)
        last = 0
        for buf, spl in ((cgh, wgh_sp), (cgl, wgl_sp), (cuh, wuh_sp), (cul, wul_sp)):
            for kc in range(8):
                dma("sync", buf[:, kc, 0:n1 - n0], spl[:, kc, n0:n1], sLd, waits=w)
                w = []
        for buf, spl in ((cdh, wdh_sp), (cdl, wdl_sp)):
            for j in range((n1 - n0) // P):
                kc = n0 // P + j
                dma("sync", buf[:, j, :], spl[:, kc, :], sLd)
        return sLd.n


    b_ldt = {}
    bbanks = [pp[0], pp[1], psD[0], psD[1]]
    R.bankprev["b0"] = R.bankprev.pop(0)
    R.bankprev["b1"] = R.bankprev.pop(1)

    def b_loads(i):
        if i >= 2:
            w_q1 = [R.at(f"b_q1free_{i - 2}")]
        else:
            w_q1 = post_a2_gate()
        dma("sync", f2[i % 2][:, :], q1_sp[i * P:(i + 1) * P, :], sLd, waits=w_q1)
        w_x = [(sLd, ld_bp)]
        if i >= 2:
            w_x.append(R.at(f"b_xfree_{i - 2}"))
        dma("sync", x2[i % 2][:, :], x_d[i * P:(i + 1) * P, :], sLd, waits=w_x)
        b_ldt[i] = sLd.n

    def b_tail(j, st):
        xt, pout, t_bs = st
        for nch in range(2):
            bk = (2 * j + nch) % 4
            w_ev = [(sT, t_bs[nch])]
            if nch == 0 and j >= 2:
                w_ev.append(R.at(f"b_poutfree_{j - 2}"))
            R.op("vector", lambda e, o=pout[:, nch * 512:(nch + 1) * 512],
                 a=bbanks[bk][:, :], b=xt[:, nch * 512:(nch + 1) * 512]:
                 e.scalar_tensor_tensor(o, a, 1.0, b, ALU.mult, ALU.add),
                 waits=w_ev, incs=[(sV, 1)], prepad=True)
            bank_release(f"b{bk}", sV, sV.n)
        v_pout = sV.n
        R.mark(f"b_xfree_{j}", sV)
        dma("gpsimd", part_sp[j * P:(j + 1) * P, :], pout[:, :], sSt,
            waits=[(sV, v_pout)])
        R.mark(f"b_poutfree_{j}", sSt)
        if taps and j < 2:
            dma("gpsimd", tap["t_part"][j * P:(j + 1) * P, :], pout[:, :], sSt)
            R.mark(f"b_poutfree_{j}", sSt)

    b_loads(0)
    b_loads(1)
    ld_cw0 = load_cw(0, post_a2_gate())
    if taps:
        dma("gpsimd", tap["t_cgh"][:, :, :], cgh[:, :, :], sSt, waits=[(sLd, ld_cw0)])
        dma("gpsimd", tap["t_cgl"][:, :, :], cgl[:, :, :], sSt)
        dma("gpsimd", tap["t_cdh"][:, :, :], cdh[:, :, :], sSt)
        dma("gpsimd", tap["t_cdl"][:, :, :], cdl[:, :, :], sSt)
    b_prev = None
    for i in range(NTIL):
        idx = i % 2
        q1t = f2[idx]
        xt = x2[i % 2]
        pout = e2[idx]
        ld_b = b_ldt.pop(i)

        w_nm = [(sLd, ld_b), (sTr, tr_ks)]
        if i == 1:
            w_nm += R.wtap("tf_qn")
        R.op("vector", lambda e, o=nmb[:, :], a=q1t, b=KSb: e.tensor_tensor(
            o, a[:, :], b[:, :], ALU.mult), waits=w_nm)
        R.op("vector", lambda e, o=n16[idx][:, :], i=nmb[:, :]: e.tensor_reduce(
            o, i.rearrange("p (h d) -> p h d", d=64), AX.X, ALU.add),
            incs=[(sV, 1)], pad=True)
        R.op("vector", lambda e, o=n16[idx][:, :], i=n16[idx][:, :]: e.tensor_scalar(
            o, i, 1e-8, None, ALU.add), waits=[(sV, sV.n)], incs=[(sV, 1)],
            pad=True)
        R.op("vector", lambda e, o=n16r[idx][:, :], i=n16[idx][:, :]: e.reciprocal(o, i),
             waits=[(sV, sV.n)], incs=[(sV, 1)])
        if taps and i == 0:
            dma("gpsimd", tap["t_norm"][:, :], n16r[idx][:, :], sSt,
                waits=[(sV, sV.n)])
        R.op("vector", lambda e, o=nmb, a=q1t, b=n16r[idx]: e.tensor_tensor(
            o[:, :].rearrange("p (h d) -> p h d", d=64),
            a[:, :].rearrange("p (h d) -> p h d", d=64),
            b[:, :].broadcast_to((P, H, 64)), ALU.mult), waits=[(sV, sV.n)],
            incs=[(sV, 1)])
        R.mark(f"b_q1free_{i}", sV)
        if taps and i == 0:
            dma("gpsimd", tap["t_qn"][:, :], nmb[:, :], sSt,
                waits=[R.at("b_q1free_0")])
            R.mark("tf_qn", sSt)
        w_qn = [R.at(f"b_qnfree_{i - 2}")] if i >= 2 else []
        R.op("vector", lambda e, o=qnh_b[idx], i=nmb: e.tensor_copy(o[:, :], i[:, :]),
             waits=w_qn)
        R.op("vector", lambda e, o=qnl_b[idx], a=nmb, b=qnh_b[idx]: e.tensor_tensor(
            o[:, :], a[:, :], b[:, :], ALU.subtract), incs=[(sV, 1)], fence=True)
        v_qn = sV.n
        w_bf = [R.at(f"b_bffree_{i - 2}")] if i >= 2 else []
        dma("sync", bfh[idx][:, :, :], qnh_b[idx][:, :], sTr,
            waits=[(sV, v_qn)] + w_bf, transpose=True)
        dma("sync", bfl[idx][:, :, :], qnl_b[idx][:, :], sTr, transpose=True)
        tr_b = sTr.n
        R.ck[f"b_qnfree_{i}"] = (sTr, tr_b)
        if taps and i == 0:
            dma("gpsimd", tap["t_qnh"][:, :], qnh_b[idx][:, :], sSt,
                waits=[(sV, v_qn)])
            dma("gpsimd", tap["t_qnl"][:, :], qnl_b[idx][:, :], sSt)
            dma("gpsimd", tap["t_bfh"][:, :, :], bfh[idx][:, :, :], sSt,
                waits=[(sTr, tr_b)])
            R.ck[f"b_qnfree_{i}"] = (sSt, sSt.n)

        w_px = [(sLd, ld_b)]
        R.op("vector", lambda e, o=xt, a=xt, b=BP_t: e.tensor_tensor(
            o[:, :], a[:, :], b[:, :], ALU.add), waits=w_px, incs=[(sV, 1)])

        t_bs = []
        for nch in range(2):
            bk = (2 * i + nch) % 4
            t_b = proj3(bbanks[bk], f"b{bk}", bfh[idx], bfl[idx], w2h, w2l, 512,
                        np.s_[nch * 512:(nch + 1) * 512],
                        [R.at("w2_ready"), (sTr, tr_b)])
            t_bs.append(t_b)
        R.mark(f"b_bffree_{i}", sT)

        st_here = (xt, pout, t_bs)
        if b_prev is not None:
            b_tail(i - 1, b_prev)
        b_prev = st_here
        if 2 <= i + 1 < NTIL:
            b_loads(i + 1)

    b_tail(NTIL - 1, b_prev)
    R.bankprev[0] = R.bankprev["b0"]
    R.bankprev[1] = R.bankprev["b1"]
    R.bankprev["D0_0"] = R.bankprev["b2"]
    R.bankprev["D1_0"] = R.bankprev["b3"]

    R.mark("b_T_done", sT)
    R.mark("b_V_done", sV)
    scope_exit(scB)
    scope_exit(scIL)

    scC = []
    qpC = {
        "yq": [scope_alloc(scC, f"yqC{j}", [P, 1024], F32) for j in range(2)],
        "sb": [scope_alloc(scC, f"sbC{j}", [P, 1024], BF16) for j in range(7)],
        "ub": [scope_alloc(scC, f"ubC{j}", [P, 1024], BF16) for j in range(2)],
        "sg": [scope_alloc(scC, "sgC0", [P, 1024], BF16)] * 2,
    }
    qpC["aa"] = qpC["yq"]
    xfh = [scope_alloc(scC, f"xfh{j}", [P, 8, P], BF16) for j in range(2)]
    xfl = [scope_alloc(scC, f"xfl{j}", [P, 8, P], BF16) for j in range(2)]
    hqh_b = [scope_alloc(scC, f"hqh{j}", [P, 1024], BF16) for j in range(2)]
    hql_b = [scope_alloc(scC, f"hql{j}", [P, 1024], BF16) for j in range(2)]
    hfh = [scope_alloc(scC, f"hfh{j}", [P, 8, P], BF16) for j in range(4)]
    hfl = [scope_alloc(scC, f"hfl{j}", [P, 8, P], BF16) for j in range(4)]

    R.bankprev["D0_1"] = R.at("a2_kv_done")
    R.bankprev["D1_1"] = R.at("a2_kv_done")
    R.bankprev["c0"] = R.bankprev.get(0)
    R.bankprev["c1"] = R.bankprev.get(1)
    if R.bankprev["c0"] is None:
        del R.bankprev["c0"]
    if R.bankprev["c1"] is None:
        del R.bankprev["c1"]

    def c_down(third, i, st_c):
        TN, kch, idx4, tr_h, pin, ld_pin = st_c
        dbanks = [psD[0], psD[1], ps_kv[:, 0:512], ps_kv[:, 512:1024]]
        for mch in range(2):
            bank = dbanks[mch + 2 * (i % 2)]
            triples = []
            for kc in range(kch):
                triples.append((bank[:, :], hfh[idx4][:, kc, :],
                                cdh[:, kc, mch * 512:(mch + 1) * 512]))
                triples.append((bank[:, :], hfh[idx4][:, kc, :],
                                cdl[:, kc, mch * 512:(mch + 1) * 512]))
                triples.append((bank[:, :], hfl[idx4][:, kc, :],
                                cdh[:, kc, mch * 512:(mch + 1) * 512]))
            mm_group(bank, f"D{mch}_{i % 2}", triples, [(sTr, tr_h)])
            t_d = sT.n
            R.op("vector", lambda e, o=pin[:, mch * 512:(mch + 1) * 512],
                 a=bank[:, :], b=pin[:, mch * 512:(mch + 1) * 512]:
                 e.scalar_tensor_tensor(o, a, 1.0, b, ALU.mult, ALU.add),
                 waits=[(sT, t_d), (sLd, ld_pin)], incs=[(sV, 1)], prepad=True)
            bank_release(f"D{mch}_{i % 2}", sV, sV.n)
        v_p = sV.n
        R.mark(f"c_hffree_{third}_{i}", sT)
        if third < 2:
            dma("gpsimd", part_sp[i * P:(i + 1) * P, :], pin[:, :], sSt,
                waits=[(sV, v_p)])
        else:
            dma("gpsimd", y_d[i * P:(i + 1) * P, :], pin[:, :], sSt,
                waits=[(sV, v_p)])
        R.mark(f"c_pdone_{third}_{i}", sSt)
        R.mark(f"c_pinfree_{third}_{i}", sSt)

    for third in range(3):
        n0, n1 = N3[third]
        TN = n1 - n0
        kch = TN // P
        if third > 0:
            ld_cw = load_cw(third, [R.at(f"c_T_done_{third - 1}")])
        else:
            ld_cw = ld_cw0
        nck = [(j * 512, min(TN, (j + 1) * 512)) for j in range((TN + 511) // 512)]
        nnc = len(nck)
        c_pending = []
        ldt = {}

        def c_loads(i):
            idx = i % 2
            idx4 = i % 4
            if i >= 2:
                w_xf = [R.at(f"c_xffree_{third}_{i - 2}")]
            elif third > 0:
                w_xf = [R.at(f"c_xffree_{third - 1}_{NTIL - 2 + i}")]
            else:
                w_xf = [R.at("b_V_done"), R.at("b_T_done")]
            if i == 0:
                w_xf.append((sLd, ld_cw))
            dma("sync", xfh[idx][:, :, :], xqh_sp[i, :, :, :], sLd, waits=w_xf)
            dma("sync", xfl[idx][:, :, :], xql_sp[i, :, :, :], sLd)
            ld_c = sLd.n
            if third == 0:
                w_pin = [R.at(f"b_poutfree_{max(i, NTIL - 2 + (i % 2))}")]
            else:
                w_pin = [R.at(f"c_pdone_{third - 1}_{i}")]
            if i >= 4:
                w_pin.append(R.at(f"c_pinfree_{third}_{i - 4}"))
            elif third > 0:
                w_pin.append(R.at(f"c_pinfree_{third - 1}_{NTIL - 4 + i}"))
            dma("sync", e2[idx4][:, :], part_sp[i * P:(i + 1) * P, :], sLd,
                waits=w_pin)
            ldt[i] = (ld_c, sLd.n)

        c_loads(0)
        for i in range(NTIL):
            idx = i % 2
            idx4 = i % 4
            hbuf = g2[idx]
            sgt = x2[idx]
            upt = f2[idx]
            pin = e2[idx4]
            ld_c, ld_pin = ldt.pop(i)
            if i + 1 < NTIL:
                c_loads(i + 1)

            cbanks = [pp[0], pp[1], psW[:, 0:512], psW[:, 512:1024]]
            for gu, (wbh, wbl) in enumerate(((cgh, cgl), (cuh, cul))):
                dst = sgt if gu == 0 else upt
                for ncj, (c0, c1) in enumerate(nck):
                    seq = gu * nnc + ncj
                    bank = cbanks[seq % 4]
                    t_c = proj3(bank, f"c{seq % 4}", xfh[idx], xfl[idx], wbh, wbl,
                                c1 - c0, np.s_[c0:c1], [(sLd, ld_c)])
                    w_ev = [(sT, t_c)]
                    if ncj == 0:
                        if third == 0 and i < 2:
                            w_ev.append(R.at("b_V_done"))
                        elif i >= 2:
                            w_ev.append(R.at(f"c_sufree_{third}_{i - 2}"))
                        elif third > 0:
                            w_ev.append(R.at(f"c_sufree_{third - 1}_{NTIL - 2 + i}"))
                    R.op("scalar", lambda e, o=dst[:, c0:c1], i=bank[:, 0:c1 - c0],
                         f=(AF.Silu if gu == 0 else AF.Copy):
                         e.activation(o, i, f), waits=w_ev, incs=[(sS, 1)],
                         prepad=True)
                    bank_release(f"c{seq % 4}", sS, sS.n)
            s_gu = sS.n
            R.mark(f"c_xffree_{third}_{i}", sT)

            if len(c_pending) >= 2:
                c_down(third, i - 2, c_pending.pop(0))

            w_h = []
            if i >= 1:
                w_h = [R.at(f"c_hfree_{third}_{i - 1}")]
            elif third > 0:
                w_h = [R.at(f"c_hfree_{third - 1}_{NTIL - 1}")]
            if third == 0 and i == 1:
                w_h += R.wtap("tf_h")
            R.op("vector", lambda e, o=hbuf[:, 0:TN], a=sgt[:, 0:TN], b=upt[:, 0:TN]:
                 e.tensor_tensor(o, a, b, ALU.mult), waits=[(sS, s_gu)] + w_h,
                 incs=[(sV, 1)])
            R.mark(f"c_sufree_{third}_{i}", sV)
            v_h = sV.n
            if taps and third == 0 and i == 0:
                dma("gpsimd", tap["t_h"][:, 0:TN], hbuf[:, 0:TN], sSt,
                    waits=[(sV, v_h)])
                R.mark("tf_h", sSt)

            w_qv = [(sV, v_h)]
            if third == 0 and i < 2:
                w_qv.append(R.at("b_V_done"))
                w_qv.append(R.at("b_T_done"))
            rq_t = quant_chain(qpC, "C", hbuf[:, 0:TN], TN, idx,
                               hqh_b[idx][:, 0:TN], hql_b[idx][:, 0:TN], w_qv)
            R.mark(f"c_hfree_{third}_{i}", sV)

            w_hf = []
            if i >= 4:
                w_hf = [R.at(f"c_hffree_{third}_{i - 4}")]
            elif third > 0:
                w_hf = [R.at(f"c_hffree_{third - 1}_{NTIL - 4 + i}")]
            else:
                w_hf = [R.at("b_T_done")]
            dma("sync", hfh[idx4][:, 0:kch, :], hqh_b[idx][:, 0:TN], sTr,
                waits=[rq_t] + w_hf, transpose=True)
            dma("sync", hfl[idx4][:, 0:kch, :], hql_b[idx][:, 0:TN], sTr,
                transpose=True)
            tr_h = sTr.n
            c_pending.append((TN, kch, idx4, tr_h, pin, ld_pin))

        for joff, st_c in enumerate(c_pending):
            c_down(third, NTIL - len(c_pending) + joff, st_c)
        c_pending = []
        R.mark(f"c_T_done_{third}", sT)

    scope_exit(scC)
    scope_exit(scCW)

    final_st = sSt.n
    R.op("gpsimd", None, waits=[(sSt, final_st)])

    with ExitStack() as es:
        for s in R.sems:
            s.h = es.enter_context(nc.semaphore(s.name))
        with nc.Block() as block:
            @block.sync
            def _(e):
                R.replay(e, "sync")

            @block.scalar
            def _(e):
                R.replay(e, "scalar")

            @block.vector
            def _(e):
                R.replay(e, "vector")

            @block.tensor
            def _(e):
                R.replay(e, "tensor")

            @block.gpsimd
            def _(e):
                R.replay(e, "gpsimd")

    return nc


_NC_CACHE = {}


def _get_nc(taps=False):
    key = bool(taps)
    if key not in _NC_CACHE:
        _NC_CACHE[key] = build_program(taps=key)
    return _NC_CACHE[key]


def make_inputs(L0_attn_x, L0_attn_Wqkv, L0_attn_Wg, L0_attn_bg, L0_attn_Wp,
                L0_attn_bp, L0_ffn_w_gate, L0_ffn_w_up, L0_ffn_w_down):
    x = np.ascontiguousarray(np.asarray(L0_attn_x, dtype=np.float32))
    wqkv = np.ascontiguousarray(np.asarray(L0_attn_Wqkv, dtype=np.float32))
    wg = np.ascontiguousarray(np.asarray(L0_attn_Wg, dtype=np.float32))
    bg = np.asarray(L0_attn_bg, dtype=np.float32)
    wp = np.ascontiguousarray(np.asarray(L0_attn_Wp, dtype=np.float32))
    bp = np.asarray(L0_attn_bp, dtype=np.float32)
    wgt = np.ascontiguousarray(np.asarray(L0_ffn_w_gate, dtype=np.float32))
    wup = np.ascontiguousarray(np.asarray(L0_ffn_w_up, dtype=np.float32))
    wdn = np.ascontiguousarray(np.asarray(L0_ffn_w_down, dtype=np.float32))

    bgt = np.ascontiguousarray(np.tile(bg[None, :], (P, 1)))
    bpt = np.ascontiguousarray(np.tile(bp[None, :], (P, 1)))
    csts = np.zeros((P, 8), dtype=np.float32)
    csts[:, 0] = 1e-5
    for j, t in enumerate(THR):
        csts[:, 1 + j] = -t

    in_maps = []
    for c in range(8):
        b, qt = divmod(c, 4)
        xs = np.ascontiguousarray(x[b, qt * NT:(qt + 1) * NT, :])
        in_maps.append({
            "x": xs, "wqkv": wqkv, "wg": wg, "wp": wp,
            "wgate": wgt, "wup": wup, "wdown": wdn,
            "bgt": bgt, "bpt": bpt, "csts_d": csts,
        })
    return in_maps


def kernel(L0_attn_x, L0_attn_Wqkv, L0_attn_Wg, L0_attn_bg, L0_attn_Wp,
           L0_attn_bp, L0_ffn_w_gate, L0_ffn_w_up, L0_ffn_w_down):
    in_maps = make_inputs(L0_attn_x, L0_attn_Wqkv, L0_attn_Wg, L0_attn_bg,
                          L0_attn_Wp, L0_attn_bp, L0_ffn_w_gate, L0_ffn_w_up,
                          L0_ffn_w_down)
    nc = _get_nc(taps=False)
    res = run_bass_kernel_spmd(nc, in_maps, core_ids=list(range(8)))
    y = np.empty((2, 4 * NT, C), dtype=np.float32)
    for c in range(8):
        b, qt = divmod(c, 4)
        y[b, qt * NT:(qt + 1) * NT, :] = res.results[c]["y"]
    return y


# version 22
# speedup vs baseline: 1.0440x; 1.0012x over previous
import numpy as np
from contextlib import ExitStack

import concourse.bass as bass
import concourse.mybir as mybir
from concourse.bass_utils import run_bass_kernel_spmd

F32 = mybir.dt.float32
BF16 = mybir.dt.bfloat16
AF = mybir.ActivationFunctionType
ALU = mybir.AluOpType
AX = mybir.AxisListType

P = 128
C = 1024
NT = 4096
NTIL = NT // P
H = 16
Dh = 64
I = 2816
N3 = [(0, 1024), (1024, 2048), (2048, 2816)]
THR = [0.25, 0.75, 1.25, 1.75, 2.5, 3.5, 5.0]

ENGS = ("sync", "scalar", "vector", "tensor", "gpsimd")


class Sem:
    __slots__ = ("name", "n", "h")

    def __init__(self, name):
        self.name, self.n, self.h = name, 0, None


class Rec:

    def __init__(self):
        self.ops = {e: [] for e in ENGS}
        self.sems = []
        self.ck = {}
        self.bankprev = {}
        self.jnk = None
        self.jnkS = None

    def sem(self, name):
        s = Sem(name)
        self.sems.append(s)
        return s

    def _junk(self, eng):
        if eng == "scalar":
            j = self.jnkS
            return lambda e, o=j: e.memzero(o)
        j = self.jnk
        return lambda e, o=j: e.memset(o, 1.0)

    def op(self, eng, fn, waits=(), incs=(), pad=False, prepad=False, fence=False):
        w = tuple((s, v) for s, v in waits if v > 0)
        if prepad:
            self.ops[eng].append((w, self._junk(eng), ()))
        if fence:
            self.ops[eng].append((w, fn, ()))
            for s, k in incs:
                s.n += k
            self.ops[eng].append(((), self._junk(eng), tuple(incs)))
        else:
            for s, k in incs:
                s.n += k
            self.ops[eng].append((w, fn, tuple(incs)))
        if pad:
            self.ops[eng].append(((), self._junk(eng), ()))

    def mark(self, name, sem):
        self.ck[name] = (sem, sem.n)

    def at(self, name):
        return self.ck[name]

    def wtap(self, name):
        return [self.ck[name]] if name in self.ck else []

    def replay(self, e, eng):
        lastw = {}
        for waits, fn, incs in self.ops[eng]:
            for s, v in waits:
                if lastw.get(id(s), 0) >= v:
                    continue
                e.wait_ge(s.h, v)
                lastw[id(s)] = v
            ins = fn(e) if fn is not None else None
            for s, k in incs:
                ins.then_inc(s.h, k)


def build_program(taps=False):
    nc = bass.Bass("TRN2", target_bir_lowering=False)

    x_d = nc.dram_tensor("x", [NT, C], F32, kind="ExternalInput").ap()
    wqkv_d = nc.dram_tensor("wqkv", [C, 3 * C], F32, kind="ExternalInput").ap()
    wg_d = nc.dram_tensor("wg", [C, C], F32, kind="ExternalInput").ap()
    wp_d = nc.dram_tensor("wp", [C, C], F32, kind="ExternalInput").ap()
    wgate_d = nc.dram_tensor("wgate", [I, C], F32, kind="ExternalInput").ap()
    wup_d = nc.dram_tensor("wup", [I, C], F32, kind="ExternalInput").ap()
    wdown_d = nc.dram_tensor("wdown", [C, I], F32, kind="ExternalInput").ap()
    bgt_d = nc.dram_tensor("bgt", [P, C], F32, kind="ExternalInput").ap()
    bpt_d = nc.dram_tensor("bpt", [P, C], F32, kind="ExternalInput").ap()
    csts_d = nc.dram_tensor("csts_d", [P, 8], F32, kind="ExternalInput").ap()
    y_d = nc.dram_tensor("y", [NT, C], F32, kind="ExternalOutput").ap()

    gates_sp = nc.dram_tensor("gates_sp", [NT, C], F32, kind="Internal").ap()
    q1_sp = nc.dram_tensor("q1_sp", [NT, C], F32, kind="Internal").ap()
    part_sp = nc.dram_tensor("part_sp", [NT, C], F32, kind="Internal").ap()
    xnh_sp = nc.dram_tensor("xnh_sp", [NTIL, P, 8, P], BF16, kind="Internal").ap()
    xnl_sp = nc.dram_tensor("xnl_sp", [NTIL, P, 8, P], BF16, kind="Internal").ap()
    xqh_sp = nc.dram_tensor("xqh_sp", [NTIL, P, 8, P], BF16, kind="Internal").ap()
    xql_sp = nc.dram_tensor("xql_sp", [NTIL, P, 8, P], BF16, kind="Internal").ap()
    wgh_sp = nc.dram_tensor("wgh_sp", [P, 8, I], BF16, kind="Internal").ap()
    wgl_sp = nc.dram_tensor("wgl_sp", [P, 8, I], BF16, kind="Internal").ap()
    wuh_sp = nc.dram_tensor("wuh_sp", [P, 8, I], BF16, kind="Internal").ap()
    wul_sp = nc.dram_tensor("wul_sp", [P, 8, I], BF16, kind="Internal").ap()
    wdh_sp = nc.dram_tensor("wdh_sp", [P, 22, C], BF16, kind="Internal").ap()
    wdl_sp = nc.dram_tensor("wdl_sp", [P, 22, C], BF16, kind="Internal").ap()
    cc_in = nc.dram_tensor("cc_in", [65, C], F32, kind="Internal").ap()
    cc_out = nc.dram_tensor("cc_out", [65, C], F32, kind="Internal").ap()

    tap = {}
    if taps:
        for nm, shp, dt in [("t_xn", [640, C], F32), ("t_gates", [640, C], F32),
                            ("t_q1", [640, C], F32), ("t_k1", [640, C], F32),
                            ("t_kvpart", [65, C], F32), ("t_kvall", [65, C], F32),
                            ("t_part", [256, C], F32), ("t_h", [P, 1024], F32),
                            ("t_xqh", [640, C], BF16), ("t_xql", [640, C], BF16),
                            ("t_ksb", [P, C], F32), ("t_norm", [P, H], F32),
                            ("t_qn", [P, C], F32), ("t_w2h", [P, C], BF16),
                            ("t_w2l", [P, C], BF16), ("t_qnh", [P, C], BF16),
                            ("t_qnl", [P, C], BF16), ("t_bfh", [P, 8, P], BF16),
                            ("t_cgh", [P, 8, 1024], BF16), ("t_cgl", [P, 8, 1024], BF16),
                            ("t_cdh", [P, 8, C], BF16), ("t_cdl", [P, 8, C], BF16),
                            *[(f"t_dbg{k}", [P, 1], F32) for k in range(15)],
                            ("t_psph", [P, 1408], BF16), ("t_pspl", [P, 1408], BF16),
                            ("t_pyq", [P, 1408], F32), ("t_paa", [P, 1408], F32)]:
            tap[nm] = nc.dram_tensor(nm, shp, dt, kind="ExternalOutput").ap()

    R = Rec()
    sLd = R.sem("sLd")
    sTr = R.sem("sTr")
    sSt = R.sem("sSt")
    sS = R.sem("sS")
    sV = R.sem("sV")
    sT = R.sem("sT")
    sCC = R.sem("sCC")
    sP = R.sem("sP")

    def sb(name, shape, dt):
        g = nc.sbuf_tensor(name, shape, dt)
        t = g.__enter__()
        return t.ap(), g

    def scope_alloc(scope, name, shape, dt):
        ap, g = sb(name, shape, dt)
        scope.append(g)
        return ap

    def scope_exit(scope):
        for g in reversed(scope):
            g.__exit__(None, None, None)
        scope.clear()

    csts, _ = sb("csts", [P, 8], F32)
    f2 = [sb(f"f2_{j}", [P, C], F32)[0] for j in range(2)]
    g2 = [sb(f"g2_{j}", [P, C], F32)[0] for j in range(2)]
    acc1 = [sb(f"acc1_{j}", [P, 1], F32)[0] for j in range(2)]
    acc2 = [sb(f"acc2_{j}", [P, 1], F32)[0] for j in range(2)]
    tmean = [sb(f"tmean_{j}", [P, 1], F32)[0] for j in range(2)]
    tvar = [sb(f"tvar_{j}", [P, 1], F32)[0] for j in range(2)]
    trstd = [sb(f"trstd_{j}", [P, 1], F32)[0] for j in range(2)]
    tnmr = [sb(f"tnmr_{j}", [P, 1], F32)[0] for j in range(2)]
    ln_a = [sb(f"ln_a{j}", [P, 1], F32)[0] for j in range(2)]
    ln_b = [sb(f"ln_b{j}", [P, 1], F32)[0] for j in range(2)]
    red2 = [sb(f"red2_{j}", [P, 88], F32)[0] for j in range(2)]
    sq2 = [sb(f"sq2_{j}", [P, 88], F32)[0] for j in range(2)]
    rq2 = [sb(f"rq2_{j}", [P, 88], F32)[0] for j in range(2)]
    kvacc, _ = sb("kvacc", [65, C], F32)
    jnkt, _ = sb("jnkt", [P, 8], F32)
    jnkS, _ = sb("jnkS", [P, 8], F32)
    R.jnk = jnkt[:, :]
    R.jnkS = jnkS[:, :]

    eps_c = csts[:, 0:1]
    thr_c = [csts[:, 1 + j:2 + j] for j in range(7)]

    ps_kv = nc.psum_tensor("ps_kv", [P, C], F32).__enter__().ap()
    pp = [nc.psum_tensor(f"pp{j}", [P, 512], F32).__enter__().ap() for j in range(2)]
    psW = nc.psum_tensor("psW", [P, C], F32).__enter__().ap()
    psD = [nc.psum_tensor(f"psD{j}", [P, 512], F32).__enter__().ap() for j in range(2)]

    def dma(eng, out, in_, sem, waits=(), transpose=False):
        if transpose:
            R.op(eng, lambda e, o=out, i=in_: e.dma_start(out=o, in_=i, transpose=True),
                 waits=waits, incs=[(sem, 16)])
        else:
            R.op(eng, lambda e, o=out, i=in_: e.dma_start(out=o, in_=i),
                 waits=waits, incs=[(sem, 16)])

    def mm_group(bank, bankkey, triples, waits_first):
        w0 = list(waits_first)
        prev = R.bankprev.get(bankkey)
        if prev is not None:
            w0.append(prev)
        n = len(triples)
        for t, (o, l, r) in enumerate(triples):
            R.op("tensor",
                 lambda e, o=o, l=l, r=r, s=(t == 0), p=(t == n - 1):
                 e.matmul(o, l, r, start=s, stop=p),
                 waits=w0 if t == 0 else [],
                 incs=[(sT, 1)] if t == n - 1 else [])

    def bank_release(bankkey, sem, val):
        R.bankprev[bankkey] = (sem, val)

    def proj3(bank, bankkey, lh, ll, wh, wl, ncols, csl, waits_first):
        triples = []
        for cch in range(8):
            triples.append((bank[:, 0:ncols], lh[:, cch, :], wh[:, cch, csl]))
            triples.append((bank[:, 0:ncols], lh[:, cch, :], wl[:, cch, csl]))
            triples.append((bank[:, 0:ncols], ll[:, cch, :], wh[:, cch, csl]))
        mm_group(bank, bankkey, triples, waits_first)
        return sT.n

    qprev = {}

    def quant_stage1(pool, key, src, width, idx, waits_v):
        nb = width // 16
        red, sq, rq = red2[idx][:, 0:nb], sq2[idx][:, 0:nb], rq2[idx][:, 0:nb]
        yq = pool["yq"][idx][:, 0:width]
        prev = [qprev[key]] if key in qprev else []
        R.op("vector", lambda e, r=red, s=src: e.tensor_reduce(
            r, s.rearrange("p (b k) -> p b k", k=16), AX.X, ALU.max,
            apply_absolute_value=True), waits=list(waits_v) + prev, pad=True)
        R.op("vector", lambda e, o=sq, i=red: e.tensor_scalar(
            o, i, 1.0 / 6.0, 1e-12, ALU.mult, ALU.max), incs=[(sV, 1)])
        v_sq = sV.n
        R.op("vector", lambda e, o=rq, i=sq: e.reciprocal(o, i),
             waits=[(sV, v_sq)], incs=[(sV, 1)])
        v_rq = sV.n
        R.op("vector", lambda e, o=yq, s=src, r=rq, k=nb: e.tensor_tensor(
            o.rearrange("p (b k) -> p b k", k=16),
            s.rearrange("p (b k) -> p b k", k=16),
            r.broadcast_to((P, k, 16)), ALU.mult), waits=[(sV, v_rq)],
            incs=[(sV, 1)])
        return {"v_yq": sV.n, "v_sq": v_sq, "sq": sq, "yq": yq,
                "idx": idx, "width": width, "nb": nb}

    def quant_stage2(pool, key, st, dst_h, dst_l, waits2=(), use_pool=True):
        width, nb, idx = st["width"], st["nb"], st["idx"]
        yq, sq = st["yq"], st["sq"]
        aa = pool["aa"][idx][:, 0:width]
        sb7 = [b[:, 0:width] for b in pool["sb"]]
        ub = [b[:, 0:width] for b in pool["ub"]]
        sgb = pool["sg"][idx][:, 0:width]
        prev = [qprev[key]] if key in qprev else []
        R.op("scalar", lambda e, o=sgb, i=yq: e.activation(o, i, AF.Sign),
             waits=[(sV, st["v_yq"])] + prev)
        R.op("scalar", lambda e, o=aa, i=yq: e.activation(o, i, AF.Abs))
        for t in range(4):
            R.op("scalar", lambda e, o=sb7[t], i=aa, b=thr_c[t]: e.activation(
                o, i, AF.Sign, bias=b), incs=[(sS, 1)] if t == 3 else [])
        s_ab4 = sS.n
        for t in range(4, 7):
            R.op("scalar", lambda e, o=sb7[t], i=aa, b=thr_c[t]: e.activation(
                o, i, AF.Sign, bias=b), incs=[(sS, 1)] if t == 6 else [])
        s_all = sS.n
        if use_pool:
            R.op("gpsimd", lambda e, o=ub[0], a=sb7[0], b=sb7[1]: e.tensor_tensor(
                o, a, b, ALU.add), waits=[(sS, s_ab4)] + list(waits2))
            R.op("gpsimd", lambda e, o=ub[1], a=sb7[2], b=sb7[3]: e.tensor_tensor(
                o, a, b, ALU.add))
            R.op("gpsimd", lambda e, o=ub[0], a=ub[0], b=ub[1]: e.tensor_tensor(
                o, a, b, ALU.add))
            R.op("gpsimd", lambda e, o=ub[1], a=sb7[4], b=sb7[5]: e.tensor_tensor(
                o, a, b, ALU.add), waits=[(sS, s_all)], incs=[(sP, 1)],
                fence=True)
            p_tree = sP.n
            if use_pool == "adds":
                R.op("vector", lambda e, o=ub[1], a=sb7[6], b=ub[1]:
                     e.scalar_tensor_tensor(o, a, 2.0, b, ALU.mult, ALU.add),
                     waits=[(sS, s_all), (sP, p_tree)], prepad=True)
                R.op("vector", lambda e, o=ub[0], a=ub[1], b=ub[0]:
                     e.scalar_tensor_tensor(o, a, 2.0, b, ALU.mult, ALU.add))
                R.op("vector", lambda e, o=ub[1], i=ub[0]: e.tensor_scalar(
                    o, i, 0.25, 3.0, ALU.mult, ALU.add))
                R.op("vector", lambda e, o=ub[0], a=ub[1], b=sgb: e.tensor_tensor(
                    o, a, b, ALU.mult))
                R.op("vector", lambda e, o=yq, a=ub[0], s=sq, k=nb: e.tensor_tensor(
                    o.rearrange("p (b k) -> p b k", k=16),
                    a.rearrange("p (b k) -> p b k", k=16),
                    s.broadcast_to((P, k, 16)), ALU.mult),
                    waits=[(sV, st["v_sq"])])
                R.op("vector", lambda e, o=dst_h, i=yq: e.tensor_copy(o, i))
                R.op("vector", lambda e, o=dst_l, a=yq, b=dst_h: e.tensor_tensor(
                    o, a, b, ALU.subtract), incs=[(sV, 1)], fence=True)
                qprev[key] = (sV, sV.n)
                return qprev[key]
            R.op("vector", lambda e, o=ub[1], a=sb7[6], b=ub[1]:
                 e.scalar_tensor_tensor(o, a, 2.0, b, ALU.mult, ALU.add),
                 waits=[(sS, s_all), (sP, p_tree)], prepad=True)
            R.op("vector", lambda e, o=ub[0], a=ub[1], b=ub[0]:
                 e.scalar_tensor_tensor(o, a, 2.0, b, ALU.mult, ALU.add))
            R.op("vector", lambda e, o=ub[1], i=ub[0]: e.tensor_scalar(
                o, i, 0.25, 3.0, ALU.mult, ALU.add))
            R.op("vector", lambda e, o=ub[0], a=ub[1], b=sgb: e.tensor_tensor(
                o, a, b, ALU.mult), incs=[(sV, 1)], fence=True)
            v_g = sV.n
            R.op("gpsimd", lambda e, o=yq, a=ub[0], s=sq, k=nb: e.tensor_tensor(
                o.rearrange("p (b k) -> p b k", k=16),
                a.rearrange("p (b k) -> p b k", k=16),
                s.broadcast_to((P, k, 16)), ALU.mult),
                waits=[(sV, v_g), (sV, st["v_sq"])])
            R.op("gpsimd", lambda e, o=dst_h, i=yq: e.tensor_copy(o, i))
            R.op("gpsimd", lambda e, o=dst_l, a=yq, b=dst_h: e.tensor_tensor(
                o, a, b, ALU.subtract), incs=[(sP, 1)], fence=True)
            qprev[key] = (sP, sP.n)
            return qprev[key]
        R.op("vector", lambda e, o=ub[0], a=sb7[0], b=sb7[1]: e.tensor_tensor(
            o, a, b, ALU.add), waits=[(sS, s_ab4)] + list(waits2))
        R.op("vector", lambda e, o=ub[1], a=sb7[2], b=sb7[3]: e.tensor_tensor(
            o, a, b, ALU.add))
        R.op("vector", lambda e, o=ub[0], a=ub[0], b=ub[1]: e.tensor_tensor(
            o, a, b, ALU.add))
        R.op("vector", lambda e, o=ub[1], a=sb7[4], b=sb7[5]: e.tensor_tensor(
            o, a, b, ALU.add), waits=[(sS, s_all)])
        R.op("vector", lambda e, o=ub[1], a=sb7[6], b=ub[1]:
             e.scalar_tensor_tensor(o, a, 2.0, b, ALU.mult, ALU.add))
        R.op("vector", lambda e, o=ub[0], a=ub[1], b=ub[0]:
             e.scalar_tensor_tensor(o, a, 2.0, b, ALU.mult, ALU.add))
        R.op("vector", lambda e, o=ub[1], i=ub[0]: e.tensor_scalar(
            o, i, 0.25, 3.0, ALU.mult, ALU.add))
        R.op("vector", lambda e, o=ub[0], a=ub[1], b=sgb: e.tensor_tensor(
            o, a, b, ALU.mult))
        R.op("vector", lambda e, o=yq, a=ub[0], s=sq, k=nb: e.tensor_tensor(
            o.rearrange("p (b k) -> p b k", k=16),
            a.rearrange("p (b k) -> p b k", k=16),
            s.broadcast_to((P, k, 16)), ALU.mult), waits=[(sV, st["v_sq"])])
        R.op("vector", lambda e, o=dst_h, i=yq: e.tensor_copy(o, i))
        R.op("vector", lambda e, o=dst_l, a=yq, b=dst_h: e.tensor_tensor(
            o, a, b, ALU.subtract), incs=[(sV, 1)], fence=True)
        qprev[key] = (sV, sV.n)
        return qprev[key]

    def quant_chain(pool, key, src, width, idx, dst_h, dst_l, waits_v,
                    mode=False):
        st = quant_stage1(pool, key, src, width, idx, waits_v)
        return quant_stage2(pool, key, st, dst_h, dst_l, use_pool=mode)

    scPREP = []
    PW = 1024
    pstg = [scope_alloc(scPREP, f"pstg{j}", [P, PW], F32) for j in range(3)]
    psp_h = scope_alloc(scPREP, "psp_h", [P, PW], BF16)
    psp_l = scope_alloc(scPREP, "psp_l", [P, PW], BF16)
    pwtr = [scope_alloc(scPREP, "pwtr0", [P, 8, P], BF16)] * 2
    qpP = {
        "yq": [scope_alloc(scPREP, f"yqP{j}", [P, PW], F32) for j in range(2)],
    }
    qpP["aa"] = qpP["yq"]
    qpP.update({
        "sb": [scope_alloc(scPREP, f"sbP{j}", [P, PW], BF16) for j in range(7)],
        "ub": [scope_alloc(scPREP, f"ubP{j}", [P, PW], BF16) for j in range(2)],
        "sg": [scope_alloc(scPREP, "sgP0", [P, PW], BF16)] * 2,
    })

    scA1 = []
    x2 = [scope_alloc(scA1, f"x2a{j}", [P, C], F32) for j in range(3)]
    xn3 = [scope_alloc(scA1, f"xn3_{j}", [P, C], F32) for j in range(3)]
    scrA1 = scope_alloc(scA1, "scrA1", [P, C], F32)
    wgh = scope_alloc(scA1, "wgh", [P, 8, C], BF16)
    wgl = scope_alloc(scA1, "wgl", [P, 8, C], BF16)
    BG = scope_alloc(scA1, "BG", [P, C], F32)
    qpA = {
        "yq": [b[:, 0:C] for b in qpP["yq"]],
        "aa": [b[:, 0:C] for b in qpP["aa"]],
        "sb": qpP["sb"],
        "ub": qpP["ub"],
        "sg": [scope_alloc(scA1, f"sgA{j}", [P, C], BF16) for j in range(2)],
    }
    xnh_b = [scope_alloc(scA1, f"xnhb{j}", [P, C], BF16) for j in range(2)]
    xnl_b = [scope_alloc(scA1, f"xnlb{j}", [P, C], BF16) for j in range(2)]
    xqh_b = [scope_alloc(scA1, f"xqhb{j}", [P, C], BF16) for j in range(2)]
    xql_b = [scope_alloc(scA1, f"xqlb{j}", [P, C], BF16) for j in range(2)]
    fmh = [scope_alloc(scA1, f"fmh{j}", [P, 8, P], BF16) for j in range(2)]
    fml = [scope_alloc(scA1, f"fml{j}", [P, 8, P], BF16) for j in range(2)]
    fqh = [scope_alloc(scA1, f"fqh{j}", [P, 8, P], BF16) for j in range(2)]
    fql = [scope_alloc(scA1, f"fql{j}", [P, 8, P], BF16) for j in range(2)]

    dma("sync", csts[:, :], csts_d[:, :], sLd)
    dma("sync", BG[:, :], bgt_d[:, :], sLd)
    for cch in range(8):
        w_stg = [R.at(f"wgsp_{cch - 1}")] if cch >= 1 else []
        dma("sync", pstg[cch % 2][:, 0:C], wg_d[cch * P:(cch + 1) * P, :], sLd,
            waits=w_stg)
        ldv = sLd.n
        R.op("vector", lambda e, o=wgh[:, cch, :], i=pstg[cch % 2][:, 0:C]:
             e.tensor_copy(o, i), waits=[(sLd, ldv)])
        R.op("vector", lambda e, o=wgl[:, cch, :], a=pstg[cch % 2][:, 0:C],
             b=wgh[:, cch, :]: e.tensor_tensor(o, a, b, ALU.subtract),
             incs=[(sV, 1)], fence=True)
        R.mark(f"wgsp_{cch}", sV)
    R.mark("wg_ready", sV)

    R.op("vector", lambda e, o=kvacc[:, :]: e.memset(o, 0.0), incs=[(sV, 1)])

    s_er_tab = {}
    a1banks = [pp[0], pp[1], psW[:, 0:512], psW[:, 512:1024]]

    ld_tab = {}

    def a1_xload(L):
        w_x = [R.at(f"a1_xfree_{L - 3}")] if L >= 3 else []
        dma("sync", x2[L % 3][:, :], x_d[L * P:(L + 1) * P, :], sLd, waits=w_x)
        ld_tab[L] = sLd.n

    def a1_tail(j, st):
        idx, xnt, s_xn, t_gs, gbt = st
        for nchv in range(2):
            bk = (2 * j + nchv) % 4
            w_gb = [(sT, t_gs[nchv])]
            if nchv == 0 and j >= 2:
                w_gb.append(R.at(f"a1_gatesfree_{j - 2}"))
            if nchv == 0:
                w_gb += R.wtap(f"tf_gates_{j - 2}")
            R.op("vector", lambda e, o=gbt[:, nchv * 512:(nchv + 1) * 512],
                 a=a1banks[bk][:, :], b=BG[:, nchv * 512:(nchv + 1) * 512]:
                 e.scalar_tensor_tensor(o, a, 1.0, b, ALU.mult, ALU.add),
                 waits=w_gb, incs=[(sV, 1)], prepad=True)
            bank_release(f"g{bk}", sV, sV.n)
        v_gb = sV.n
        R.op("scalar", lambda e, o=gbt, i=gbt: e.activation(o[:, :], i[:, :],
             AF.Sigmoid), waits=[(sV, v_gb)], incs=[(sS, 1)])
        s_gates = sS.n
        dma("gpsimd", gates_sp[j * P:(j + 1) * P, :], gbt[:, :], sSt,
            waits=[(sS, s_gates)])
        R.mark(f"a1_gatesfree_{j}", sSt)
        if taps and j < 5:
            dma("gpsimd", tap["t_xn"][j * P:(j + 1) * P, :], xnt[:, :], sSt,
                waits=[(sS, s_xn)])
            R.mark(f"tf_xn_{j}", sSt)
            dma("gpsimd", tap["t_gates"][j * P:(j + 1) * P, :], gbt[:, :], sSt,
                waits=[(sS, s_gates)])
            R.mark(f"tf_gates_{j}", sSt)

        w_qv = [(sS, s_xn)]
        if j >= 2:
            w_qv.append(R.at(f"a1_xqfree_{j - 2}"))
        w_qv += R.wtap(f"tf_xq_{j - 2}")
        rq_t = quant_chain(qpA, "A", xnt[:, :], C, idx, xqh_b[idx][:, :],
                           xql_b[idx][:, :], w_qv, mode="adds")
        R.mark(f"a1_xnfree_{j}", sV)
        w_fq = []
        if j >= 2:
            w_fq = [R.at(f"a1_fqfree_{j - 2}")]
        dma("sync", fqh[idx][:, :, :], xqh_b[idx][:, :], sTr,
            waits=[rq_t] + w_fq, transpose=True)
        dma("sync", fql[idx][:, :, :], xql_b[idx][:, :], sTr, transpose=True)
        tr_fq = sTr.n
        R.ck[f"a1_xqfree_{j}"] = (sTr, tr_fq)
        dma("gpsimd", xqh_sp[j, :, :, :], fqh[idx][:, :, :], sSt,
            waits=[(sTr, tr_fq)])
        dma("gpsimd", xql_sp[j, :, :, :], fql[idx][:, :, :], sSt)
        R.mark(f"a1_fqfree_{j}", sSt)
        if taps and j < 5:
            dma("gpsimd", tap["t_xqh"][j * P:(j + 1) * P, :], xqh_b[idx][:, :],
                sSt, waits=[rq_t])
            dma("gpsimd", tap["t_xql"][j * P:(j + 1) * P, :], xql_b[idx][:, :],
                sSt)
            R.mark(f"tf_xq_{j}", sSt)

    ln_nmr = {}

    def ln_front(j):
        idx = j % 2
        xt = x2[j % 3]
        w_st = [(sLd, ld_tab.pop(j))]
        if j >= 2:
            w_st.append(R.at(f"ln_free_{j - 2}"))
        R.op("scalar", lambda e, o=scrA1[:, :], i=xt, a=acc1[idx]: e.activation(
            o, i[:, :], AF.Copy, accum_out=a[:, :]), waits=w_st)
        R.op("scalar", lambda e, o=scrA1[:, :], i=xt, a=acc2[idx]: e.activation(
            o, i[:, :], AF.Square, accum_out=a[:, :]), incs=[(sS, 1)],
            fence=True)
        s_stats = sS.n
        R.op("vector", lambda e, o=tmean[idx], i=acc1[idx]: e.tensor_scalar(
            o[:, :], i[:, :], 1.0 / C, None, ALU.mult), waits=[(sS, s_stats)],
            incs=[(sV, 1)], pad=True, prepad=True)
        R.op("vector", lambda e, o=ln_b[idx], a=tmean[idx], b=tmean[idx]:
             e.tensor_tensor(o[:, :], a[:, :], b[:, :], ALU.mult),
             waits=[(sV, sV.n)], incs=[(sV, 1)], pad=True)
        R.op("vector", lambda e, o=ln_a[idx], i=acc2[idx]: e.tensor_scalar(
            o[:, :], i[:, :], 1.0 / C, None, ALU.mult), waits=[(sV, sV.n)],
            incs=[(sV, 1)], pad=True)
        R.op("vector", lambda e, o=tvar[idx], a=ln_a[idx], b=ln_b[idx]:
             e.tensor_tensor(o[:, :], a[:, :], b[:, :], ALU.subtract),
             waits=[(sV, sV.n)], incs=[(sV, 1)], pad=True)
        R.op("vector", lambda e, o=tvar[idx], i=tvar[idx]: e.tensor_scalar(
            o[:, :], i[:, :], 1e-5, None, ALU.add), waits=[(sV, sV.n)],
            incs=[(sV, 1)], pad=True)
        v_var = sV.n
        if taps and j < 5:
            dma("gpsimd", tap[f"t_dbg{j}"][:, :], acc2[idx][:, :], sSt,
                waits=[(sS, s_stats)])
            dma("gpsimd", tap[f"t_dbg{5 + j}"][:, :], tvar[idx][:, :], sSt,
                waits=[(sV, v_var)])
        R.op("scalar", lambda e, o=ln_a[idx], i=tvar[idx]: e.activation(
            o[:, :], i[:, :], AF.Sqrt), waits=[(sV, v_var)], incs=[(sS, 1)])
        s_rstd = sS.n
        R.op("vector", lambda e, o=ln_b[idx], i=ln_a[idx]: e.reciprocal(
            o[:, :], i[:, :]), waits=[(sS, s_rstd)], incs=[(sV, 1)])
        R.op("vector", lambda e, o=ln_b[idx], a=tvar[idx], b=ln_b[idx]:
             e.tensor_tensor(o[:, :], a[:, :], b[:, :], ALU.mult),
             waits=[(sV, sV.n)], incs=[(sV, 1)], pad=True)
        R.op("vector", lambda e, o=ln_b[idx], a=ln_a[idx], b=ln_b[idx]:
             e.tensor_tensor(o[:, :], a[:, :], b[:, :], ALU.add),
             waits=[(sV, sV.n)], incs=[(sV, 1)], pad=True)
        R.op("vector", lambda e, o=ln_b[idx], i=ln_b[idx]: e.tensor_scalar(
            o[:, :], i[:, :], 0.5, None, ALU.mult), waits=[(sV, sV.n)],
            incs=[(sV, 1)], pad=True)
        R.op("vector", lambda e, o=trstd[idx], i=ln_b[idx]: e.reciprocal(
            o[:, :], i[:, :]), waits=[(sV, sV.n)], incs=[(sV, 1)])
        R.op("vector", lambda e, o=tnmr[idx], a=tmean[idx], b=trstd[idx]:
             e.scalar_tensor_tensor(o[:, :], a[:, :], -1.0, b[:, :],
                                    ALU.mult, ALU.mult), waits=[(sV, sV.n)],
             incs=[(sV, 1)], pad=True)
        ln_nmr[j] = sV.n
        if taps and j < 5:
            dma("gpsimd", tap[f"t_dbg{10 + j}"][:, :], trstd[idx][:, :], sSt,
                waits=[(sV, ln_nmr[j])])
        R.mark(f"ln_free_{j}", sV)

    a1_xload(0)
    a1_xload(1)
    ln_front(0)
    a1_prev = None
    for i in range(NTIL):
        idx = i % 2
        xt = x2[i % 3]
        xnt = xn3[i % 3]
        gbt = g2[idx]

        if i + 2 < NTIL:
            a1_xload(i + 2)

        w_xn = [(sV, ln_nmr.pop(i))]
        if i >= 3:
            w_xn.append(R.at(f"a1_xnfree_{i - 3}"))
        w_xn += R.wtap(f"tf_xn_{i - 3}")
        R.op("scalar", lambda e, o=xnt, i=xt, sc=trstd[idx], b=tnmr[idx]:
             e.activation(o[:, :], i[:, :], AF.Identity, bias=b[:, :], scale=sc[:, :]),
             waits=w_xn, incs=[(sS, 1)])
        s_xn = sS.n
        R.mark(f"a1_xfree_{i}", sS)

        w_sp = [R.at(f"a1_xnhfree_{i - 2}")] if i >= 2 else []
        R.op("vector", lambda e, o=xnh_b[idx], i=xnt: e.tensor_copy(o[:, :], i[:, :]),
             waits=[(sS, s_xn)] + w_sp)
        R.op("vector", lambda e, o=xnl_b[idx], a=xnt, b=xnh_b[idx]: e.tensor_tensor(
            o[:, :], a[:, :], b[:, :], ALU.subtract), incs=[(sV, 1)], fence=True)
        v_split = sV.n
        w_fm = []
        if i >= 2:
            w_fm = [R.at(f"a1_fmfreeT_{i - 2}"), R.at(f"a1_fmfreeS_{i - 2}")]
        dma("sync", fmh[idx][:, :, :], xnh_b[idx][:, :], sTr,
            waits=[(sV, v_split)] + w_fm, transpose=True)
        dma("sync", fml[idx][:, :, :], xnl_b[idx][:, :], sTr, transpose=True)
        tr_fm = sTr.n
        R.ck[f"a1_xnhfree_{i}"] = (sTr, tr_fm)
        dma("gpsimd", xnh_sp[i, :, :, :], fmh[idx][:, :, :], sSt, waits=[(sTr, tr_fm)])
        dma("gpsimd", xnl_sp[i, :, :, :], fml[idx][:, :, :], sSt)
        R.mark(f"a1_fmfreeS_{i}", sSt)

        t_gs = []
        for nchv in range(2):
            bk = (2 * i + nchv) % 4
            t_g = proj3(a1banks[bk], f"g{bk}", fmh[idx], fml[idx], wgh, wgl, 512,
                        np.s_[nchv * 512:(nchv + 1) * 512],
                        [R.at("wg_ready"), (sTr, tr_fm)])
            t_gs.append(t_g)
        R.mark(f"a1_fmfreeT_{i}", sT)

        if i + 1 < NTIL:
            ln_front(i + 1)

        st_here = (idx, xnt, s_xn, t_gs, gbt)
        if a1_prev is not None:
            a1_tail(i - 1, a1_prev)
        a1_prev = st_here

    a1_tail(NTIL - 1, a1_prev)
    R.bankprev[0] = R.bankprev["g0"]
    R.bankprev[1] = R.bankprev["g1"]

    R.mark("a1_T_done", sT)
    R.mark("a1_V_done", sV)
    R.mark("a1_S_done", sS)
    a1_spills = sSt.n

    prep_descs = []
    for w_d_ap, nchunks, kwidth, sp_h, sp_l in (
            (wgate_d, 22, C, wgh_sp, wgl_sp),
            (wup_d, 22, C, wuh_sp, wul_sp),
            (wdown_d, 8, I, wdh_sp, wdl_sp)):
        for j in range(nchunks):
            for u in range((kwidth + PW - 1) // PW):
                k0 = u * PW
                k1 = min(kwidth, k0 + PW)
                prep_descs.append((w_d_ap, j, k0, k1, k0 // P, sp_h, sp_l))

    prep_state = {"clock": 0, "ld": {}, "st": {}}

    def prep_load(k):
        w_d_ap, j, k0, k1, kofs, sp_h, sp_l = prep_descs[k]
        pidx = k % 3
        wseg = k1 - k0
        w_stage = [R.at(f"prep_stg_free_{pidx}")] if f"prep_stg_free_{pidx}" in R.ck else []
        dma("sync", pstg[pidx][:, 0:wseg], w_d_ap[j * P:(j + 1) * P, k0:k1], sLd,
            waits=w_stage)
        prep_state["ld"][k] = sLd.n

    def prep_stage1(k):
        w_d_ap, j, k0, k1, kofs, sp_h, sp_l = prep_descs[k]
        pidx = k % 3
        wseg = k1 - k0
        st = quant_stage1(qpP, "P", pstg[pidx][:, 0:wseg], wseg, k % 2,
                          [(sLd, prep_state["ld"].pop(k))])
        R.ck[f"prep_stg_free_{pidx}"] = (sV, sV.n)
        prep_state["st"][k] = st

    def prep_stage2(k):
        w_d_ap, j, k0, k1, kofs, sp_h, sp_l = prep_descs[k]
        st = prep_state["st"].pop(k)
        wseg = st["width"]
        ksub = wseg // P
        w2 = [R.at("prep_sp_free")] if "prep_sp_free" in R.ck else []
        rq_t = quant_stage2(qpP, "P", st, psp_h[:, 0:wseg], psp_l[:, 0:wseg],
                            waits2=w2, use_pool=False)
        w_tr = [R.at("prep_wtr_free_1")] if "prep_wtr_free_1" in R.ck else []
        dma("sync", pwtr[0][:, 0:ksub, :], psp_h[:, 0:wseg], sTr,
            waits=[rq_t] + w_tr, transpose=True)
        tr_h = sTr.n
        dma("gpsimd", sp_h[:, kofs:kofs + ksub, j * P:(j + 1) * P],
            pwtr[0][:, 0:ksub, :], sSt, waits=[(sTr, tr_h)])
        R.mark("prep_wtr_free_0", sSt)
        dma("sync", pwtr[1][:, 0:ksub, :], psp_l[:, 0:wseg], sTr,
            waits=[R.at("prep_wtr_free_0")], transpose=True)
        tr_l = sTr.n
        R.ck["prep_sp_free"] = (sTr, tr_l)
        dma("gpsimd", sp_l[:, kofs:kofs + ksub, j * P:(j + 1) * P],
            pwtr[1][:, 0:ksub, :], sSt, waits=[(sTr, tr_l)])
        R.mark("prep_wtr_free_1", sSt)

    NPREP = len(prep_descs)
    prep_pt = {"n": 0}

    def prep_pump():
        prep_pt["n"] += 1
        want = (prep_pt["n"] * (NPREP + 2)) // (4 * NTIL)
        while prep_state["clock"] < want:
            prep_advance()

    def prep_advance():
        k = prep_state["clock"]
        if k < NPREP:
            prep_load(k)
        if 1 <= k < NPREP + 1:
            prep_stage1(k - 1)
        if 2 <= k < NPREP + 2:
            prep_stage2(k - 2)
        prep_state["clock"] += 1

    scope_exit(scA1)

    scA2 = []
    wqh = scope_alloc(scA2, "wqh", [P, 8, 3 * C], BF16)
    wql = scope_alloc(scA2, "wql", [P, 8, 3 * C], BF16)
    qstg = [scope_alloc(scA2, f"qstg{j}", [P, 512], F32) for j in range(2)]
    lfh = [scope_alloc(scA2, f"lfh{j}", [P, 8, P], BF16) for j in range(2)]
    lfl = [scope_alloc(scA2, f"lfl{j}", [P, 8, P], BF16) for j in range(2)]
    zb = [scope_alloc(scA2, "zb0", [P, 512], F32)] * 2
    ebz = [scope_alloc(scA2, "ebz0", [P, 512], F32)] * 2
    k1b = [scope_alloc(scA2, f"k1b{j}", [P, C], F32) for j in range(2)]
    vxb = [scope_alloc(scA2, f"vxb{j}", [P, H, 65], F32) for j in range(2)]

    a1_gate = [R.at("a1_T_done"), R.at("a1_V_done"), R.at("a1_S_done"),
               (sSt, a1_spills), (sTr, sTr.n)]

    def wq_piece(k8):
        hf, cch = divmod(k8, 8)
        cs = np.s_[hf * 512:(hf + 1) * 512]
        qsg = qstg[k8 % 2]
        w_stg = [R.at(f"wqsp_{k8 - 2}")] if k8 >= 2 else list(a1_gate)
        dma("sync", qsg[:, :], wqkv_d[cch * P:(cch + 1) * P, cs], sLd,
            waits=w_stg)
        ldv = sLd.n
        R.op("vector", lambda e, o=wqh[:, cch, cs], i=qsg[:, :]:
             e.tensor_copy(o, i), waits=[(sLd, ldv)])
        R.op("vector", lambda e, o=wql[:, cch, cs], a=qsg[:, :],
             b=wqh[:, cch, cs]: e.tensor_tensor(o, a, b, ALU.subtract),
             incs=[(sV, 1)], fence=True)
        R.mark(f"wqsp_{k8}", sV)
        if cch == 7:
            R.mark(f"wq_col_{hf}", sV)
        if k8 == 47:
            R.mark("wq_ready", sV)
            R.op("vector", lambda e, o=vxb[0][:, :, 64:65]: e.memset(o, 1.0))
            R.op("vector", lambda e, o=vxb[1][:, :, 64:65]: e.memset(o, 1.0),
                 incs=[(sV, 1)], fence=True)
            R.mark("vones", sV)

    a2_ldt = {}

    def a2_loads(i):
        idx = i % 2
        w_g = [R.at(f"a1_gatesfree_{i}")]
        if i >= 2:
            w_g.append(R.at(f"a2_gatesbuf_free_{i - 2}"))
        else:
            w_g.append(R.at(f"a1_gatesfree_{NTIL - 2 + i}"))
        dma("sync", g2[idx][:, :], gates_sp[i * P:(i + 1) * P, :], sLd, waits=w_g)
        w_l = [R.at(f"a2_lffree_{i - 2}")] if i >= 2 else \
            list(a1_gate) + [R.at(f"a1_fqfree_{i}")]
        dma("sync", lfh[idx][:, :, :], xnh_sp[i, :, :, :], sLd, waits=w_l)
        dma("sync", lfl[idx][:, :, :], xnl_sp[i, :, :, :], sLd)
        a2_ldt[i] = sLd.n

    a2_loads(0)
    for k8 in range(16):
        wq_piece(k8)
    for i in range(NTIL):
        idx = i % 2
        gt = g2[idx]
        q1t = f2[idx]
        ld_i = a2_ldt.pop(i)
        if i + 1 < NTIL:
            a2_loads(i + 1)

        for nch in range(6):
            if i == 0 and nch >= 1:
                for k8 in range(8 + 8 * nch, min(48, 16 + 8 * nch)):
                    wq_piece(k8)
            bank = pp[nch % 2]
            wq_gate = R.at(f"wq_col_{nch}") if i == 0 else R.at("wq_ready")
            t_mm = proj3(bank, nch % 2, lfh[idx], lfl[idx], wqh, wql, 512,
                         np.s_[nch * 512:(nch + 1) * 512],
                         [wq_gate, (sLd, ld_i)])
            if nch < 4:
                dsts = q1t if nch < 2 else k1b[idx]
                dcol = (nch % 2) * 512
                w_z = [(sT, t_mm)]
                prev_er = s_er_tab.get((i, nch - 1), s_er_tab.get((i - 1, 3)))
                if prev_er is not None and nch <= 3:
                    w_z.append(prev_er)
                if nch < 2 and i >= 2:
                    w_z.append(R.at(f"a2_q1buf_free_{i - 2}"))
                if nch >= 2 and i >= 2:
                    w_z.append(R.at(f"a2_einsum_{i - 2}"))
                    w_z += R.wtap(f"tf_k1_{i - 2}")
                R.op("vector", lambda e, o=zb[nch % 2][:, :], a=bank[:, :],
                     b=gt[:, dcol:dcol + 512]:
                     e.tensor_tensor(o, a, b, ALU.mult), waits=w_z, incs=[(sV, 1)],
                     prepad=True)
                bank_release(nch % 2, sV, sV.n)
                v_z = sV.n
                R.op("vector", lambda e, o=ebz[nch % 2][:, :], i=zb[nch % 2][:, :]:
                     e.tensor_scalar(o, i, 0.0, None, ALU.min), incs=[(sV, 1)])
                v_zm = sV.n
                R.op("scalar", lambda e, o=ebz[nch % 2][:, :], i=ebz[nch % 2][:, :]:
                     e.activation(o, i, AF.Exp), waits=[(sV, v_zm)])
                R.op("scalar", lambda e, o=zb[nch % 2][:, :], i=zb[nch % 2][:, :]:
                     e.activation(o, i, AF.Relu), incs=[(sS, 1)])
                s_er = sS.n
                s_er_tab[(i, nch)] = (sS, s_er)
                R.op("vector", lambda e, o=dsts[:, dcol:dcol + 512],
                     a=ebz[nch % 2][:, :], b=zb[nch % 2][:, :]:
                     e.tensor_tensor(o, a, b, ALU.add), waits=[(sS, s_er)],
                     incs=[(sV, 1)], fence=(nch >= 2))
                if nch == 1:
                    R.mark(f"a2_q1done_{i}", sV)
                if nch == 3:
                    R.mark(f"a2_k1done_{i}", sV)
                    R.mark(f"a2_gatesbuf_free_{i}", sV)
                if nch in (1, 3):
                    prep_pump()
            else:
                h0 = (nch - 4) * 8
                w_v = [(sT, t_mm)]
                if nch == 4 and i >= 2:
                    w_v.append(R.at(f"a2_vfree_{i - 2}"))
                R.op("scalar", lambda e, o=vxb[idx][:, h0:h0 + 8, 0:64], a=bank[:, :]:
                     e.activation(o, a.rearrange("p (h d) -> p h d", d=64), AF.Copy),
                     waits=w_v, incs=[(sS, 1)], prepad=True, fence=(nch == 5))
                bank_release(nch % 2, sS, sS.n)
                if nch == 5:
                    R.mark(f"a2_vdone_{i}", sS)

        prep_pump()

        dma("gpsimd", q1_sp[i * P:(i + 1) * P, :], q1t[:, :], sSt,
            waits=[R.at(f"a2_q1done_{i}")])
        R.mark(f"a2_q1buf_free_{i}", sSt)
        if taps and i < 5:
            dma("gpsimd", tap["t_q1"][i * P:(i + 1) * P, :], q1t[:, :], sSt)
            R.mark(f"a2_q1buf_free_{i}", sSt)
            dma("gpsimd", tap["t_k1"][i * P:(i + 1) * P, :], k1b[idx][:, :], sSt,
                waits=[R.at(f"a2_k1done_{i}")])
            R.mark(f"tf_k1_{i}", sSt)

        w_kv = [R.at(f"a2_k1done_{i}"), R.at(f"a2_vdone_{i}"), R.at("vones")]
        if i >= 1:
            w_kv.append(R.at(f"a2_kvadd_{i - 1}"))
        for h in range(H):
            R.op("tensor", lambda e, o=ps_kv[0:65, h * 64:(h + 1) * 64],
                 l=vxb[idx][:, h, :], r=k1b[idx][:, h * 64:(h + 1) * 64]:
                 e.matmul(o, l, r, start=True, stop=True),
                 waits=w_kv if h == 0 else [],
                 incs=[(sT, 1)] if h == H - 1 else [])
        t_ein = sT.n
        R.mark(f"a2_einsum_{i}", sT)
        R.mark(f"a2_vfree_{i}", sT)
        R.mark(f"a2_lffree_{i}", sT)
        R.op("vector", lambda e, o=kvacc[:, :], a=ps_kv[0:65, :], b=kvacc[:, :]:
             e.scalar_tensor_tensor(o, a, 1.0, b, ALU.mult, ALU.add),
             waits=[(sT, t_ein)], incs=[(sV, 1)], prepad=True, fence=True)
        R.mark(f"a2_kvadd_{i}", sV)

        prep_pump()

    while prep_state["clock"] < NPREP + 2:
        prep_advance()
    R.mark("wprep_done", sSt)
    R.mark("prep_tr_done", sTr)
    R.mark("a2_S_done", sS)
    R.mark("a2_T_done", sT)
    R.mark("a2_kv_done", sV)
    scope_exit(scA2)
    scope_exit(scPREP)

    def post_a2_gate():
        return [R.at("a2_T_done"), R.at("a2_kv_done"), R.at("wprep_done"),
                R.at("prep_tr_done"), R.at("a2_S_done")]

    scCW = []
    x2 = [scope_alloc(scCW, f"x2c{j}", [P, C], F32) for j in range(2)]
    e2 = [scope_alloc(scCW, f"e2_{j}", [P, C], F32) for j in range(4)]
    cgh = scope_alloc(scCW, "cgh", [P, 8, 1024], BF16)
    cgl = scope_alloc(scCW, "cgl", [P, 8, 1024], BF16)
    cuh = scope_alloc(scCW, "cuh", [P, 8, 1024], BF16)
    cul = scope_alloc(scCW, "cul", [P, 8, 1024], BF16)
    cdh = scope_alloc(scCW, "cdh", [P, 8, C], BF16)
    cdl = scope_alloc(scCW, "cdl", [P, 8, C], BF16)

    scIL = []
    kv_all = scope_alloc(scIL, "kv_all", [65, C], F32)
    KSb = scope_alloc(scIL, "KSb", [P, C], F32)
    w2h = scope_alloc(scIL, "w2h", [P, 8, C], BF16)
    w2l = scope_alloc(scIL, "w2l", [P, 8, C], BF16)
    scILb = []
    wpst = [scope_alloc(scILb, f"wpst{j}", [64, C], F32) for j in range(4)]

    if taps:
        dma("gpsimd", tap["t_kvpart"][:, :], kvacc[:, :], sSt,
            waits=[R.at("a2_kv_done")])
    dma("gpsimd", cc_in[:, :], kvacc[:, :], sSt, waits=[R.at("a2_kv_done")])
    st_cc = sSt.n
    R.op("gpsimd", lambda e: e.collective_compute(
        "AllReduce", ALU.add, replica_groups=[[0, 1, 2, 3], [4, 5, 6, 7]],
        ins=[cc_in[:, :]], outs=[cc_out[:, :]]),
        waits=[(sSt, st_cc)], incs=[(sCC, 1)])
    dma("gpsimd", kv_all[:, :], cc_out[:, :], sSt,
        waits=[(sCC, 1)] + post_a2_gate())
    st_kv = sSt.n
    if taps:
        dma("gpsimd", tap["t_kvall"][:, :], kv_all[:, :], sSt, waits=[(sSt, st_kv)])

    dma("sync", KSb[0:1, :], kv_all[64:65, :], sTr,
        waits=[(sSt, st_kv)] + post_a2_gate())
    kk = 1
    while kk < P:
        step = min(kk, P - kk)
        dma("sync", KSb[kk:kk + step, :], KSb[0:step, :], sTr,
            waits=[(sTr, sTr.n)])
        kk += step
    tr_ks = sTr.n
    if taps:
        dma("gpsimd", tap["t_ksb"][:, :], KSb[:, :], sSt, waits=[(sTr, tr_ks)])

    w2_ldt, w2_mmt = {}, {}

    def w2_load(h):
        w_wp = []
        if h == 0:
            w_wp.append((sSt, st_kv))
            w_wp.extend(post_a2_gate())
        if h >= 4:
            w_wp.append((sT, w2_mmt[h - 4]))
        dma("sync", wpst[h % 4][:, :], wp_d[h * 64:(h + 1) * 64, :], sLd,
            waits=w_wp)
        w2_ldt[h] = sLd.n

    for h in range(4):
        w2_load(h)
    for pr in range(8):
        for hh in range(2):
            h = 2 * pr + hh
            for nch2 in range(2):
                w_mm = [(sLd, w2_ldt.pop(h))] if nch2 == 0 else []
                if pr >= 1 and hh == 0 and nch2 == 0:
                    w_mm.append(R.at(f"w2evac_{pr - 1}"))
                R.op("tensor", lambda e,
                     o=psW[hh * 64:hh * 64 + 64, nch2 * 512:(nch2 + 1) * 512],
                     l=kv_all[0:64, h * 64:(h + 1) * 64],
                     r=wpst[h % 4][:, nch2 * 512:(nch2 + 1) * 512]:
                     e.matmul(o, l, r, start=True, stop=True),
                     waits=w_mm,
                     incs=[(sT, 1)] if nch2 == 1 else [])
            w2_mmt[h] = sT.n
            if h + 4 < 16:
                w2_load(h + 4)
        t_w2 = sT.n
        R.op("vector", lambda e, o=w2h[:, pr, :], i=psW[:, :]: e.tensor_copy(o, i),
             waits=[(sT, t_w2)], prepad=True)
        R.op("vector", lambda e, o=w2l[:, pr, :], a=psW[:, :], b=w2h[:, pr, :]:
             e.tensor_tensor(o, a, b, ALU.subtract), incs=[(sV, 1)], fence=True)
        R.mark(f"w2evac_{pr}", sV)
        if taps and pr == 0:
            dma("gpsimd", tap["t_w2h"][:, :], w2h[:, 0, :], sSt,
                waits=[R.at("w2evac_0")])
            dma("gpsimd", tap["t_w2l"][:, :], w2l[:, 0, :], sSt)
    R.mark("w2_ready", sV)
    R.bankprev["c2"] = R.at("w2_ready")
    R.bankprev["c3"] = R.at("w2_ready")
    scope_exit(scILb)

    scB = []
    BP_t = scope_alloc(scB, "BPt", [P, C], F32)
    nmb = scope_alloc(scB, "nmb", [P, C], F32)
    n16 = [scope_alloc(scB, f"n16_{j}", [P, H], F32) for j in range(2)]
    n16r = [scope_alloc(scB, f"n16r_{j}", [P, H], F32) for j in range(2)]
    qnh_b = [scope_alloc(scB, f"qnh{j}", [P, C], BF16) for j in range(2)]
    qnl_b = [scope_alloc(scB, f"qnl{j}", [P, C], BF16) for j in range(2)]
    bfh = [scope_alloc(scB, f"bfh{j}", [P, 8, P], BF16) for j in range(2)]
    bfl = [scope_alloc(scB, f"bfl{j}", [P, 8, P], BF16) for j in range(2)]

    dma("sync", BP_t[:, :], bpt_d[:, :], sLd, waits=post_a2_gate())
    ld_bp = sLd.n

    def load_cw(third, waits):
        n0, n1 = N3[third]
        w = list(waits)
        last = 0
        for buf, spl in ((cgh, wgh_sp), (cgl, wgl_sp), (cuh, wuh_sp), (cul, wul_sp)):
            for kc in range(8):
                dma("sync", buf[:, kc, 0:n1 - n0], spl[:, kc, n0:n1], sLd, waits=w)
                w = []
        for buf, spl in ((cdh, wdh_sp), (cdl, wdl_sp)):
            for j in range((n1 - n0) // P):
                kc = n0 // P + j
                dma("sync", buf[:, j, :], spl[:, kc, :], sLd)
        return sLd.n


    b_ldt = {}
    bbanks = [pp[0], pp[1], psD[0], psD[1]]
    R.bankprev["b0"] = R.bankprev.pop(0)
    R.bankprev["b1"] = R.bankprev.pop(1)

    def b_loads(i):
        if i >= 2:
            w_q1 = [R.at(f"b_q1free_{i - 2}")]
        else:
            w_q1 = post_a2_gate()
        dma("sync", f2[i % 2][:, :], q1_sp[i * P:(i + 1) * P, :], sLd, waits=w_q1)
        w_x = [(sLd, ld_bp)]
        if i >= 2:
            w_x.append(R.at(f"b_xfree_{i - 2}"))
        dma("sync", x2[i % 2][:, :], x_d[i * P:(i + 1) * P, :], sLd, waits=w_x)
        b_ldt[i] = sLd.n

    def b_tail(j, st):
        xt, pout, t_bs = st
        for nch in range(2):
            bk = (2 * j + nch) % 4
            w_ev = [(sT, t_bs[nch])]
            if nch == 0 and j >= 2:
                w_ev.append(R.at(f"b_poutfree_{j - 2}"))
            R.op("vector", lambda e, o=pout[:, nch * 512:(nch + 1) * 512],
                 a=bbanks[bk][:, :], b=xt[:, nch * 512:(nch + 1) * 512]:
                 e.scalar_tensor_tensor(o, a, 1.0, b, ALU.mult, ALU.add),
                 waits=w_ev, incs=[(sV, 1)], prepad=True)
            bank_release(f"b{bk}", sV, sV.n)
        v_pout = sV.n
        R.mark(f"b_xfree_{j}", sV)
        dma("gpsimd", part_sp[j * P:(j + 1) * P, :], pout[:, :], sSt,
            waits=[(sV, v_pout)])
        R.mark(f"b_poutfree_{j}", sSt)
        if taps and j < 2:
            dma("gpsimd", tap["t_part"][j * P:(j + 1) * P, :], pout[:, :], sSt)
            R.mark(f"b_poutfree_{j}", sSt)

    b_loads(0)
    b_loads(1)
    ld_cw0 = load_cw(0, post_a2_gate())
    if taps:
        dma("gpsimd", tap["t_cgh"][:, :, :], cgh[:, :, :], sSt, waits=[(sLd, ld_cw0)])
        dma("gpsimd", tap["t_cgl"][:, :, :], cgl[:, :, :], sSt)
        dma("gpsimd", tap["t_cdh"][:, :, :], cdh[:, :, :], sSt)
        dma("gpsimd", tap["t_cdl"][:, :, :], cdl[:, :, :], sSt)
    b_prev = None
    for i in range(NTIL):
        idx = i % 2
        q1t = f2[idx]
        xt = x2[i % 2]
        pout = e2[idx]
        ld_b = b_ldt.pop(i)

        w_nm = [(sLd, ld_b), (sTr, tr_ks)]
        if i == 1:
            w_nm += R.wtap("tf_qn")
        R.op("vector", lambda e, o=nmb[:, :], a=q1t, b=KSb: e.tensor_tensor(
            o, a[:, :], b[:, :], ALU.mult), waits=w_nm)
        R.op("vector", lambda e, o=n16[idx][:, :], i=nmb[:, :]: e.tensor_reduce(
            o, i.rearrange("p (h d) -> p h d", d=64), AX.X, ALU.add),
            incs=[(sV, 1)], pad=True)
        R.op("vector", lambda e, o=n16[idx][:, :], i=n16[idx][:, :]: e.tensor_scalar(
            o, i, 1e-8, None, ALU.add), waits=[(sV, sV.n)], incs=[(sV, 1)],
            pad=True)
        R.op("vector", lambda e, o=n16r[idx][:, :], i=n16[idx][:, :]: e.reciprocal(o, i),
             waits=[(sV, sV.n)], incs=[(sV, 1)])
        if taps and i == 0:
            dma("gpsimd", tap["t_norm"][:, :], n16r[idx][:, :], sSt,
                waits=[(sV, sV.n)])
        R.op("vector", lambda e, o=nmb, a=q1t, b=n16r[idx]: e.tensor_tensor(
            o[:, :].rearrange("p (h d) -> p h d", d=64),
            a[:, :].rearrange("p (h d) -> p h d", d=64),
            b[:, :].broadcast_to((P, H, 64)), ALU.mult), waits=[(sV, sV.n)],
            incs=[(sV, 1)])
        R.mark(f"b_q1free_{i}", sV)
        if taps and i == 0:
            dma("gpsimd", tap["t_qn"][:, :], nmb[:, :], sSt,
                waits=[R.at("b_q1free_0")])
            R.mark("tf_qn", sSt)
        w_qn = [R.at(f"b_qnfree_{i - 2}")] if i >= 2 else []
        R.op("vector", lambda e, o=qnh_b[idx], i=nmb: e.tensor_copy(o[:, :], i[:, :]),
             waits=w_qn)
        R.op("vector", lambda e, o=qnl_b[idx], a=nmb, b=qnh_b[idx]: e.tensor_tensor(
            o[:, :], a[:, :], b[:, :], ALU.subtract), incs=[(sV, 1)], fence=True)
        v_qn = sV.n
        w_bf = [R.at(f"b_bffree_{i - 2}")] if i >= 2 else []
        dma("sync", bfh[idx][:, :, :], qnh_b[idx][:, :], sTr,
            waits=[(sV, v_qn)] + w_bf, transpose=True)
        dma("sync", bfl[idx][:, :, :], qnl_b[idx][:, :], sTr, transpose=True)
        tr_b = sTr.n
        R.ck[f"b_qnfree_{i}"] = (sTr, tr_b)
        if taps and i == 0:
            dma("gpsimd", tap["t_qnh"][:, :], qnh_b[idx][:, :], sSt,
                waits=[(sV, v_qn)])
            dma("gpsimd", tap["t_qnl"][:, :], qnl_b[idx][:, :], sSt)
            dma("gpsimd", tap["t_bfh"][:, :, :], bfh[idx][:, :, :], sSt,
                waits=[(sTr, tr_b)])
            R.ck[f"b_qnfree_{i}"] = (sSt, sSt.n)

        w_px = [(sLd, ld_b)]
        R.op("vector", lambda e, o=xt, a=xt, b=BP_t: e.tensor_tensor(
            o[:, :], a[:, :], b[:, :], ALU.add), waits=w_px, incs=[(sV, 1)])

        t_bs = []
        for nch in range(2):
            bk = (2 * i + nch) % 4
            t_b = proj3(bbanks[bk], f"b{bk}", bfh[idx], bfl[idx], w2h, w2l, 512,
                        np.s_[nch * 512:(nch + 1) * 512],
                        [R.at("w2_ready"), (sTr, tr_b)])
            t_bs.append(t_b)
        R.mark(f"b_bffree_{i}", sT)

        st_here = (xt, pout, t_bs)
        if b_prev is not None:
            b_tail(i - 1, b_prev)
        b_prev = st_here
        if 2 <= i + 1 < NTIL:
            b_loads(i + 1)

    b_tail(NTIL - 1, b_prev)
    R.bankprev[0] = R.bankprev["b0"]
    R.bankprev[1] = R.bankprev["b1"]
    R.bankprev["D0_0"] = R.bankprev["b2"]
    R.bankprev["D1_0"] = R.bankprev["b3"]

    R.mark("b_T_done", sT)
    R.mark("b_V_done", sV)
    scope_exit(scB)
    scope_exit(scIL)

    scC = []
    qpC = {
        "yq": [scope_alloc(scC, f"yqC{j}", [P, 1024], F32) for j in range(2)],
        "sb": [scope_alloc(scC, f"sbC{j}", [P, 1024], BF16) for j in range(7)],
        "ub": [scope_alloc(scC, f"ubC{j}", [P, 1024], BF16) for j in range(2)],
        "sg": [scope_alloc(scC, "sgC0", [P, 1024], BF16)] * 2,
    }
    qpC["aa"] = qpC["yq"]
    xfh = [scope_alloc(scC, f"xfh{j}", [P, 8, P], BF16) for j in range(2)]
    xfl = [scope_alloc(scC, f"xfl{j}", [P, 8, P], BF16) for j in range(2)]
    hqh_b = [scope_alloc(scC, f"hqh{j}", [P, 1024], BF16) for j in range(2)]
    hql_b = [scope_alloc(scC, f"hql{j}", [P, 1024], BF16) for j in range(2)]
    hfh = [scope_alloc(scC, f"hfh{j}", [P, 8, P], BF16) for j in range(4)]
    hfl = [scope_alloc(scC, f"hfl{j}", [P, 8, P], BF16) for j in range(4)]

    R.bankprev["D0_1"] = R.at("a2_kv_done")
    R.bankprev["D1_1"] = R.at("a2_kv_done")
    R.bankprev["c0"] = R.bankprev.get(0)
    R.bankprev["c1"] = R.bankprev.get(1)
    if R.bankprev["c0"] is None:
        del R.bankprev["c0"]
    if R.bankprev["c1"] is None:
        del R.bankprev["c1"]

    def c_down(third, i, st_c):
        TN, kch, idx4, tr_h, pin, ld_pin = st_c
        dbanks = [psD[0], psD[1], ps_kv[:, 0:512], ps_kv[:, 512:1024]]
        for mch in range(2):
            bank = dbanks[mch + 2 * (i % 2)]
            triples = []
            for kc in range(kch):
                triples.append((bank[:, :], hfh[idx4][:, kc, :],
                                cdh[:, kc, mch * 512:(mch + 1) * 512]))
                triples.append((bank[:, :], hfh[idx4][:, kc, :],
                                cdl[:, kc, mch * 512:(mch + 1) * 512]))
                triples.append((bank[:, :], hfl[idx4][:, kc, :],
                                cdh[:, kc, mch * 512:(mch + 1) * 512]))
            mm_group(bank, f"D{mch}_{i % 2}", triples, [(sTr, tr_h)])
            t_d = sT.n
            R.op("vector", lambda e, o=pin[:, mch * 512:(mch + 1) * 512],
                 a=bank[:, :], b=pin[:, mch * 512:(mch + 1) * 512]:
                 e.scalar_tensor_tensor(o, a, 1.0, b, ALU.mult, ALU.add),
                 waits=[(sT, t_d), (sLd, ld_pin)], incs=[(sV, 1)], prepad=True)
            bank_release(f"D{mch}_{i % 2}", sV, sV.n)
        v_p = sV.n
        R.mark(f"c_hffree_{third}_{i}", sT)
        if third < 2:
            dma("gpsimd", part_sp[i * P:(i + 1) * P, :], pin[:, :], sSt,
                waits=[(sV, v_p)])
        else:
            dma("gpsimd", y_d[i * P:(i + 1) * P, :], pin[:, :], sSt,
                waits=[(sV, v_p)])
        R.mark(f"c_pdone_{third}_{i}", sSt)
        R.mark(f"c_pinfree_{third}_{i}", sSt)

    for third in range(3):
        n0, n1 = N3[third]
        TN = n1 - n0
        kch = TN // P
        if third > 0:
            ld_cw = load_cw(third, [R.at(f"c_T_done_{third - 1}")])
        else:
            ld_cw = ld_cw0
        nck = [(j * 512, min(TN, (j + 1) * 512)) for j in range((TN + 511) // 512)]
        nnc = len(nck)
        c_pending = []
        ldt = {}

        def c_loads(i):
            idx = i % 2
            idx4 = i % 4
            if i >= 2:
                w_xf = [R.at(f"c_xffree_{third}_{i - 2}")]
            elif third > 0:
                w_xf = [R.at(f"c_xffree_{third - 1}_{NTIL - 2 + i}")]
            else:
                w_xf = [R.at("b_V_done"), R.at("b_T_done")]
            if i == 0:
                w_xf.append((sLd, ld_cw))
            dma("sync", xfh[idx][:, :, :], xqh_sp[i, :, :, :], sLd, waits=w_xf)
            dma("sync", xfl[idx][:, :, :], xql_sp[i, :, :, :], sLd)
            ld_c = sLd.n
            if third == 0:
                w_pin = [R.at(f"b_poutfree_{max(i, NTIL - 2 + (i % 2))}")]
            else:
                w_pin = [R.at(f"c_pdone_{third - 1}_{i}")]
            if i >= 4:
                w_pin.append(R.at(f"c_pinfree_{third}_{i - 4}"))
            elif third > 0:
                w_pin.append(R.at(f"c_pinfree_{third - 1}_{NTIL - 4 + i}"))
            dma("sync", e2[idx4][:, :], part_sp[i * P:(i + 1) * P, :], sLd,
                waits=w_pin)
            ldt[i] = (ld_c, sLd.n)

        c_loads(0)
        for i in range(NTIL):
            idx = i % 2
            idx4 = i % 4
            hbuf = g2[idx]
            sgt = x2[idx]
            upt = f2[idx]
            pin = e2[idx4]
            ld_c, ld_pin = ldt.pop(i)
            if i + 1 < NTIL:
                c_loads(i + 1)

            cbanks = [pp[0], pp[1], psW[:, 0:512], psW[:, 512:1024]]
            for gu, (wbh, wbl) in enumerate(((cgh, cgl), (cuh, cul))):
                dst = sgt if gu == 0 else upt
                for ncj, (c0, c1) in enumerate(nck):
                    seq = gu * nnc + ncj
                    bank = cbanks[seq % 4]
                    t_c = proj3(bank, f"c{seq % 4}", xfh[idx], xfl[idx], wbh, wbl,
                                c1 - c0, np.s_[c0:c1], [(sLd, ld_c)])
                    w_ev = [(sT, t_c)]
                    if ncj == 0:
                        if third == 0 and i < 2:
                            w_ev.append(R.at("b_V_done"))
                        elif i >= 2:
                            w_ev.append(R.at(f"c_sufree_{third}_{i - 2}"))
                        elif third > 0:
                            w_ev.append(R.at(f"c_sufree_{third - 1}_{NTIL - 2 + i}"))
                    R.op("scalar", lambda e, o=dst[:, c0:c1], i=bank[:, 0:c1 - c0],
                         f=(AF.Silu if gu == 0 else AF.Copy):
                         e.activation(o, i, f), waits=w_ev, incs=[(sS, 1)],
                         prepad=True)
                    bank_release(f"c{seq % 4}", sS, sS.n)
            s_gu = sS.n
            R.mark(f"c_xffree_{third}_{i}", sT)

            if len(c_pending) >= 2:
                c_down(third, i - 2, c_pending.pop(0))

            w_h = []
            if i >= 1:
                w_h = [R.at(f"c_hfree_{third}_{i - 1}")]
            elif third > 0:
                w_h = [R.at(f"c_hfree_{third - 1}_{NTIL - 1}")]
            if third == 0 and i == 1:
                w_h += R.wtap("tf_h")
            R.op("vector", lambda e, o=hbuf[:, 0:TN], a=sgt[:, 0:TN], b=upt[:, 0:TN]:
                 e.tensor_tensor(o, a, b, ALU.mult), waits=[(sS, s_gu)] + w_h,
                 incs=[(sV, 1)])
            R.mark(f"c_sufree_{third}_{i}", sV)
            v_h = sV.n
            if taps and third == 0 and i == 0:
                dma("gpsimd", tap["t_h"][:, 0:TN], hbuf[:, 0:TN], sSt,
                    waits=[(sV, v_h)])
                R.mark("tf_h", sSt)

            w_qv = [(sV, v_h)]
            if third == 0 and i < 2:
                w_qv.append(R.at("b_V_done"))
                w_qv.append(R.at("b_T_done"))
            rq_t = quant_chain(qpC, "C", hbuf[:, 0:TN], TN, idx,
                               hqh_b[idx][:, 0:TN], hql_b[idx][:, 0:TN], w_qv)
            R.mark(f"c_hfree_{third}_{i}", sV)

            w_hf = []
            if i >= 4:
                w_hf = [R.at(f"c_hffree_{third}_{i - 4}")]
            elif third > 0:
                w_hf = [R.at(f"c_hffree_{third - 1}_{NTIL - 4 + i}")]
            else:
                w_hf = [R.at("b_T_done")]
            dma("sync", hfh[idx4][:, 0:kch, :], hqh_b[idx][:, 0:TN], sTr,
                waits=[rq_t] + w_hf, transpose=True)
            dma("sync", hfl[idx4][:, 0:kch, :], hql_b[idx][:, 0:TN], sTr,
                transpose=True)
            tr_h = sTr.n
            c_pending.append((TN, kch, idx4, tr_h, pin, ld_pin))

        for joff, st_c in enumerate(c_pending):
            c_down(third, NTIL - len(c_pending) + joff, st_c)
        c_pending = []
        R.mark(f"c_T_done_{third}", sT)

    scope_exit(scC)
    scope_exit(scCW)

    final_st = sSt.n
    R.op("gpsimd", None, waits=[(sSt, final_st)])

    with ExitStack() as es:
        for s in R.sems:
            s.h = es.enter_context(nc.semaphore(s.name))
        with nc.Block() as block:
            @block.sync
            def _(e):
                R.replay(e, "sync")

            @block.scalar
            def _(e):
                R.replay(e, "scalar")

            @block.vector
            def _(e):
                R.replay(e, "vector")

            @block.tensor
            def _(e):
                R.replay(e, "tensor")

            @block.gpsimd
            def _(e):
                R.replay(e, "gpsimd")

    return nc


_NC_CACHE = {}


def _get_nc(taps=False):
    key = bool(taps)
    if key not in _NC_CACHE:
        _NC_CACHE[key] = build_program(taps=key)
    return _NC_CACHE[key]


def make_inputs(L0_attn_x, L0_attn_Wqkv, L0_attn_Wg, L0_attn_bg, L0_attn_Wp,
                L0_attn_bp, L0_ffn_w_gate, L0_ffn_w_up, L0_ffn_w_down):
    x = np.ascontiguousarray(np.asarray(L0_attn_x, dtype=np.float32))
    wqkv = np.ascontiguousarray(np.asarray(L0_attn_Wqkv, dtype=np.float32))
    wg = np.ascontiguousarray(np.asarray(L0_attn_Wg, dtype=np.float32))
    bg = np.asarray(L0_attn_bg, dtype=np.float32)
    wp = np.ascontiguousarray(np.asarray(L0_attn_Wp, dtype=np.float32))
    bp = np.asarray(L0_attn_bp, dtype=np.float32)
    wgt = np.ascontiguousarray(np.asarray(L0_ffn_w_gate, dtype=np.float32))
    wup = np.ascontiguousarray(np.asarray(L0_ffn_w_up, dtype=np.float32))
    wdn = np.ascontiguousarray(np.asarray(L0_ffn_w_down, dtype=np.float32))

    bgt = np.ascontiguousarray(np.tile(bg[None, :], (P, 1)))
    bpt = np.ascontiguousarray(np.tile(bp[None, :], (P, 1)))
    csts = np.zeros((P, 8), dtype=np.float32)
    csts[:, 0] = 1e-5
    for j, t in enumerate(THR):
        csts[:, 1 + j] = -t

    in_maps = []
    for c in range(8):
        b, qt = divmod(c, 4)
        xs = np.ascontiguousarray(x[b, qt * NT:(qt + 1) * NT, :])
        in_maps.append({
            "x": xs, "wqkv": wqkv, "wg": wg, "wp": wp,
            "wgate": wgt, "wup": wup, "wdown": wdn,
            "bgt": bgt, "bpt": bpt, "csts_d": csts,
        })
    return in_maps


def kernel(L0_attn_x, L0_attn_Wqkv, L0_attn_Wg, L0_attn_bg, L0_attn_Wp,
           L0_attn_bp, L0_ffn_w_gate, L0_ffn_w_up, L0_ffn_w_down):
    in_maps = make_inputs(L0_attn_x, L0_attn_Wqkv, L0_attn_Wg, L0_attn_bg,
                          L0_attn_Wp, L0_attn_bp, L0_ffn_w_gate, L0_ffn_w_up,
                          L0_ffn_w_down)
    nc = _get_nc(taps=False)
    res = run_bass_kernel_spmd(nc, in_maps, core_ids=list(range(8)))
    y = np.empty((2, 4 * NT, C), dtype=np.float32)
    for c in range(8):
        b, qt = divmod(c, 4)
        y[b, qt * NT:(qt + 1) * NT, :] = res.results[c]["y"]
    return y


# version 23 (final)
# speedup vs baseline: 1.0542x; 1.0098x over previous
import numpy as np
from contextlib import ExitStack

import concourse.bass as bass
import concourse.mybir as mybir
from concourse.bass_utils import run_bass_kernel_spmd

F32 = mybir.dt.float32
BF16 = mybir.dt.bfloat16
AF = mybir.ActivationFunctionType
ALU = mybir.AluOpType
AX = mybir.AxisListType

P = 128
C = 1024
NT = 4096
NTIL = NT // P
H = 16
Dh = 64
I = 2816
N3 = [(0, 1024), (1024, 2048), (2048, 2816)]
THR = [0.25, 0.75, 1.25, 1.75, 2.5, 3.5, 5.0]

ENGS = ("sync", "scalar", "vector", "tensor", "gpsimd")


class Sem:
    __slots__ = ("name", "n", "h")

    def __init__(self, name):
        self.name, self.n, self.h = name, 0, None


class Rec:

    def __init__(self):
        self.ops = {e: [] for e in ENGS}
        self.sems = []
        self.ck = {}
        self.bankprev = {}
        self.jnk = None
        self.jnkS = None

    def sem(self, name):
        s = Sem(name)
        self.sems.append(s)
        return s

    def _junk(self, eng):
        if eng == "scalar":
            j = self.jnkS
            return lambda e, o=j: e.memzero(o)
        j = self.jnk
        return lambda e, o=j: e.memset(o, 1.0)

    def op(self, eng, fn, waits=(), incs=(), pad=False, prepad=False, fence=False):
        w = tuple((s, v) for s, v in waits if v > 0)
        if prepad:
            self.ops[eng].append((w, self._junk(eng), ()))
        if fence:
            self.ops[eng].append((w, fn, ()))
            for s, k in incs:
                s.n += k
            self.ops[eng].append(((), self._junk(eng), tuple(incs)))
        else:
            for s, k in incs:
                s.n += k
            self.ops[eng].append((w, fn, tuple(incs)))
        if pad:
            self.ops[eng].append(((), self._junk(eng), ()))

    def mark(self, name, sem):
        self.ck[name] = (sem, sem.n)

    def at(self, name):
        return self.ck[name]

    def wtap(self, name):
        return [self.ck[name]] if name in self.ck else []

    def replay(self, e, eng):
        lastw = {}
        for waits, fn, incs in self.ops[eng]:
            for s, v in waits:
                if lastw.get(id(s), 0) >= v:
                    continue
                e.wait_ge(s.h, v)
                lastw[id(s)] = v
            ins = fn(e) if fn is not None else None
            for s, k in incs:
                ins.then_inc(s.h, k)


def build_program(taps=False):
    nc = bass.Bass("TRN2", target_bir_lowering=False)

    x_d = nc.dram_tensor("x", [NT, C], F32, kind="ExternalInput").ap()
    wqkv_d = nc.dram_tensor("wqkv", [C, 3 * C], F32, kind="ExternalInput").ap()
    wg_d = nc.dram_tensor("wg", [C, C], F32, kind="ExternalInput").ap()
    wp_d = nc.dram_tensor("wp", [C, C], F32, kind="ExternalInput").ap()
    wgate_d = nc.dram_tensor("wgate", [I, C], F32, kind="ExternalInput").ap()
    wup_d = nc.dram_tensor("wup", [I, C], F32, kind="ExternalInput").ap()
    wdown_d = nc.dram_tensor("wdown", [C, I], F32, kind="ExternalInput").ap()
    bgt_d = nc.dram_tensor("bgt", [P, C], F32, kind="ExternalInput").ap()
    bpt_d = nc.dram_tensor("bpt", [P, C], F32, kind="ExternalInput").ap()
    csts_d = nc.dram_tensor("csts_d", [P, 8], F32, kind="ExternalInput").ap()
    y_d = nc.dram_tensor("y", [NT, C], F32, kind="ExternalOutput").ap()

    gates_sp = nc.dram_tensor("gates_sp", [NT, C], F32, kind="Internal").ap()
    q1_sp = nc.dram_tensor("q1_sp", [NT, C], F32, kind="Internal").ap()
    part_sp = nc.dram_tensor("part_sp", [NT, C], F32, kind="Internal").ap()
    xnh_sp = nc.dram_tensor("xnh_sp", [NTIL, P, 8, P], BF16, kind="Internal").ap()
    xnl_sp = nc.dram_tensor("xnl_sp", [NTIL, P, 8, P], BF16, kind="Internal").ap()
    xqh_sp = nc.dram_tensor("xqh_sp", [NTIL, P, 8, P], BF16, kind="Internal").ap()
    xql_sp = nc.dram_tensor("xql_sp", [NTIL, P, 8, P], BF16, kind="Internal").ap()
    wgh_sp = nc.dram_tensor("wgh_sp", [P, 8, I], BF16, kind="Internal").ap()
    wgl_sp = nc.dram_tensor("wgl_sp", [P, 8, I], BF16, kind="Internal").ap()
    wuh_sp = nc.dram_tensor("wuh_sp", [P, 8, I], BF16, kind="Internal").ap()
    wul_sp = nc.dram_tensor("wul_sp", [P, 8, I], BF16, kind="Internal").ap()
    wdh_sp = nc.dram_tensor("wdh_sp", [P, 22, C], BF16, kind="Internal").ap()
    wdl_sp = nc.dram_tensor("wdl_sp", [P, 22, C], BF16, kind="Internal").ap()
    cc_in = nc.dram_tensor("cc_in", [65, C], F32, kind="Internal").ap()
    cc_out = nc.dram_tensor("cc_out", [65, C], F32, kind="Internal").ap()

    tap = {}
    if taps:
        for nm, shp, dt in [("t_xn", [640, C], F32), ("t_gates", [640, C], F32),
                            ("t_q1", [640, C], F32), ("t_k1", [640, C], F32),
                            ("t_kvpart", [65, C], F32), ("t_kvall", [65, C], F32),
                            ("t_part", [256, C], F32), ("t_h", [P, 1024], F32),
                            ("t_xqh", [640, C], BF16), ("t_xql", [640, C], BF16),
                            ("t_ksb", [P, C], F32), ("t_norm", [P, H], F32),
                            ("t_qn", [P, C], F32), ("t_w2h", [P, C], BF16),
                            ("t_w2l", [P, C], BF16), ("t_qnh", [P, C], BF16),
                            ("t_qnl", [P, C], BF16), ("t_bfh", [P, 8, P], BF16),
                            ("t_cgh", [P, 8, 1024], BF16), ("t_cgl", [P, 8, 1024], BF16),
                            ("t_cdh", [P, 8, C], BF16), ("t_cdl", [P, 8, C], BF16),
                            *[(f"t_dbg{k}", [P, 1], F32) for k in range(15)],
                            ("t_psph", [P, 1408], BF16), ("t_pspl", [P, 1408], BF16),
                            ("t_pyq", [P, 1408], F32), ("t_paa", [P, 1408], F32)]:
            tap[nm] = nc.dram_tensor(nm, shp, dt, kind="ExternalOutput").ap()

    R = Rec()
    sLd = R.sem("sLd")
    sTr = R.sem("sTr")
    sSt = R.sem("sSt")
    sS = R.sem("sS")
    sV = R.sem("sV")
    sT = R.sem("sT")
    sCC = R.sem("sCC")
    sP = R.sem("sP")

    def sb(name, shape, dt):
        g = nc.sbuf_tensor(name, shape, dt)
        t = g.__enter__()
        return t.ap(), g

    def scope_alloc(scope, name, shape, dt):
        ap, g = sb(name, shape, dt)
        scope.append(g)
        return ap

    def scope_exit(scope):
        for g in reversed(scope):
            g.__exit__(None, None, None)
        scope.clear()

    csts, _ = sb("csts", [P, 8], F32)
    f2 = [sb(f"f2_{j}", [P, C], F32)[0] for j in range(2)]
    g2 = [sb(f"g2_{j}", [P, C], F32)[0] for j in range(2)]
    acc1 = [sb(f"acc1_{j}", [P, 1], F32)[0] for j in range(2)]
    acc2 = [sb(f"acc2_{j}", [P, 1], F32)[0] for j in range(2)]
    tmean = [sb(f"tmean_{j}", [P, 1], F32)[0] for j in range(2)]
    tvar = [sb(f"tvar_{j}", [P, 1], F32)[0] for j in range(2)]
    trstd = [sb(f"trstd_{j}", [P, 1], F32)[0] for j in range(2)]
    tnmr = [sb(f"tnmr_{j}", [P, 1], F32)[0] for j in range(2)]
    ln_a = [sb(f"ln_a{j}", [P, 1], F32)[0] for j in range(2)]
    ln_b = [sb(f"ln_b{j}", [P, 1], F32)[0] for j in range(2)]
    red2 = [sb(f"red2_{j}", [P, 88], F32)[0] for j in range(2)]
    sq2 = [sb(f"sq2_{j}", [P, 88], F32)[0] for j in range(2)]
    rq2 = [sb(f"rq2_{j}", [P, 88], F32)[0] for j in range(2)]
    kvacc, _ = sb("kvacc", [65, C], F32)
    jnkt, _ = sb("jnkt", [P, 8], F32)
    jnkS, _ = sb("jnkS", [P, 8], F32)
    R.jnk = jnkt[:, :]
    R.jnkS = jnkS[:, :]

    eps_c = csts[:, 0:1]
    thr_c = [csts[:, 1 + j:2 + j] for j in range(7)]

    ps_kv = nc.psum_tensor("ps_kv", [P, C], F32).__enter__().ap()
    pp = [nc.psum_tensor(f"pp{j}", [P, 512], F32).__enter__().ap() for j in range(2)]
    psW = nc.psum_tensor("psW", [P, C], F32).__enter__().ap()
    psD = [nc.psum_tensor(f"psD{j}", [P, 512], F32).__enter__().ap() for j in range(2)]

    def dma(eng, out, in_, sem, waits=(), transpose=False):
        if transpose:
            R.op(eng, lambda e, o=out, i=in_: e.dma_start(out=o, in_=i, transpose=True),
                 waits=waits, incs=[(sem, 16)])
        else:
            R.op(eng, lambda e, o=out, i=in_: e.dma_start(out=o, in_=i),
                 waits=waits, incs=[(sem, 16)])

    def mm_group(bank, bankkey, triples, waits_first):
        w0 = list(waits_first)
        prev = R.bankprev.get(bankkey)
        if prev is not None:
            w0.append(prev)
        n = len(triples)
        for t, (o, l, r) in enumerate(triples):
            R.op("tensor",
                 lambda e, o=o, l=l, r=r, s=(t == 0), p=(t == n - 1):
                 e.matmul(o, l, r, start=s, stop=p),
                 waits=w0 if t == 0 else [],
                 incs=[(sT, 1)] if t == n - 1 else [])

    def bank_release(bankkey, sem, val):
        R.bankprev[bankkey] = (sem, val)

    def proj3(bank, bankkey, lh, ll, wh, wl, ncols, csl, waits_first):
        triples = []
        for cch in range(8):
            triples.append((bank[:, 0:ncols], lh[:, cch, :], wh[:, cch, csl]))
            triples.append((bank[:, 0:ncols], lh[:, cch, :], wl[:, cch, csl]))
            triples.append((bank[:, 0:ncols], ll[:, cch, :], wh[:, cch, csl]))
        mm_group(bank, bankkey, triples, waits_first)
        return sT.n

    qprev = {}

    def quant_stage1(pool, key, src, width, idx, waits_v):
        nb = width // 16
        red, sq, rq = red2[idx][:, 0:nb], sq2[idx][:, 0:nb], rq2[idx][:, 0:nb]
        yq = pool["yq"][idx][:, 0:width]
        prev = [qprev[key]] if key in qprev else []
        R.op("vector", lambda e, r=red, s=src: e.tensor_reduce(
            r, s.rearrange("p (b k) -> p b k", k=16), AX.X, ALU.max,
            apply_absolute_value=True), waits=list(waits_v) + prev, pad=True)
        R.op("vector", lambda e, o=sq, i=red: e.tensor_scalar(
            o, i, 1.0 / 6.0, 1e-12, ALU.mult, ALU.max), incs=[(sV, 1)])
        v_sq = sV.n
        R.op("vector", lambda e, o=rq, i=sq: e.reciprocal(o, i),
             waits=[(sV, v_sq)], incs=[(sV, 1)])
        v_rq = sV.n
        R.op("vector", lambda e, o=yq, s=src, r=rq, k=nb: e.tensor_tensor(
            o.rearrange("p (b k) -> p b k", k=16),
            s.rearrange("p (b k) -> p b k", k=16),
            r.broadcast_to((P, k, 16)), ALU.mult), waits=[(sV, v_rq)],
            incs=[(sV, 1)])
        return {"v_yq": sV.n, "v_sq": v_sq, "sq": sq, "yq": yq,
                "idx": idx, "width": width, "nb": nb}

    def quant_stage2(pool, key, st, dst_h, dst_l, waits2=(), use_pool=True):
        width, nb, idx = st["width"], st["nb"], st["idx"]
        yq, sq = st["yq"], st["sq"]
        aa = pool["aa"][idx][:, 0:width]
        sb7 = [b[:, 0:width] for b in pool["sb"]]
        ub = [b[:, 0:width] for b in pool["ub"]]
        sgb = pool["sg"][idx][:, 0:width]
        prev = [qprev[key]] if key in qprev else []
        R.op("scalar", lambda e, o=sgb, i=yq: e.activation(o, i, AF.Sign),
             waits=[(sV, st["v_yq"])] + prev)
        R.op("scalar", lambda e, o=aa, i=yq: e.activation(o, i, AF.Abs))
        for t in range(4):
            R.op("scalar", lambda e, o=sb7[t], i=aa, b=thr_c[t]: e.activation(
                o, i, AF.Sign, bias=b), incs=[(sS, 1)] if t == 3 else [])
        s_ab4 = sS.n
        for t in range(4, 7):
            R.op("scalar", lambda e, o=sb7[t], i=aa, b=thr_c[t]: e.activation(
                o, i, AF.Sign, bias=b), incs=[(sS, 1)] if t == 6 else [])
        s_all = sS.n
        if use_pool:
            R.op("gpsimd", lambda e, o=ub[0], a=sb7[0], b=sb7[1]: e.tensor_tensor(
                o, a, b, ALU.add), waits=[(sS, s_ab4)] + list(waits2))
            R.op("gpsimd", lambda e, o=ub[1], a=sb7[2], b=sb7[3]: e.tensor_tensor(
                o, a, b, ALU.add))
            R.op("gpsimd", lambda e, o=ub[0], a=ub[0], b=ub[1]: e.tensor_tensor(
                o, a, b, ALU.add))
            R.op("gpsimd", lambda e, o=ub[1], a=sb7[4], b=sb7[5]: e.tensor_tensor(
                o, a, b, ALU.add), waits=[(sS, s_all)], incs=[(sP, 1)],
                fence=True)
            p_tree = sP.n
            if use_pool == "adds":
                R.op("vector", lambda e, o=ub[1], a=sb7[6], b=ub[1]:
                     e.scalar_tensor_tensor(o, a, 2.0, b, ALU.mult, ALU.add),
                     waits=[(sS, s_all), (sP, p_tree)], prepad=True)
                R.op("vector", lambda e, o=ub[0], a=ub[1], b=ub[0]:
                     e.scalar_tensor_tensor(o, a, 2.0, b, ALU.mult, ALU.add))
                R.op("vector", lambda e, o=ub[1], i=ub[0]: e.tensor_scalar(
                    o, i, 0.25, 3.0, ALU.mult, ALU.add))
                R.op("vector", lambda e, o=ub[0], a=ub[1], b=sgb: e.tensor_tensor(
                    o, a, b, ALU.mult))
                R.op("vector", lambda e, o=yq, a=ub[0], s=sq, k=nb: e.tensor_tensor(
                    o.rearrange("p (b k) -> p b k", k=16),
                    a.rearrange("p (b k) -> p b k", k=16),
                    s.broadcast_to((P, k, 16)), ALU.mult),
                    waits=[(sV, st["v_sq"])])
                R.op("vector", lambda e, o=dst_h, i=yq: e.tensor_copy(o, i))
                R.op("vector", lambda e, o=dst_l, a=yq, b=dst_h: e.tensor_tensor(
                    o, a, b, ALU.subtract), incs=[(sV, 1)], fence=True)
                qprev[key] = (sV, sV.n)
                return qprev[key]
            R.op("vector", lambda e, o=ub[1], a=sb7[6], b=ub[1]:
                 e.scalar_tensor_tensor(o, a, 2.0, b, ALU.mult, ALU.add),
                 waits=[(sS, s_all), (sP, p_tree)], prepad=True)
            R.op("vector", lambda e, o=ub[0], a=ub[1], b=ub[0]:
                 e.scalar_tensor_tensor(o, a, 2.0, b, ALU.mult, ALU.add))
            R.op("vector", lambda e, o=ub[1], i=ub[0]: e.tensor_scalar(
                o, i, 0.25, 3.0, ALU.mult, ALU.add))
            R.op("vector", lambda e, o=ub[0], a=ub[1], b=sgb: e.tensor_tensor(
                o, a, b, ALU.mult), incs=[(sV, 1)], fence=True)
            v_g = sV.n
            R.op("gpsimd", lambda e, o=yq, a=ub[0], s=sq, k=nb: e.tensor_tensor(
                o.rearrange("p (b k) -> p b k", k=16),
                a.rearrange("p (b k) -> p b k", k=16),
                s.broadcast_to((P, k, 16)), ALU.mult),
                waits=[(sV, v_g), (sV, st["v_sq"])])
            R.op("gpsimd", lambda e, o=dst_h, i=yq: e.tensor_copy(o, i))
            R.op("gpsimd", lambda e, o=dst_l, a=yq, b=dst_h: e.tensor_tensor(
                o, a, b, ALU.subtract), incs=[(sP, 1)], fence=True)
            qprev[key] = (sP, sP.n)
            return qprev[key]
        R.op("vector", lambda e, o=ub[0], a=sb7[0], b=sb7[1]: e.tensor_tensor(
            o, a, b, ALU.add), waits=[(sS, s_ab4)] + list(waits2))
        R.op("vector", lambda e, o=ub[1], a=sb7[2], b=sb7[3]: e.tensor_tensor(
            o, a, b, ALU.add))
        R.op("vector", lambda e, o=ub[0], a=ub[0], b=ub[1]: e.tensor_tensor(
            o, a, b, ALU.add))
        R.op("vector", lambda e, o=ub[1], a=sb7[4], b=sb7[5]: e.tensor_tensor(
            o, a, b, ALU.add), waits=[(sS, s_all)])
        R.op("vector", lambda e, o=ub[1], a=sb7[6], b=ub[1]:
             e.scalar_tensor_tensor(o, a, 2.0, b, ALU.mult, ALU.add))
        R.op("vector", lambda e, o=ub[0], a=ub[1], b=ub[0]:
             e.scalar_tensor_tensor(o, a, 2.0, b, ALU.mult, ALU.add))
        R.op("vector", lambda e, o=ub[1], i=ub[0]: e.tensor_scalar(
            o, i, 0.25, 3.0, ALU.mult, ALU.add))
        R.op("vector", lambda e, o=ub[0], a=ub[1], b=sgb: e.tensor_tensor(
            o, a, b, ALU.mult))
        R.op("vector", lambda e, o=yq, a=ub[0], s=sq, k=nb: e.tensor_tensor(
            o.rearrange("p (b k) -> p b k", k=16),
            a.rearrange("p (b k) -> p b k", k=16),
            s.broadcast_to((P, k, 16)), ALU.mult), waits=[(sV, st["v_sq"])])
        R.op("vector", lambda e, o=dst_h, i=yq: e.tensor_copy(o, i))
        R.op("vector", lambda e, o=dst_l, a=yq, b=dst_h: e.tensor_tensor(
            o, a, b, ALU.subtract), incs=[(sV, 1)], fence=True)
        qprev[key] = (sV, sV.n)
        return qprev[key]

    def quant_chain(pool, key, src, width, idx, dst_h, dst_l, waits_v,
                    mode=False):
        st = quant_stage1(pool, key, src, width, idx, waits_v)
        return quant_stage2(pool, key, st, dst_h, dst_l, use_pool=mode)

    scPREP = []
    PW = 1024
    pstg = [scope_alloc(scPREP, f"pstg{j}", [P, PW], F32) for j in range(3)]
    psp_h = scope_alloc(scPREP, "psp_h", [P, PW], BF16)
    psp_l = scope_alloc(scPREP, "psp_l", [P, PW], BF16)
    pwtr = [scope_alloc(scPREP, "pwtr0", [P, 8, P], BF16)] * 2
    qpP = {
        "yq": [scope_alloc(scPREP, f"yqP{j}", [P, PW], F32) for j in range(2)],
    }
    qpP["aa"] = qpP["yq"]
    qpP.update({
        "sb": [scope_alloc(scPREP, f"sbP{j}", [P, PW], BF16) for j in range(7)],
        "ub": [scope_alloc(scPREP, f"ubP{j}", [P, PW], BF16) for j in range(2)],
        "sg": [scope_alloc(scPREP, "sgP0", [P, PW], BF16)] * 2,
    })

    scA1 = []
    x2 = [scope_alloc(scA1, f"x2a{j}", [P, C], F32) for j in range(3)]
    xn3 = [scope_alloc(scA1, f"xn3_{j}", [P, C], F32) for j in range(3)]
    scrA1 = scope_alloc(scA1, "scrA1", [P, C], F32)
    wgh = scope_alloc(scA1, "wgh", [P, 8, C], BF16)
    wgl = scope_alloc(scA1, "wgl", [P, 8, C], BF16)
    BG = scope_alloc(scA1, "BG", [P, C], F32)
    qpA = {
        "yq": [b[:, 0:C] for b in qpP["yq"]],
        "aa": [b[:, 0:C] for b in qpP["aa"]],
        "sb": qpP["sb"],
        "ub": qpP["ub"],
        "sg": [scope_alloc(scA1, f"sgA{j}", [P, C], BF16) for j in range(2)],
    }
    xnh_b = [scope_alloc(scA1, f"xnhb{j}", [P, C], BF16) for j in range(2)]
    xnl_b = [scope_alloc(scA1, f"xnlb{j}", [P, C], BF16) for j in range(2)]
    xqh_b = [scope_alloc(scA1, f"xqhb{j}", [P, C], BF16) for j in range(2)]
    xql_b = [scope_alloc(scA1, f"xqlb{j}", [P, C], BF16) for j in range(2)]
    fmh = [scope_alloc(scA1, f"fmh{j}", [P, 8, P], BF16) for j in range(2)]
    fml = [scope_alloc(scA1, f"fml{j}", [P, 8, P], BF16) for j in range(2)]
    fqh = [scope_alloc(scA1, f"fqh{j}", [P, 8, P], BF16) for j in range(2)]
    fql = [scope_alloc(scA1, f"fql{j}", [P, 8, P], BF16) for j in range(2)]

    dma("sync", csts[:, :], csts_d[:, :], sLd)
    dma("sync", BG[:, :], bgt_d[:, :], sLd)
    for cch in range(8):
        w_stg = [R.at(f"wgsp_{cch - 1}")] if cch >= 1 else []
        dma("sync", pstg[cch % 2][:, 0:C], wg_d[cch * P:(cch + 1) * P, :], sLd,
            waits=w_stg)
        ldv = sLd.n
        R.op("vector", lambda e, o=wgh[:, cch, :], i=pstg[cch % 2][:, 0:C]:
             e.tensor_copy(o, i), waits=[(sLd, ldv)])
        R.op("vector", lambda e, o=wgl[:, cch, :], a=pstg[cch % 2][:, 0:C],
             b=wgh[:, cch, :]: e.tensor_tensor(o, a, b, ALU.subtract),
             incs=[(sV, 1)], fence=True)
        R.mark(f"wgsp_{cch}", sV)
    R.mark("wg_ready", sV)

    R.op("vector", lambda e, o=kvacc[:, :]: e.memset(o, 0.0), incs=[(sV, 1)])

    s_er_tab = {}
    a1banks = [pp[0], pp[1], psW[:, 0:512], psW[:, 512:1024]]

    ld_tab = {}

    def a1_xload(L):
        w_x = [R.at(f"a1_xfree_{L - 3}")] if L >= 3 else []
        dma("sync", x2[L % 3][:, :], x_d[L * P:(L + 1) * P, :], sLd, waits=w_x)
        ld_tab[L] = sLd.n

    def a1_tail(j, st):
        idx, xnt, s_xn, t_gs, gbt = st
        for nchv in range(2):
            bk = (2 * j + nchv) % 4
            w_gb = [(sT, t_gs[nchv])]
            if nchv == 0 and j >= 2:
                w_gb.append(R.at(f"a1_gatesfree_{j - 2}"))
            if nchv == 0:
                w_gb += R.wtap(f"tf_gates_{j - 2}")
            R.op("vector", lambda e, o=gbt[:, nchv * 512:(nchv + 1) * 512],
                 a=a1banks[bk][:, :], b=BG[:, nchv * 512:(nchv + 1) * 512]:
                 e.scalar_tensor_tensor(o, a, 1.0, b, ALU.mult, ALU.add),
                 waits=w_gb, incs=[(sV, 1)], prepad=True)
            bank_release(f"g{bk}", sV, sV.n)
        v_gb = sV.n
        R.op("scalar", lambda e, o=gbt, i=gbt: e.activation(o[:, :], i[:, :],
             AF.Sigmoid), waits=[(sV, v_gb)], incs=[(sS, 1)])
        s_gates = sS.n
        dma("gpsimd", gates_sp[j * P:(j + 1) * P, :], gbt[:, :], sSt,
            waits=[(sS, s_gates)])
        R.mark(f"a1_gatesfree_{j}", sSt)
        if taps and j < 5:
            dma("gpsimd", tap["t_xn"][j * P:(j + 1) * P, :], xnt[:, :], sSt,
                waits=[(sS, s_xn)])
            R.mark(f"tf_xn_{j}", sSt)
            dma("gpsimd", tap["t_gates"][j * P:(j + 1) * P, :], gbt[:, :], sSt,
                waits=[(sS, s_gates)])
            R.mark(f"tf_gates_{j}", sSt)

        w_qv = [(sS, s_xn)]
        if j >= 2:
            w_qv.append(R.at(f"a1_xqfree_{j - 2}"))
        w_qv += R.wtap(f"tf_xq_{j - 2}")
        rq_t = quant_chain(qpA, "A", xnt[:, :], C, idx, xqh_b[idx][:, :],
                           xql_b[idx][:, :], w_qv, mode="adds")
        R.mark(f"a1_xnfree_{j}", sV)
        w_fq = []
        if j >= 2:
            w_fq = [R.at(f"a1_fqfree_{j - 2}")]
        dma("sync", fqh[idx][:, :, :], xqh_b[idx][:, :], sTr,
            waits=[rq_t] + w_fq, transpose=True)
        dma("sync", fql[idx][:, :, :], xql_b[idx][:, :], sTr, transpose=True)
        tr_fq = sTr.n
        R.ck[f"a1_xqfree_{j}"] = (sTr, tr_fq)
        dma("gpsimd", xqh_sp[j, :, :, :], fqh[idx][:, :, :], sSt,
            waits=[(sTr, tr_fq)])
        dma("gpsimd", xql_sp[j, :, :, :], fql[idx][:, :, :], sSt)
        R.mark(f"a1_fqfree_{j}", sSt)
        if taps and j < 5:
            dma("gpsimd", tap["t_xqh"][j * P:(j + 1) * P, :], xqh_b[idx][:, :],
                sSt, waits=[rq_t])
            dma("gpsimd", tap["t_xql"][j * P:(j + 1) * P, :], xql_b[idx][:, :],
                sSt)
            R.mark(f"tf_xq_{j}", sSt)

    ln_nmr = {}

    def ln_front(j):
        idx = j % 2
        xt = x2[j % 3]
        w_st = [(sLd, ld_tab.pop(j))]
        if j >= 2:
            w_st.append(R.at(f"ln_free_{j - 2}"))
        R.op("scalar", lambda e, o=scrA1[:, :], i=xt, a=acc1[idx]: e.activation(
            o, i[:, :], AF.Copy, accum_out=a[:, :]), waits=w_st)
        R.op("scalar", lambda e, o=scrA1[:, :], i=xt, a=acc2[idx]: e.activation(
            o, i[:, :], AF.Square, accum_out=a[:, :]), incs=[(sS, 1)],
            fence=True)
        s_stats = sS.n
        R.op("vector", lambda e, o=tmean[idx], i=acc1[idx]: e.tensor_scalar(
            o[:, :], i[:, :], 1.0 / C, None, ALU.mult), waits=[(sS, s_stats)],
            incs=[(sV, 1)], pad=True, prepad=True)
        R.op("vector", lambda e, o=ln_b[idx], a=tmean[idx], b=tmean[idx]:
             e.tensor_tensor(o[:, :], a[:, :], b[:, :], ALU.mult),
             waits=[(sV, sV.n)], incs=[(sV, 1)], pad=True)
        R.op("vector", lambda e, o=ln_a[idx], i=acc2[idx]: e.tensor_scalar(
            o[:, :], i[:, :], 1.0 / C, None, ALU.mult), waits=[(sV, sV.n)],
            incs=[(sV, 1)], pad=True)
        R.op("vector", lambda e, o=tvar[idx], a=ln_a[idx], b=ln_b[idx]:
             e.tensor_tensor(o[:, :], a[:, :], b[:, :], ALU.subtract),
             waits=[(sV, sV.n)], incs=[(sV, 1)], pad=True)
        R.op("vector", lambda e, o=tvar[idx], i=tvar[idx]: e.tensor_scalar(
            o[:, :], i[:, :], 1e-5, None, ALU.add), waits=[(sV, sV.n)],
            incs=[(sV, 1)], pad=True)
        v_var = sV.n
        if taps and j < 5:
            dma("gpsimd", tap[f"t_dbg{j}"][:, :], acc2[idx][:, :], sSt,
                waits=[(sS, s_stats)])
            dma("gpsimd", tap[f"t_dbg{5 + j}"][:, :], tvar[idx][:, :], sSt,
                waits=[(sV, v_var)])
        R.op("scalar", lambda e, o=ln_a[idx], i=tvar[idx]: e.activation(
            o[:, :], i[:, :], AF.Sqrt), waits=[(sV, v_var)], incs=[(sS, 1)])
        s_rstd = sS.n
        R.op("vector", lambda e, o=ln_b[idx], i=ln_a[idx]: e.reciprocal(
            o[:, :], i[:, :]), waits=[(sS, s_rstd)], incs=[(sV, 1)])
        R.op("vector", lambda e, o=ln_b[idx], a=tvar[idx], b=ln_b[idx]:
             e.tensor_tensor(o[:, :], a[:, :], b[:, :], ALU.mult),
             waits=[(sV, sV.n)], incs=[(sV, 1)], pad=True)
        R.op("vector", lambda e, o=ln_b[idx], a=ln_a[idx], b=ln_b[idx]:
             e.tensor_tensor(o[:, :], a[:, :], b[:, :], ALU.add),
             waits=[(sV, sV.n)], incs=[(sV, 1)], pad=True)
        R.op("vector", lambda e, o=ln_b[idx], i=ln_b[idx]: e.tensor_scalar(
            o[:, :], i[:, :], 0.5, None, ALU.mult), waits=[(sV, sV.n)],
            incs=[(sV, 1)], pad=True)
        R.op("vector", lambda e, o=trstd[idx], i=ln_b[idx]: e.reciprocal(
            o[:, :], i[:, :]), waits=[(sV, sV.n)], incs=[(sV, 1)])
        R.op("vector", lambda e, o=tnmr[idx], a=tmean[idx], b=trstd[idx]:
             e.scalar_tensor_tensor(o[:, :], a[:, :], -1.0, b[:, :],
                                    ALU.mult, ALU.mult), waits=[(sV, sV.n)],
             incs=[(sV, 1)], pad=True)
        ln_nmr[j] = sV.n
        if taps and j < 5:
            dma("gpsimd", tap[f"t_dbg{10 + j}"][:, :], trstd[idx][:, :], sSt,
                waits=[(sV, ln_nmr[j])])
        R.mark(f"ln_free_{j}", sV)

    a1_xload(0)
    a1_xload(1)
    ln_front(0)
    a1_prev = None
    for i in range(NTIL):
        idx = i % 2
        xt = x2[i % 3]
        xnt = xn3[i % 3]
        gbt = g2[idx]

        if i + 2 < NTIL:
            a1_xload(i + 2)

        w_xn = [(sV, ln_nmr.pop(i))]
        if i >= 3:
            w_xn.append(R.at(f"a1_xnfree_{i - 3}"))
        w_xn += R.wtap(f"tf_xn_{i - 3}")
        R.op("scalar", lambda e, o=xnt, i=xt, sc=trstd[idx], b=tnmr[idx]:
             e.activation(o[:, :], i[:, :], AF.Identity, bias=b[:, :], scale=sc[:, :]),
             waits=w_xn, incs=[(sS, 1)])
        s_xn = sS.n
        R.mark(f"a1_xfree_{i}", sS)

        w_sp = [R.at(f"a1_xnhfree_{i - 2}")] if i >= 2 else []
        R.op("vector", lambda e, o=xnh_b[idx], i=xnt: e.tensor_copy(o[:, :], i[:, :]),
             waits=[(sS, s_xn)] + w_sp)
        R.op("vector", lambda e, o=xnl_b[idx], a=xnt, b=xnh_b[idx]: e.tensor_tensor(
            o[:, :], a[:, :], b[:, :], ALU.subtract), incs=[(sV, 1)], fence=True)
        v_split = sV.n
        w_fm = []
        if i >= 2:
            w_fm = [R.at(f"a1_fmfreeT_{i - 2}"), R.at(f"a1_fmfreeS_{i - 2}")]
        dma("sync", fmh[idx][:, :, :], xnh_b[idx][:, :], sTr,
            waits=[(sV, v_split)] + w_fm, transpose=True)
        dma("sync", fml[idx][:, :, :], xnl_b[idx][:, :], sTr, transpose=True)
        tr_fm = sTr.n
        R.ck[f"a1_xnhfree_{i}"] = (sTr, tr_fm)
        dma("gpsimd", xnh_sp[i, :, :, :], fmh[idx][:, :, :], sSt, waits=[(sTr, tr_fm)])
        dma("gpsimd", xnl_sp[i, :, :, :], fml[idx][:, :, :], sSt)
        R.mark(f"a1_fmfreeS_{i}", sSt)

        t_gs = []
        for nchv in range(2):
            bk = (2 * i + nchv) % 4
            t_g = proj3(a1banks[bk], f"g{bk}", fmh[idx], fml[idx], wgh, wgl, 512,
                        np.s_[nchv * 512:(nchv + 1) * 512],
                        [R.at("wg_ready"), (sTr, tr_fm)])
            t_gs.append(t_g)
        R.mark(f"a1_fmfreeT_{i}", sT)

        if i + 1 < NTIL:
            ln_front(i + 1)

        st_here = (idx, xnt, s_xn, t_gs, gbt)
        if a1_prev is not None:
            a1_tail(i - 1, a1_prev)
        a1_prev = st_here

    a1_tail(NTIL - 1, a1_prev)
    R.bankprev[0] = R.bankprev["g0"]
    R.bankprev[1] = R.bankprev["g1"]

    R.mark("a1_T_done", sT)
    R.mark("a1_V_done", sV)
    R.mark("a1_S_done", sS)
    a1_spills = sSt.n

    prep_descs = []
    for w_d_ap, nchunks, kwidth, sp_h, sp_l in (
            (wgate_d, 22, C, wgh_sp, wgl_sp),
            (wup_d, 22, C, wuh_sp, wul_sp),
            (wdown_d, 8, I, wdh_sp, wdl_sp)):
        for j in range(nchunks):
            for u in range((kwidth + PW - 1) // PW):
                k0 = u * PW
                k1 = min(kwidth, k0 + PW)
                prep_descs.append((w_d_ap, j, k0, k1, k0 // P, sp_h, sp_l))

    prep_state = {"clock": 0, "ld": {}, "st": {}}

    def prep_load(k):
        w_d_ap, j, k0, k1, kofs, sp_h, sp_l = prep_descs[k]
        pidx = k % 3
        wseg = k1 - k0
        w_stage = [R.at(f"prep_stg_free_{pidx}")] if f"prep_stg_free_{pidx}" in R.ck else []
        dma("sync", pstg[pidx][:, 0:wseg], w_d_ap[j * P:(j + 1) * P, k0:k1], sLd,
            waits=w_stage)
        prep_state["ld"][k] = sLd.n

    def prep_stage1(k):
        w_d_ap, j, k0, k1, kofs, sp_h, sp_l = prep_descs[k]
        pidx = k % 3
        wseg = k1 - k0
        st = quant_stage1(qpP, "P", pstg[pidx][:, 0:wseg], wseg, k % 2,
                          [(sLd, prep_state["ld"].pop(k))])
        R.ck[f"prep_stg_free_{pidx}"] = (sV, sV.n)
        prep_state["st"][k] = st

    def prep_stage2(k):
        w_d_ap, j, k0, k1, kofs, sp_h, sp_l = prep_descs[k]
        st = prep_state["st"].pop(k)
        wseg = st["width"]
        ksub = wseg // P
        w2 = [R.at("prep_sp_free")] if "prep_sp_free" in R.ck else []
        rq_t = quant_stage2(qpP, "P", st, psp_h[:, 0:wseg], psp_l[:, 0:wseg],
                            waits2=w2, use_pool=False)
        w_tr = [R.at("prep_wtr_free_1")] if "prep_wtr_free_1" in R.ck else []
        dma("sync", pwtr[0][:, 0:ksub, :], psp_h[:, 0:wseg], sTr,
            waits=[rq_t] + w_tr, transpose=True)
        tr_h = sTr.n
        dma("gpsimd", sp_h[:, kofs:kofs + ksub, j * P:(j + 1) * P],
            pwtr[0][:, 0:ksub, :], sSt, waits=[(sTr, tr_h)])
        R.mark("prep_wtr_free_0", sSt)
        dma("sync", pwtr[1][:, 0:ksub, :], psp_l[:, 0:wseg], sTr,
            waits=[R.at("prep_wtr_free_0")], transpose=True)
        tr_l = sTr.n
        R.ck["prep_sp_free"] = (sTr, tr_l)
        dma("gpsimd", sp_l[:, kofs:kofs + ksub, j * P:(j + 1) * P],
            pwtr[1][:, 0:ksub, :], sSt, waits=[(sTr, tr_l)])
        R.mark("prep_wtr_free_1", sSt)

    NPREP = len(prep_descs)
    prep_pt = {"n": 0}

    def prep_pump():
        prep_pt["n"] += 1
        want = (prep_pt["n"] * (NPREP + 2)) // (6 * NTIL)
        while prep_state["clock"] < want:
            prep_advance()

    def prep_advance():
        k = prep_state["clock"]
        if k < NPREP:
            prep_load(k)
        if 1 <= k < NPREP + 1:
            prep_stage1(k - 1)
        if 2 <= k < NPREP + 2:
            prep_stage2(k - 2)
        prep_state["clock"] += 1

    scope_exit(scA1)

    scA2 = []
    wqh = scope_alloc(scA2, "wqh", [P, 8, 3 * C], BF16)
    wql = scope_alloc(scA2, "wql", [P, 8, 3 * C], BF16)
    qstg = [scope_alloc(scA2, f"qstg{j}", [P, 512], F32) for j in range(2)]
    lfh = [scope_alloc(scA2, f"lfh{j}", [P, 8, P], BF16) for j in range(2)]
    lfl = [scope_alloc(scA2, f"lfl{j}", [P, 8, P], BF16) for j in range(2)]
    zb = [scope_alloc(scA2, "zb0", [P, 512], F32)] * 2
    ebz = [scope_alloc(scA2, "ebz0", [P, 512], F32)] * 2
    k1b = [scope_alloc(scA2, f"k1b{j}", [P, C], F32) for j in range(2)]
    vxb = [scope_alloc(scA2, "vxb0", [P, H, 65], F32)] * 2

    a1_gate = [R.at("a1_T_done"), R.at("a1_V_done"), R.at("a1_S_done"),
               (sSt, a1_spills), (sTr, sTr.n)]

    def wq_piece(k8):
        hf, cch = divmod(k8, 8)
        cs = np.s_[hf * 512:(hf + 1) * 512]
        qsg = qstg[k8 % 2]
        w_stg = [R.at(f"wqsp_{k8 - 2}")] if k8 >= 2 else list(a1_gate)
        dma("sync", qsg[:, :], wqkv_d[cch * P:(cch + 1) * P, cs], sLd,
            waits=w_stg)
        ldv = sLd.n
        R.op("vector", lambda e, o=wqh[:, cch, cs], i=qsg[:, :]:
             e.tensor_copy(o, i), waits=[(sLd, ldv)])
        R.op("vector", lambda e, o=wql[:, cch, cs], a=qsg[:, :],
             b=wqh[:, cch, cs]: e.tensor_tensor(o, a, b, ALU.subtract),
             incs=[(sV, 1)], fence=True)
        R.mark(f"wqsp_{k8}", sV)
        if cch == 7:
            R.mark(f"wq_col_{hf}", sV)
        if k8 == 47:
            R.mark("wq_ready", sV)
            R.op("vector", lambda e, o=vxb[0][:, :, 64:65]: e.memset(o, 1.0),
                 incs=[(sV, 1)], fence=True)
            R.mark("vones", sV)

    a2_ldt = {}

    def a2_loads(i):
        idx = i % 2
        w_g = [R.at(f"a1_gatesfree_{i}")]
        if i >= 2:
            w_g.append(R.at(f"a2_gatesbuf_free_{i - 2}"))
        else:
            w_g.append(R.at(f"a1_gatesfree_{NTIL - 2 + i}"))
        dma("sync", g2[idx][:, :], gates_sp[i * P:(i + 1) * P, :], sLd, waits=w_g)
        w_l = [R.at(f"a2_lffree_{i - 2}")] if i >= 2 else \
            list(a1_gate) + [R.at(f"a1_fqfree_{i}")]
        dma("sync", lfh[idx][:, :, :], xnh_sp[i, :, :, :], sLd, waits=w_l)
        dma("sync", lfl[idx][:, :, :], xnl_sp[i, :, :, :], sLd)
        a2_ldt[i] = sLd.n

    a2_loads(0)
    for k8 in range(16):
        wq_piece(k8)
    for i in range(NTIL):
        idx = i % 2
        gt = g2[idx]
        q1t = f2[idx]
        ld_i = a2_ldt.pop(i)
        if i + 1 < NTIL:
            a2_loads(i + 1)

        for nch in range(6):
            if i == 0 and nch >= 1:
                for k8 in range(8 + 8 * nch, min(48, 16 + 8 * nch)):
                    wq_piece(k8)
            bank = pp[nch % 2]
            wq_gate = R.at(f"wq_col_{nch}") if i == 0 else R.at("wq_ready")
            t_mm = proj3(bank, nch % 2, lfh[idx], lfl[idx], wqh, wql, 512,
                         np.s_[nch * 512:(nch + 1) * 512],
                         [wq_gate, (sLd, ld_i)])
            if nch < 4:
                dsts = q1t if nch < 2 else k1b[idx]
                dcol = (nch % 2) * 512
                w_z = [(sT, t_mm)]
                prev_er = s_er_tab.get((i, nch - 1), s_er_tab.get((i - 1, 3)))
                if prev_er is not None and nch <= 3:
                    w_z.append(prev_er)
                if nch < 2 and i >= 2:
                    w_z.append(R.at(f"a2_q1buf_free_{i - 2}"))
                if nch >= 2 and i >= 2:
                    w_z.append(R.at(f"a2_einsum_{i - 2}"))
                    w_z += R.wtap(f"tf_k1_{i - 2}")
                R.op("vector", lambda e, o=zb[nch % 2][:, :], a=bank[:, :],
                     b=gt[:, dcol:dcol + 512]:
                     e.tensor_tensor(o, a, b, ALU.mult), waits=w_z, incs=[(sV, 1)],
                     prepad=True)
                bank_release(nch % 2, sV, sV.n)
                v_z = sV.n
                R.op("vector", lambda e, o=ebz[nch % 2][:, :], i=zb[nch % 2][:, :]:
                     e.tensor_scalar(o, i, 0.0, None, ALU.min), incs=[(sV, 1)])
                v_zm = sV.n
                R.op("scalar", lambda e, o=ebz[nch % 2][:, :], i=ebz[nch % 2][:, :]:
                     e.activation(o, i, AF.Exp), waits=[(sV, v_zm)])
                R.op("scalar", lambda e, o=zb[nch % 2][:, :], i=zb[nch % 2][:, :]:
                     e.activation(o, i, AF.Relu), incs=[(sS, 1)])
                s_er = sS.n
                s_er_tab[(i, nch)] = (sS, s_er)
                R.op("vector", lambda e, o=dsts[:, dcol:dcol + 512],
                     a=ebz[nch % 2][:, :], b=zb[nch % 2][:, :]:
                     e.tensor_tensor(o, a, b, ALU.add), waits=[(sS, s_er)],
                     incs=[(sV, 1)], fence=(nch >= 2))
                if nch == 1:
                    R.mark(f"a2_q1done_{i}", sV)
                if nch == 3:
                    R.mark(f"a2_k1done_{i}", sV)
                    R.mark(f"a2_gatesbuf_free_{i}", sV)
                if nch in (0, 1, 2, 3):
                    prep_pump()
            else:
                h0 = (nch - 4) * 8
                w_v = [(sT, t_mm)]
                if nch == 4 and i >= 1:
                    w_v.append(R.at(f"a2_vfree_{i - 1}"))
                R.op("scalar", lambda e, o=vxb[idx][:, h0:h0 + 8, 0:64], a=bank[:, :]:
                     e.activation(o, a.rearrange("p (h d) -> p h d", d=64), AF.Copy),
                     waits=w_v, incs=[(sS, 1)], prepad=True, fence=(nch == 5))
                bank_release(nch % 2, sS, sS.n)
                if nch == 5:
                    R.mark(f"a2_vdone_{i}", sS)

        prep_pump()

        dma("gpsimd", q1_sp[i * P:(i + 1) * P, :], q1t[:, :], sSt,
            waits=[R.at(f"a2_q1done_{i}")])
        R.mark(f"a2_q1buf_free_{i}", sSt)
        if taps and i < 5:
            dma("gpsimd", tap["t_q1"][i * P:(i + 1) * P, :], q1t[:, :], sSt)
            R.mark(f"a2_q1buf_free_{i}", sSt)
            dma("gpsimd", tap["t_k1"][i * P:(i + 1) * P, :], k1b[idx][:, :], sSt,
                waits=[R.at(f"a2_k1done_{i}")])
            R.mark(f"tf_k1_{i}", sSt)

        w_kv = [R.at(f"a2_k1done_{i}"), R.at(f"a2_vdone_{i}"), R.at("vones")]
        if i >= 1:
            w_kv.append(R.at(f"a2_kvadd_{i - 1}"))
        for h in range(H):
            R.op("tensor", lambda e, o=ps_kv[0:65, h * 64:(h + 1) * 64],
                 l=vxb[idx][:, h, :], r=k1b[idx][:, h * 64:(h + 1) * 64]:
                 e.matmul(o, l, r, start=True, stop=True),
                 waits=w_kv if h == 0 else [],
                 incs=[(sT, 1)] if h == H - 1 else [])
        t_ein = sT.n
        R.mark(f"a2_einsum_{i}", sT)
        R.mark(f"a2_vfree_{i}", sT)
        R.mark(f"a2_lffree_{i}", sT)
        R.op("vector", lambda e, o=kvacc[:, :], a=ps_kv[0:65, :], b=kvacc[:, :]:
             e.scalar_tensor_tensor(o, a, 1.0, b, ALU.mult, ALU.add),
             waits=[(sT, t_ein)], incs=[(sV, 1)], prepad=True, fence=True)
        R.mark(f"a2_kvadd_{i}", sV)

        prep_pump()

    while prep_state["clock"] < NPREP + 2:
        prep_advance()
    R.mark("wprep_done", sSt)
    R.mark("prep_tr_done", sTr)
    R.mark("a2_S_done", sS)
    R.mark("a2_T_done", sT)
    R.mark("a2_kv_done", sV)
    scope_exit(scA2)
    scope_exit(scPREP)

    def post_a2_gate():
        return [R.at("a2_T_done"), R.at("a2_kv_done"), R.at("wprep_done"),
                R.at("prep_tr_done"), R.at("a2_S_done")]

    scCW = []
    x2 = [scope_alloc(scCW, f"x2c{j}", [P, C], F32) for j in range(2)]
    e2 = [scope_alloc(scCW, f"e2_{j}", [P, C], F32) for j in range(4)]
    cgh = scope_alloc(scCW, "cgh", [P, 8, 1024], BF16)
    cgl = scope_alloc(scCW, "cgl", [P, 8, 1024], BF16)
    cuh = scope_alloc(scCW, "cuh", [P, 8, 1024], BF16)
    cul = scope_alloc(scCW, "cul", [P, 8, 1024], BF16)
    cdh = scope_alloc(scCW, "cdh", [P, 8, C], BF16)
    cdl = scope_alloc(scCW, "cdl", [P, 8, C], BF16)

    scIL = []
    kv_all = scope_alloc(scIL, "kv_all", [65, C], F32)
    KSb = scope_alloc(scIL, "KSb", [P, C], F32)
    w2h = scope_alloc(scIL, "w2h", [P, 8, C], BF16)
    w2l = scope_alloc(scIL, "w2l", [P, 8, C], BF16)
    scILb = []
    wpst = [scope_alloc(scILb, f"wpst{j}", [64, C], F32) for j in range(4)]

    if taps:
        dma("gpsimd", tap["t_kvpart"][:, :], kvacc[:, :], sSt,
            waits=[R.at("a2_kv_done")])
    dma("gpsimd", cc_in[:, :], kvacc[:, :], sSt, waits=[R.at("a2_kv_done")])
    st_cc = sSt.n
    R.op("gpsimd", lambda e: e.collective_compute(
        "AllReduce", ALU.add, replica_groups=[[0, 1, 2, 3], [4, 5, 6, 7]],
        ins=[cc_in[:, :]], outs=[cc_out[:, :]]),
        waits=[(sSt, st_cc)], incs=[(sCC, 1)])
    dma("gpsimd", kv_all[:, :], cc_out[:, :], sSt,
        waits=[(sCC, 1)] + post_a2_gate())
    st_kv = sSt.n
    if taps:
        dma("gpsimd", tap["t_kvall"][:, :], kv_all[:, :], sSt, waits=[(sSt, st_kv)])

    dma("sync", KSb[0:1, :], kv_all[64:65, :], sTr,
        waits=[(sSt, st_kv)] + post_a2_gate())
    kk = 1
    while kk < P:
        step = min(kk, P - kk)
        dma("sync", KSb[kk:kk + step, :], KSb[0:step, :], sTr,
            waits=[(sTr, sTr.n)])
        kk += step
    tr_ks = sTr.n
    if taps:
        dma("gpsimd", tap["t_ksb"][:, :], KSb[:, :], sSt, waits=[(sTr, tr_ks)])

    w2_ldt, w2_mmt = {}, {}

    def w2_load(h):
        w_wp = []
        if h == 0:
            w_wp.append((sSt, st_kv))
            w_wp.extend(post_a2_gate())
        if h >= 4:
            w_wp.append((sT, w2_mmt[h - 4]))
        dma("sync", wpst[h % 4][:, :], wp_d[h * 64:(h + 1) * 64, :], sLd,
            waits=w_wp)
        w2_ldt[h] = sLd.n

    for h in range(4):
        w2_load(h)
    for pr in range(8):
        for hh in range(2):
            h = 2 * pr + hh
            for nch2 in range(2):
                w_mm = [(sLd, w2_ldt.pop(h))] if nch2 == 0 else []
                if pr >= 1 and hh == 0 and nch2 == 0:
                    w_mm.append(R.at(f"w2evac_{pr - 1}"))
                R.op("tensor", lambda e,
                     o=psW[hh * 64:hh * 64 + 64, nch2 * 512:(nch2 + 1) * 512],
                     l=kv_all[0:64, h * 64:(h + 1) * 64],
                     r=wpst[h % 4][:, nch2 * 512:(nch2 + 1) * 512]:
                     e.matmul(o, l, r, start=True, stop=True),
                     waits=w_mm,
                     incs=[(sT, 1)] if nch2 == 1 else [])
            w2_mmt[h] = sT.n
            if h + 4 < 16:
                w2_load(h + 4)
        t_w2 = sT.n
        R.op("vector", lambda e, o=w2h[:, pr, :], i=psW[:, :]: e.tensor_copy(o, i),
             waits=[(sT, t_w2)], prepad=True)
        R.op("vector", lambda e, o=w2l[:, pr, :], a=psW[:, :], b=w2h[:, pr, :]:
             e.tensor_tensor(o, a, b, ALU.subtract), incs=[(sV, 1)], fence=True)
        R.mark(f"w2evac_{pr}", sV)
        if taps and pr == 0:
            dma("gpsimd", tap["t_w2h"][:, :], w2h[:, 0, :], sSt,
                waits=[R.at("w2evac_0")])
            dma("gpsimd", tap["t_w2l"][:, :], w2l[:, 0, :], sSt)
    R.mark("w2_ready", sV)
    R.bankprev["c2"] = R.at("w2_ready")
    R.bankprev["c3"] = R.at("w2_ready")
    scope_exit(scILb)

    scB = []
    BP_t = scope_alloc(scB, "BPt", [P, C], F32)
    nmb = scope_alloc(scB, "nmb", [P, C], F32)
    n16 = [scope_alloc(scB, f"n16_{j}", [P, H], F32) for j in range(2)]
    n16r = [scope_alloc(scB, f"n16r_{j}", [P, H], F32) for j in range(2)]
    qnh_b = [scope_alloc(scB, f"qnh{j}", [P, C], BF16) for j in range(2)]
    qnl_b = [scope_alloc(scB, f"qnl{j}", [P, C], BF16) for j in range(2)]
    bfh = [scope_alloc(scB, f"bfh{j}", [P, 8, P], BF16) for j in range(2)]
    bfl = [scope_alloc(scB, f"bfl{j}", [P, 8, P], BF16) for j in range(2)]

    dma("sync", BP_t[:, :], bpt_d[:, :], sLd, waits=post_a2_gate())
    ld_bp = sLd.n

    def load_cw(third, waits):
        n0, n1 = N3[third]
        w = list(waits)
        last = 0
        for buf, spl in ((cgh, wgh_sp), (cgl, wgl_sp), (cuh, wuh_sp), (cul, wul_sp)):
            for kc in range(8):
                dma("sync", buf[:, kc, 0:n1 - n0], spl[:, kc, n0:n1], sLd, waits=w)
                w = []
        for buf, spl in ((cdh, wdh_sp), (cdl, wdl_sp)):
            for j in range((n1 - n0) // P):
                kc = n0 // P + j
                dma("sync", buf[:, j, :], spl[:, kc, :], sLd)
        return sLd.n


    b_ldt = {}
    bbanks = [pp[0], pp[1], psD[0], psD[1]]
    R.bankprev["b0"] = R.bankprev.pop(0)
    R.bankprev["b1"] = R.bankprev.pop(1)

    def b_loads(i):
        if i >= 2:
            w_q1 = [R.at(f"b_q1free_{i - 2}")]
        else:
            w_q1 = post_a2_gate()
        dma("sync", f2[i % 2][:, :], q1_sp[i * P:(i + 1) * P, :], sLd, waits=w_q1)
        w_x = [(sLd, ld_bp)]
        if i >= 2:
            w_x.append(R.at(f"b_xfree_{i - 2}"))
        dma("sync", x2[i % 2][:, :], x_d[i * P:(i + 1) * P, :], sLd, waits=w_x)
        b_ldt[i] = sLd.n

    def b_tail(j, st):
        xt, pout, t_bs = st
        for nch in range(2):
            bk = (2 * j + nch) % 4
            w_ev = [(sT, t_bs[nch])]
            if nch == 0 and j >= 2:
                w_ev.append(R.at(f"b_poutfree_{j - 2}"))
            R.op("vector", lambda e, o=pout[:, nch * 512:(nch + 1) * 512],
                 a=bbanks[bk][:, :], b=xt[:, nch * 512:(nch + 1) * 512]:
                 e.scalar_tensor_tensor(o, a, 1.0, b, ALU.mult, ALU.add),
                 waits=w_ev, incs=[(sV, 1)], prepad=True)
            bank_release(f"b{bk}", sV, sV.n)
        v_pout = sV.n
        R.mark(f"b_xfree_{j}", sV)
        dma("gpsimd", part_sp[j * P:(j + 1) * P, :], pout[:, :], sSt,
            waits=[(sV, v_pout)])
        R.mark(f"b_poutfree_{j}", sSt)
        if taps and j < 2:
            dma("gpsimd", tap["t_part"][j * P:(j + 1) * P, :], pout[:, :], sSt)
            R.mark(f"b_poutfree_{j}", sSt)

    b_loads(0)
    b_loads(1)
    ld_cw0 = load_cw(0, post_a2_gate())
    if taps:
        dma("gpsimd", tap["t_cgh"][:, :, :], cgh[:, :, :], sSt, waits=[(sLd, ld_cw0)])
        dma("gpsimd", tap["t_cgl"][:, :, :], cgl[:, :, :], sSt)
        dma("gpsimd", tap["t_cdh"][:, :, :], cdh[:, :, :], sSt)
        dma("gpsimd", tap["t_cdl"][:, :, :], cdl[:, :, :], sSt)
    b_prev = None
    for i in range(NTIL):
        idx = i % 2
        q1t = f2[idx]
        xt = x2[i % 2]
        pout = e2[idx]
        ld_b = b_ldt.pop(i)

        w_nm = [(sLd, ld_b), (sTr, tr_ks)]
        if i == 1:
            w_nm += R.wtap("tf_qn")
        R.op("vector", lambda e, o=nmb[:, :], a=q1t, b=KSb: e.tensor_tensor(
            o, a[:, :], b[:, :], ALU.mult), waits=w_nm)
        R.op("vector", lambda e, o=n16[idx][:, :], i=nmb[:, :]: e.tensor_reduce(
            o, i.rearrange("p (h d) -> p h d", d=64), AX.X, ALU.add),
            incs=[(sV, 1)], pad=True)
        R.op("vector", lambda e, o=n16[idx][:, :], i=n16[idx][:, :]: e.tensor_scalar(
            o, i, 1e-8, None, ALU.add), waits=[(sV, sV.n)], incs=[(sV, 1)],
            pad=True)
        R.op("vector", lambda e, o=n16r[idx][:, :], i=n16[idx][:, :]: e.reciprocal(o, i),
             waits=[(sV, sV.n)], incs=[(sV, 1)])
        if taps and i == 0:
            dma("gpsimd", tap["t_norm"][:, :], n16r[idx][:, :], sSt,
                waits=[(sV, sV.n)])
        R.op("vector", lambda e, o=nmb, a=q1t, b=n16r[idx]: e.tensor_tensor(
            o[:, :].rearrange("p (h d) -> p h d", d=64),
            a[:, :].rearrange("p (h d) -> p h d", d=64),
            b[:, :].broadcast_to((P, H, 64)), ALU.mult), waits=[(sV, sV.n)],
            incs=[(sV, 1)])
        R.mark(f"b_q1free_{i}", sV)
        if taps and i == 0:
            dma("gpsimd", tap["t_qn"][:, :], nmb[:, :], sSt,
                waits=[R.at("b_q1free_0")])
            R.mark("tf_qn", sSt)
        w_qn = [R.at(f"b_qnfree_{i - 2}")] if i >= 2 else []
        R.op("vector", lambda e, o=qnh_b[idx], i=nmb: e.tensor_copy(o[:, :], i[:, :]),
             waits=w_qn)
        R.op("vector", lambda e, o=qnl_b[idx], a=nmb, b=qnh_b[idx]: e.tensor_tensor(
            o[:, :], a[:, :], b[:, :], ALU.subtract), incs=[(sV, 1)], fence=True)
        v_qn = sV.n
        w_bf = [R.at(f"b_bffree_{i - 2}")] if i >= 2 else []
        dma("sync", bfh[idx][:, :, :], qnh_b[idx][:, :], sTr,
            waits=[(sV, v_qn)] + w_bf, transpose=True)
        dma("sync", bfl[idx][:, :, :], qnl_b[idx][:, :], sTr, transpose=True)
        tr_b = sTr.n
        R.ck[f"b_qnfree_{i}"] = (sTr, tr_b)
        if taps and i == 0:
            dma("gpsimd", tap["t_qnh"][:, :], qnh_b[idx][:, :], sSt,
                waits=[(sV, v_qn)])
            dma("gpsimd", tap["t_qnl"][:, :], qnl_b[idx][:, :], sSt)
            dma("gpsimd", tap["t_bfh"][:, :, :], bfh[idx][:, :, :], sSt,
                waits=[(sTr, tr_b)])
            R.ck[f"b_qnfree_{i}"] = (sSt, sSt.n)

        w_px = [(sLd, ld_b)]
        R.op("vector", lambda e, o=xt, a=xt, b=BP_t: e.tensor_tensor(
            o[:, :], a[:, :], b[:, :], ALU.add), waits=w_px, incs=[(sV, 1)])

        t_bs = []
        for nch in range(2):
            bk = (2 * i + nch) % 4
            t_b = proj3(bbanks[bk], f"b{bk}", bfh[idx], bfl[idx], w2h, w2l, 512,
                        np.s_[nch * 512:(nch + 1) * 512],
                        [R.at("w2_ready"), (sTr, tr_b)])
            t_bs.append(t_b)
        R.mark(f"b_bffree_{i}", sT)

        st_here = (xt, pout, t_bs)
        if b_prev is not None:
            b_tail(i - 1, b_prev)
        b_prev = st_here
        if 2 <= i + 1 < NTIL:
            b_loads(i + 1)

    b_tail(NTIL - 1, b_prev)
    R.bankprev[0] = R.bankprev["b0"]
    R.bankprev[1] = R.bankprev["b1"]
    R.bankprev["D0_0"] = R.bankprev["b2"]
    R.bankprev["D1_0"] = R.bankprev["b3"]

    R.mark("b_T_done", sT)
    R.mark("b_V_done", sV)
    scope_exit(scB)
    scope_exit(scIL)

    scC = []
    qpC = {
        "yq": [scope_alloc(scC, f"yqC{j}", [P, 1024], F32) for j in range(2)],
        "sb": [scope_alloc(scC, f"sbC{j}", [P, 1024], BF16) for j in range(7)],
        "ub": [scope_alloc(scC, f"ubC{j}", [P, 1024], BF16) for j in range(2)],
        "sg": [scope_alloc(scC, "sgC0", [P, 1024], BF16)] * 2,
    }
    qpC["aa"] = qpC["yq"]
    xfh = [scope_alloc(scC, f"xfh{j}", [P, 8, P], BF16) for j in range(2)]
    xfl = [scope_alloc(scC, f"xfl{j}", [P, 8, P], BF16) for j in range(2)]
    hqh_b = [scope_alloc(scC, f"hqh{j}", [P, 1024], BF16) for j in range(2)]
    hql_b = [scope_alloc(scC, f"hql{j}", [P, 1024], BF16) for j in range(2)]
    hfh = [scope_alloc(scC, f"hfh{j}", [P, 8, P], BF16) for j in range(4)]
    hfl = [scope_alloc(scC, f"hfl{j}", [P, 8, P], BF16) for j in range(4)]

    R.bankprev["D0_1"] = R.at("a2_kv_done")
    R.bankprev["D1_1"] = R.at("a2_kv_done")
    R.bankprev["c0"] = R.bankprev.get(0)
    R.bankprev["c1"] = R.bankprev.get(1)
    if R.bankprev["c0"] is None:
        del R.bankprev["c0"]
    if R.bankprev["c1"] is None:
        del R.bankprev["c1"]

    def c_down(third, i, st_c):
        TN, kch, idx4, tr_h, pin, ld_pin = st_c
        dbanks = [psD[0], psD[1], ps_kv[:, 0:512], ps_kv[:, 512:1024]]
        for mch in range(2):
            bank = dbanks[mch + 2 * (i % 2)]
            triples = []
            for kc in range(kch):
                triples.append((bank[:, :], hfh[idx4][:, kc, :],
                                cdh[:, kc, mch * 512:(mch + 1) * 512]))
                triples.append((bank[:, :], hfh[idx4][:, kc, :],
                                cdl[:, kc, mch * 512:(mch + 1) * 512]))
                triples.append((bank[:, :], hfl[idx4][:, kc, :],
                                cdh[:, kc, mch * 512:(mch + 1) * 512]))
            mm_group(bank, f"D{mch}_{i % 2}", triples, [(sTr, tr_h)])
            t_d = sT.n
            R.op("vector", lambda e, o=pin[:, mch * 512:(mch + 1) * 512],
                 a=bank[:, :], b=pin[:, mch * 512:(mch + 1) * 512]:
                 e.scalar_tensor_tensor(o, a, 1.0, b, ALU.mult, ALU.add),
                 waits=[(sT, t_d), (sLd, ld_pin)], incs=[(sV, 1)], prepad=True)
            bank_release(f"D{mch}_{i % 2}", sV, sV.n)
        v_p = sV.n
        R.mark(f"c_hffree_{third}_{i}", sT)
        if third < 2:
            dma("gpsimd", part_sp[i * P:(i + 1) * P, :], pin[:, :], sSt,
                waits=[(sV, v_p)])
        else:
            dma("gpsimd", y_d[i * P:(i + 1) * P, :], pin[:, :], sSt,
                waits=[(sV, v_p)])
        R.mark(f"c_pdone_{third}_{i}", sSt)
        R.mark(f"c_pinfree_{third}_{i}", sSt)

    for third in range(3):
        n0, n1 = N3[third]
        TN = n1 - n0
        kch = TN // P
        if third > 0:
            ld_cw = load_cw(third, [R.at(f"c_T_done_{third - 1}")])
        else:
            ld_cw = ld_cw0
        nck = [(j * 512, min(TN, (j + 1) * 512)) for j in range((TN + 511) // 512)]
        nnc = len(nck)
        c_pending = []
        ldt = {}

        def c_loads(i):
            idx = i % 2
            idx4 = i % 4
            if i >= 2:
                w_xf = [R.at(f"c_xffree_{third}_{i - 2}")]
            elif third > 0:
                w_xf = [R.at(f"c_xffree_{third - 1}_{NTIL - 2 + i}")]
            else:
                w_xf = [R.at("b_V_done"), R.at("b_T_done")]
            if i == 0:
                w_xf.append((sLd, ld_cw))
            dma("sync", xfh[idx][:, :, :], xqh_sp[i, :, :, :], sLd, waits=w_xf)
            dma("sync", xfl[idx][:, :, :], xql_sp[i, :, :, :], sLd)
            ld_c = sLd.n
            if third == 0:
                w_pin = [R.at(f"b_poutfree_{max(i, NTIL - 2 + (i % 2))}")]
            else:
                w_pin = [R.at(f"c_pdone_{third - 1}_{i}")]
            if i >= 4:
                w_pin.append(R.at(f"c_pinfree_{third}_{i - 4}"))
            elif third > 0:
                w_pin.append(R.at(f"c_pinfree_{third - 1}_{NTIL - 4 + i}"))
            dma("sync", e2[idx4][:, :], part_sp[i * P:(i + 1) * P, :], sLd,
                waits=w_pin)
            ldt[i] = (ld_c, sLd.n)

        c_loads(0)
        for i in range(NTIL):
            idx = i % 2
            idx4 = i % 4
            hbuf = g2[idx]
            sgt = x2[idx]
            upt = f2[idx]
            pin = e2[idx4]
            ld_c, ld_pin = ldt.pop(i)
            if i + 1 < NTIL:
                c_loads(i + 1)

            cbanks = [pp[0], pp[1], psW[:, 0:512], psW[:, 512:1024]]
            for gu, (wbh, wbl) in enumerate(((cgh, cgl), (cuh, cul))):
                dst = sgt if gu == 0 else upt
                for ncj, (c0, c1) in enumerate(nck):
                    seq = gu * nnc + ncj
                    bank = cbanks[seq % 4]
                    t_c = proj3(bank, f"c{seq % 4}", xfh[idx], xfl[idx], wbh, wbl,
                                c1 - c0, np.s_[c0:c1], [(sLd, ld_c)])
                    w_ev = [(sT, t_c)]
                    if ncj == 0:
                        if third == 0 and i < 2:
                            w_ev.append(R.at("b_V_done"))
                        elif i >= 2:
                            w_ev.append(R.at(f"c_sufree_{third}_{i - 2}"))
                        elif third > 0:
                            w_ev.append(R.at(f"c_sufree_{third - 1}_{NTIL - 2 + i}"))
                    R.op("scalar", lambda e, o=dst[:, c0:c1], i=bank[:, 0:c1 - c0],
                         f=(AF.Silu if gu == 0 else AF.Copy):
                         e.activation(o, i, f), waits=w_ev, incs=[(sS, 1)],
                         prepad=True)
                    bank_release(f"c{seq % 4}", sS, sS.n)
            s_gu = sS.n
            R.mark(f"c_xffree_{third}_{i}", sT)

            if len(c_pending) >= 2:
                c_down(third, i - 2, c_pending.pop(0))

            w_h = []
            if i >= 1:
                w_h = [R.at(f"c_hfree_{third}_{i - 1}")]
            elif third > 0:
                w_h = [R.at(f"c_hfree_{third - 1}_{NTIL - 1}")]
            if third == 0 and i == 1:
                w_h += R.wtap("tf_h")
            R.op("vector", lambda e, o=hbuf[:, 0:TN], a=sgt[:, 0:TN], b=upt[:, 0:TN]:
                 e.tensor_tensor(o, a, b, ALU.mult), waits=[(sS, s_gu)] + w_h,
                 incs=[(sV, 1)])
            R.mark(f"c_sufree_{third}_{i}", sV)
            v_h = sV.n
            if taps and third == 0 and i == 0:
                dma("gpsimd", tap["t_h"][:, 0:TN], hbuf[:, 0:TN], sSt,
                    waits=[(sV, v_h)])
                R.mark("tf_h", sSt)

            w_qv = [(sV, v_h)]
            if third == 0 and i < 2:
                w_qv.append(R.at("b_V_done"))
                w_qv.append(R.at("b_T_done"))
            rq_t = quant_chain(qpC, "C", hbuf[:, 0:TN], TN, idx,
                               hqh_b[idx][:, 0:TN], hql_b[idx][:, 0:TN], w_qv)
            R.mark(f"c_hfree_{third}_{i}", sV)

            w_hf = []
            if i >= 4:
                w_hf = [R.at(f"c_hffree_{third}_{i - 4}")]
            elif third > 0:
                w_hf = [R.at(f"c_hffree_{third - 1}_{NTIL - 4 + i}")]
            else:
                w_hf = [R.at("b_T_done")]
            dma("sync", hfh[idx4][:, 0:kch, :], hqh_b[idx][:, 0:TN], sTr,
                waits=[rq_t] + w_hf, transpose=True)
            dma("sync", hfl[idx4][:, 0:kch, :], hql_b[idx][:, 0:TN], sTr,
                transpose=True)
            tr_h = sTr.n
            c_pending.append((TN, kch, idx4, tr_h, pin, ld_pin))

        for joff, st_c in enumerate(c_pending):
            c_down(third, NTIL - len(c_pending) + joff, st_c)
        c_pending = []
        R.mark(f"c_T_done_{third}", sT)

    scope_exit(scC)
    scope_exit(scCW)

    final_st = sSt.n
    R.op("gpsimd", None, waits=[(sSt, final_st)])

    with ExitStack() as es:
        for s in R.sems:
            s.h = es.enter_context(nc.semaphore(s.name))
        with nc.Block() as block:
            @block.sync
            def _(e):
                R.replay(e, "sync")

            @block.scalar
            def _(e):
                R.replay(e, "scalar")

            @block.vector
            def _(e):
                R.replay(e, "vector")

            @block.tensor
            def _(e):
                R.replay(e, "tensor")

            @block.gpsimd
            def _(e):
                R.replay(e, "gpsimd")

    return nc


_NC_CACHE = {}


def _get_nc(taps=False):
    key = bool(taps)
    if key not in _NC_CACHE:
        _NC_CACHE[key] = build_program(taps=key)
    return _NC_CACHE[key]


def make_inputs(L0_attn_x, L0_attn_Wqkv, L0_attn_Wg, L0_attn_bg, L0_attn_Wp,
                L0_attn_bp, L0_ffn_w_gate, L0_ffn_w_up, L0_ffn_w_down):
    x = np.ascontiguousarray(np.asarray(L0_attn_x, dtype=np.float32))
    wqkv = np.ascontiguousarray(np.asarray(L0_attn_Wqkv, dtype=np.float32))
    wg = np.ascontiguousarray(np.asarray(L0_attn_Wg, dtype=np.float32))
    bg = np.asarray(L0_attn_bg, dtype=np.float32)
    wp = np.ascontiguousarray(np.asarray(L0_attn_Wp, dtype=np.float32))
    bp = np.asarray(L0_attn_bp, dtype=np.float32)
    wgt = np.ascontiguousarray(np.asarray(L0_ffn_w_gate, dtype=np.float32))
    wup = np.ascontiguousarray(np.asarray(L0_ffn_w_up, dtype=np.float32))
    wdn = np.ascontiguousarray(np.asarray(L0_ffn_w_down, dtype=np.float32))

    bgt = np.ascontiguousarray(np.tile(bg[None, :], (P, 1)))
    bpt = np.ascontiguousarray(np.tile(bp[None, :], (P, 1)))
    csts = np.zeros((P, 8), dtype=np.float32)
    csts[:, 0] = 1e-5
    for j, t in enumerate(THR):
        csts[:, 1 + j] = -t

    in_maps = []
    for c in range(8):
        b, qt = divmod(c, 4)
        xs = np.ascontiguousarray(x[b, qt * NT:(qt + 1) * NT, :])
        in_maps.append({
            "x": xs, "wqkv": wqkv, "wg": wg, "wp": wp,
            "wgate": wgt, "wup": wup, "wdown": wdn,
            "bgt": bgt, "bpt": bpt, "csts_d": csts,
        })
    return in_maps


def kernel(L0_attn_x, L0_attn_Wqkv, L0_attn_Wg, L0_attn_bg, L0_attn_Wp,
           L0_attn_bp, L0_ffn_w_gate, L0_ffn_w_up, L0_ffn_w_down):
    in_maps = make_inputs(L0_attn_x, L0_attn_Wqkv, L0_attn_Wg, L0_attn_bg,
                          L0_attn_Wp, L0_attn_bp, L0_ffn_w_gate, L0_ffn_w_up,
                          L0_ffn_w_down)
    nc = _get_nc(taps=False)
    res = run_bass_kernel_spmd(nc, in_maps, core_ids=list(range(8)))
    y = np.empty((2, 4 * NT, C), dtype=np.float32)
    for c in range(8):
        b, qt = divmod(c, 4)
        y[b, qt * NT:(qt + 1) * NT, :] = res.results[c]["y"]
    return y
